# version 12
# baseline (speedup 1.0000x reference)
import contextlib
import math
import numpy as np
import ml_dtypes
import concourse.bass as bass
import concourse.mybir as mybir
from concourse.bass_utils import run_bass_kernel_spmd

F32 = mybir.dt.float32
BF16 = mybir.dt.bfloat16
ALU = mybir.AluOpType
AF = mybir.ActivationFunctionType
f32 = np.float32

P = 128
D = 1024
DC = 8
T = 16384
NCORES = 8
TC = 2048
NT = 16
KB = 512
NKB = T // KB
HALO_G = 1408
NEG = -1.0e30

HEADS = 8
DH = 64
INNER = 512
HID = 2730
WIN = 512
KW = 640
NW = 5
QK_SCALE = 8.0

HT_G = 11
HT_K1 = 10
HT_A1 = 6
HT_K2 = 5
HT_A2 = 1
NFT = HT_K1 + NT

STAGE2_STUB = False



def _rotary_tables(pos):
    pos = np.asarray(pos, np.float64)
    half = np.arange(0, DH, 2, dtype=np.float64)
    inv_freq = 1.0 / (10000.0 ** (half / DH))
    freqs = pos[:, None] * inv_freq[None, :]
    freqs = np.concatenate([freqs, freqs], axis=-1)
    base = (half + 0.4 * DH) / (1.4 * DH)
    power = (pos - T // 2) / float(WIN // 2)
    scale = base[None, :] ** power[:, None]
    scale = np.concatenate([scale, scale], axis=-1)
    cos, sin = np.cos(freqs), np.sin(freqs)
    sgn = np.ones((1, DH)); sgn[0, :32] = -1.0
    cq = cos * scale * QK_SCALE
    bq = sin * scale * sgn * QK_SCALE
    ck = cos / scale
    bk = sin / scale * sgn
    return np.concatenate([cq, bq, ck, bk], axis=1).astype(np.float32)


def _local_masks(tile_starts):
    out = np.zeros((len(tile_starts), P, KW), np.float32)
    pp = np.arange(P)[:, None]
    cj = np.arange(KW)[None, :]
    for n, t0 in enumerate(tile_starts):
        i = t0 + pp
        j = t0 - WIN + cj
        allowed = (j <= i) & (j >= i - WIN) & (j >= 0)
        out[n][~allowed] = NEG
    return out


def _chunk_cols(vec):
    return np.ascontiguousarray(np.asarray(vec, np.float32).reshape(DC, P).T)


def make_inputs(full):
    x = np.asarray(full["L0_attn_x"], np.float32).reshape(T, D)
    wq = np.asarray(full["L0_attn_Wq"], np.float32)
    wk = np.asarray(full["L0_attn_Wk"], np.float32)
    wv = np.asarray(full["L0_attn_Wv"], np.float32)
    com = dict(
        wqT=np.ascontiguousarray(wq.T), wkT=np.ascontiguousarray(wk.T),
        wvT=np.ascontiguousarray(wv.T),
        bq_c=_chunk_cols(full["L0_attn_bq"]),
        bk_c=_chunk_cols(full["L0_attn_bk"]),
        bv_r=np.ascontiguousarray(np.asarray(full["L0_attn_bv"],
                                             np.float32)[None, :]),
        identb_in=np.eye(P, dtype=ml_dtypes.bfloat16),
        identf_in=np.eye(P, dtype=np.float32),
    )
    wqkv = np.asarray(full["L0_ffn_wqkv"], np.float32)
    wo = np.asarray(full["L0_ffn_wo"], np.float32)
    w1 = np.asarray(full["L0_ffn_w1"], np.float32)
    w2 = np.asarray(full["L0_ffn_w2"], np.float32)
    b1 = np.asarray(full["L0_ffn_b1"], np.float32)
    b2 = np.asarray(full["L0_ffn_b2"], np.float32)
    lng = np.asarray(full["L0_ffn_ln_g"], np.float32)
    lnb = np.asarray(full["L0_ffn_ln_b"], np.float32)
    qsc = np.asarray(full["L0_ffn_q_scale"], np.float32)
    ksc = np.asarray(full["L0_ffn_k_scale"], np.float32)
    for l in range(2):
        com[f"wqkv{l}"] = np.ascontiguousarray(wqkv[l])
        com[f"wo{l}"] = np.ascontiguousarray(wo[l])
        com[f"w1a{l}"] = np.ascontiguousarray(w1[l][:, :HID])
        com[f"w1g{l}"] = np.ascontiguousarray(w1[l][:, HID:])
        com[f"w2{l}"] = np.ascontiguousarray(w2[l])
        com[f"b1a{l}"] = np.ascontiguousarray(b1[l][None, :HID])
        com[f"b1g{l}"] = np.ascontiguousarray(b1[l][None, HID:])
        com[f"b2c{l}"] = _chunk_cols(b2[l])
        com[f"b2r{l}"] = np.ascontiguousarray(b2[l][None, :])
        com[f"lngc{l}"] = _chunk_cols(lng[l])
        com[f"lnbc{l}"] = _chunk_cols(lnb[l])
        com[f"qs{l}"] = np.ascontiguousarray(np.tile(qsc[l], HEADS)[None, :])
        com[f"ks{l}"] = np.ascontiguousarray(np.tile(ksc[l], HEADS)[None, :])
    in_maps = []
    for c in range(NCORES):
        s = c * TC
        m = dict(com)
        m["x0"] = np.ascontiguousarray(x[s:s + TC])
        m["hz"] = np.full((P, DC), 0.0 if c == 0 else 1.0, np.float32)
        pos = np.arange(s - HT_K1 * P, s + TC)
        m["rot"] = _rotary_tables(pos)
        m["msk1"] = _local_masks(
            [s - HT_A1 * P + P * n for n in range(HT_A1 + NT)]
        ).reshape((HT_A1 + NT) * P, KW)
        m["msk2"] = _local_masks(
            [s - HT_A2 * P + P * n for n in range(HT_A2 + NT)]
        ).reshape((HT_A2 + NT) * P, KW)
        in_maps.append(m)
    return in_maps



class Sems:
    def __init__(self, nc, ctx, names):
        self.h = {n: ctx.enter_context(nc.semaphore(n)) for n in names}
        self.c = {n: 0 for n in names}

    def inc(self, inst, name, n=1):
        inst.then_inc(self.h[name], n)
        self.c[name] += n
        return self.c[name]


def build_program(trunc=99, dbg=False, s2stop=99):
    nc = bass.Bass("TRN2", target_bir_lowering=False,
                   detect_race_conditions=False)

    def ein(name, shape, dt=F32):
        return nc.dram_tensor(name, shape, dt, kind="ExternalInput").ap()

    x0_d = ein("x0", [TC, D])
    wqT_d = ein("wqT", [D, D])
    wkT_d = ein("wkT", [D, D])
    wvT_d = ein("wvT", [D, D])
    bq_d = ein("bq_c", [P, DC])
    bk_d = ein("bk_c", [P, DC])
    bv_d = ein("bv_r", [1, D])
    identb_d = ein("identb_in", [P, P], BF16)
    identf_d = ein("identf_in", [P, P])
    rot_d = ein("rot", [NFT * P, 4 * DH])
    msk1_d = ein("msk1", [(HT_A1 + NT) * P, KW])
    msk2_d = ein("msk2", [(HT_A2 + NT) * P, KW])
    wqkv_d = [ein(f"wqkv{l}", [D, 3 * INNER]) for l in range(2)]
    wo_d = [ein(f"wo{l}", [INNER, D]) for l in range(2)]
    w1a_d = [ein(f"w1a{l}", [D, HID]) for l in range(2)]
    w1g_d = [ein(f"w1g{l}", [D, HID]) for l in range(2)]
    w2_d = [ein(f"w2{l}", [HID, D]) for l in range(2)]
    b1a_d = [ein(f"b1a{l}", [1, HID]) for l in range(2)]
    b1g_d = [ein(f"b1g{l}", [1, HID]) for l in range(2)]
    b2_d = [ein(f"b2c{l}", [P, DC]) for l in range(2)]
    b2r_d = [ein(f"b2r{l}", [1, D]) for l in range(2)]
    hz_d = ein("hz", [P, DC])
    lng_d = [ein(f"lngc{l}", [P, DC]) for l in range(2)]
    lnb_d = [ein(f"lnbc{l}", [P, DC]) for l in range(2)]
    qs_d = [ein(f"qs{l}", [1, INNER]) for l in range(2)]
    ks_d = [ein(f"ks{l}", [1, INNER]) for l in range(2)]

    out_d = nc.dram_tensor("x2", [TC, D], F32, kind="ExternalOutput").ap()

    kin_kg = [nc.dram_tensor(f"kin_k{g}", [D, KB], BF16).ap()
              for g in range(4)]
    kin_vg = [nc.dram_tensor(f"kin_v{g}", [4 * P, D], BF16).ap()
              for g in range(4)]
    kout_kg = [nc.dram_tensor(f"kout_k{g}", [NCORES * D, KB], BF16).ap()
               for g in range(4)]
    kout_vg = [nc.dram_tensor(f"kout_v{g}", [NCORES * 4 * P, D], BF16).ap()
               for g in range(4)]
    x1_scr = nc.dram_tensor("x1_scr", [TC, D], F32).ap()
    x1t_in = nc.dram_tensor("x1t_in", [HALO_G, D], F32).ap()
    x1t_out = nc.dram_tensor("x1t_out", [NCORES * HALO_G, D], F32).ap()
    x1h_scr = nc.dram_tensor("x1h_scr", [HALO_G, D], F32).ap()
    if dbg:
        dbg_o = nc.dram_tensor("dbg_o", [P, NT * D], F32,
                               kind="ExternalOutput").ap()
        dbg_l = nc.dram_tensor("dbg_l", [P, NT], F32,
                               kind="ExternalOutput").ap()
        dbg_m = nc.dram_tensor("dbg_m", [P, NT], F32,
                               kind="ExternalOutput").ap()
        dbg_q = nc.dram_tensor("dbg_q", [P, DC * TC], BF16,
                               kind="ExternalOutput").ap()

    with contextlib.ExitStack() as ctx:
        sm = Sems(nc, ctx, ["dX", "dW", "dKV", "dST", "dOU", "cc",
                            "T", "V", "S", "G", "dM", "dH"])
        H, C = sm.h, sm.c

        def sb(name, shape, dt):
            return ctx.enter_context(nc.sbuf_tensor(name, shape, dt))

        identb = sb("identb", [P, P], BF16)
        identf = sb("identf", [P, P], F32)
        ones_c = sb("ones_c", [P, 1], F32)
        ones_r = sb("ones_r", [1, P], F32)
        one1 = sb("one1", [1, 1], F32)
        onesb = sb("onesb", [P, 1], BF16)

        block = ctx.enter_context(nc.Block())
        TT, VV, SS, GG, SY = nc.tensor, nc.vector, nc.scalar, nc.gpsimd, nc.sync

        stage2_state = {}

        with contextlib.ExitStack() as s1:
            def sb1(name, shape, dt):
                return s1.enter_context(nc.sbuf_tensor(name, shape, dt))

            stt = sb1("stt", [P, 12], F32)
            agg = sb1("agg", [P, 2], F32)
            rstd = sb1("rstd", [P, 1], F32)
            q_s = sb1("q_s", [P, DC * TC], BF16)
            negm = sb1("negm", [P, NT], F32)
            l_s = sb1("l_s", [P, NT], F32)
            o_acc = sb1("o_acc", [P, NT * D], F32)

            p1sb_stack = contextlib.ExitStack()

            def sbp1(name, shape, dt):
                return p1sb_stack.enter_context(nc.sbuf_tensor(name, shape, dt))

            wq_s = sbp1("wq_s", [P, 64 * P], BF16)
            wk_s = sbp1("wk_s", [P, 64 * P], BF16)
            wv_s = sbp1("wv_s", [P, 64 * P], BF16)
            bq32 = sbp1("bq32", [P, DC], F32)
            bk_s = sbp1("bk_s", [P, DC], F32)
            bv_r = sbp1("bv_rs", [1, D], F32)
            BV = sbp1("BV", [P, D], F32)
            x0st = sbp1("x0st", [P, 2 * D], F32)
            h_tm = sbp1("h_tm", [P, D], F32)
            h_bf = sbp1("h_bf", [P, D], BF16)
            hf_s = sbp1("hf_s", [P, DC * 512], BF16)
            sq_s = sbp1("sq_s", [P, 512], F32)
            nrow = sbp1("nrow", [1, 512], F32)
            kst = sbp1("kst", [P, DC * 512], BF16)
            vst = sbp1("vst", [P, D], BF16)

            GG.memset(ones_c[:], 1.0)
            GG.memset(ones_r[:], 1.0)
            GG.memset(one1[:], 1.0)
            GG.memset(onesb[:], 1.0)
            sm.inc(GG.dma_start(out=identb[:], in_=identb_d), "dW", 16)
            sm.inc(GG.dma_start(out=identf[:], in_=identf_d), "dW", 16)
            wview = lambda w: w.rearrange("(ec p) n -> p ec n", p=P)
            sview = lambda t: t[:].rearrange("p (ec n) -> p ec n", n=DC * P)
            sm.inc(GG.dma_start(out=sview(wq_s), in_=wview(wqT_d)), "dW", 16)
            sm.inc(GG.dma_start(out=sview(wk_s), in_=wview(wkT_d)), "dW", 16)
            sm.inc(GG.dma_start(out=sview(wv_s), in_=wview(wvT_d)), "dW", 16)
            sm.inc(GG.dma_start(out=bq32[:], in_=bq_d), "dW", 16)
            sm.inc(GG.dma_start(out=bk_s[:], in_=bk_d), "dW", 16)
            sm.inc(GG.dma_start(out=bv_r[:], in_=bv_d), "dW", 16)
            w_rdy = C["dW"]

            with contextlib.ExitStack() as p1ps:
                def ps1(name, shape, dt=F32):
                    return p1ps.enter_context(nc.psum_tensor(name, shape, dt))

                pj_ps = ps1("pj_ps", [P, 2 * 512])
                v_ps = ps1("v_ps", [P, D])
                tr_ps = ps1("tr_ps", [P, 512])
                nrm_ps = ps1("nrm_ps", [1, 512])
                nm_ps = ps1("nm_ps", [P, 4])

                VV.wait_ge(H["dW"], w_rdy)
                sm.inc(VV.tensor_scalar_mul(bq32[:], bq32[:], 1.0 / 32.0),
                       "V")
                bq32_rdy = C["V"]

                TT.wait_ge(H["dW"], w_rdy)
                _mm = TT.matmul(v_ps[:, 0:512], ones_r[:], bv_r[:, 0:512],
                          start=True, stop=True)
                _mm = TT.matmul(v_ps[:, 512:1024], ones_r[:], bv_r[:, 512:1024],
                          start=True, stop=True)
                sm.inc(_mm, "T")
                bvp = C["T"]
                VV.wait_ge(H["T"], bvp)
                sm.inc(VV.tensor_copy(BV[:], v_ps[:]), "V")
                bv_rdy = C["V"]

                for i in range(2):
                    sm.inc(SY.dma_start(out=x0st[:, i * D:(i + 1) * D],
                                        in_=x0_d[i * P:(i + 1) * P, :]),
                           "dX", 16)

                qk_evac = [0] * 4
                v_evac_last = bv_rdy
                tr_evac_last = 0
                nm_read = [0] * 4
                nrow_done = 0
                kst_store = 0
                cc_cnt = {}
                x0_cons = [0] * NT
                vst_store = [0] * NT

                ngrp = {0: 0, 11: 1, 12: 2}.get(trunc, 4)
                for g_i in range(ngrp):
                    for ii in range(4):
                        i = g_i * 4 + ii
                        xs = x0st[:, (i % 2) * D:(i % 2 + 1) * D]
                        VV.wait_ge(H["dX"], 16 * (i + 1))
                        VV.bn_stats(stt[:, 0:6], xs[:, 0:512])
                        sm.inc(VV.bn_stats(stt[:, 6:12], xs[:, 512:1024]),
                               "V")
                        VV.wait_ge(H["V"], C["V"])
                        sm.inc(VV.bn_aggr(agg[:], stt[:]), "V")
                        bn_done = C["V"]
                        SS.wait_ge(H["V"], bn_done)
                        SS.activation(rstd[:], agg[:, 1:2], AF.Ln,
                                      bias=f32(1e-5), scale=f32(1.0))
                        sm.inc(SS.activation(rstd[:], rstd[:], AF.Exp,
                                             scale=f32(-0.5)), "S")
                        rstd_done = C["S"]
                        VV.wait_ge(H["S"], rstd_done)
                        VV.tensor_scalar(h_tm[:], xs, agg[:, 0:1], rstd[:],
                                         ALU.subtract, ALU.mult)
                        sm.inc(VV.tensor_copy(h_bf[:], h_tm[:]), "V")
                        x0_cons[i] = C["V"]
                        cast_done = C["V"]
                        if i + 2 < NT:
                            SY.wait_ge(H["V"], x0_cons[i])
                            sm.inc(SY.dma_start(
                                out=x0st[:, (i % 2) * D:(i % 2 + 1) * D],
                                in_=x0_d[(i + 2) * P:(i + 3) * P, :]),
                                "dX", 16)

                        TT.wait_ge(H["V"], cast_done)
                        if g_i > 0 and ii == 0:
                            TT.wait_ge(H["S"], qk_evac[g_i - 1])
                        for c2 in range(2):
                            TT.wait_ge(H["S"], tr_evac_last)
                            for c in range(4):
                                cc2 = c2 * 4 + c
                                _mm = TT.matmul(tr_ps[:, c * P:(c + 1) * P],
                                          h_bf[:, cc2 * P:(cc2 + 1) * P],
                                          identb[:],
                                          start=(c == 0), stop=(c == 3))
                            sm.inc(_mm, "T")
                            tcnt = C["T"]
                            SS.wait_ge(H["T"], tcnt)
                            dst = hf_s[:].rearrange(
                                "p (ec g) -> p ec g", g=512)[
                                :, c2 * 4:(c2 + 1) * 4, ii * P:(ii + 1) * P]
                            sm.inc(SS.copy(dst, tr_ps[:].rearrange(
                                "p (c q) -> p c q", c=4)), "S")
                            tr_evac_last = C["S"]

                        TT.wait_ge(H["S"], tr_evac_last)
                        TT.wait_ge(H["V"], v_evac_last)
                        for ec in range(DC):
                            lhs = hf_s[:, ec * 512 + ii * P:
                                       ec * 512 + (ii + 1) * P]
                            for hh in range(2):
                                _mm = TT.matmul(v_ps[:, hh * 512:(hh + 1) * 512],
                                          lhs,
                                          wv_s[:, ec * 8 * P + hh * 512:
                                               ec * 8 * P + (hh + 1) * 512],
                                          start=(ec == 0),
                                          stop=(ec == DC - 1))
                        sm.inc(_mm, "T")
                        vmm = C["T"]
                        VV.wait_ge(H["T"], vmm)
                        if i > 0:
                            VV.wait_ge(H["dST"], vst_store[i - 1])
                        sm.inc(VV.tensor_tensor(vst[:], v_ps[:], BV[:],
                                                ALU.add), "V")
                        v_evac_last = C["V"]
                        SY.wait_ge(H["V"], v_evac_last)
                        sm.inc(SY.dma_start(
                            out=kin_vg[i // 4][(i % 4) * P:(i % 4 + 1) * P, :],
                            in_=vst[:]), "dST", 16)
                        vst_store[i] = C["dST"]

                    q_ev = [0, 0]
                    for dc_i in range(DC):
                        TT.wait_ge(H["S"], tr_evac_last)
                        if dc_i >= 2:
                            TT.wait_ge(H["S"], q_ev[dc_i % 2])
                        for ec in range(DC):
                            _mm = TT.matmul(pj_ps[:, (dc_i % 2) * 512:
                                            (dc_i % 2 + 1) * 512],
                                      wq_s[:, (ec * 8 + dc_i) * P:
                                           (ec * 8 + dc_i + 1) * P],
                                      hf_s[:, ec * 512:(ec + 1) * 512],
                                      start=(ec == 0), stop=(ec == DC - 1))
                        sm.inc(_mm, "T")
                        qmm = C["T"]
                        SS.wait_ge(H["T"], qmm)
                        SS.wait_ge(H["V"], bq32_rdy)
                        sm.inc(SS.activation(
                            q_s[:, dc_i * TC + g_i * 512:
                                dc_i * TC + (g_i + 1) * 512],
                            pj_ps[:, (dc_i % 2) * 512:(dc_i % 2 + 1) * 512],
                            AF.Identity, bias=bq32[:, dc_i:dc_i + 1],
                            scale=f32(1.0 / 32.0)), "S")
                        q_ev[dc_i % 2] = C["S"]
                    q_done = C["S"]

                    nrm_mm = 0
                    for dc_i in range(DC):
                        SS.wait_ge(H["S"], q_done)
                        if dc_i > 0:
                            SS.wait_ge(H["T"], nrm_mm)
                        sm.inc(SS.activation(
                            sq_s[:],
                            q_s[:, dc_i * TC + g_i * 512:
                                dc_i * TC + (g_i + 1) * 512],
                            AF.Square), "S")
                        sqd = C["S"]
                        TT.wait_ge(H["S"], sqd)
                        if dc_i == 0 and g_i > 0:
                            TT.wait_ge(H["S"], nrow_done)
                        _mm = TT.matmul(nrm_ps[:], ones_c[:], sq_s[:],
                                  start=(dc_i == 0), stop=(dc_i == DC - 1))
                        sm.inc(_mm, "T")
                        nrm_mm = C["T"]
                    SS.wait_ge(H["T"], nrm_mm)
                    SS.activation(nrow[:], nrm_ps[:], AF.Ln, bias=f32(1e-30),
                                  scale=f32(1.0))
                    sm.inc(SS.activation(nrow[:], nrow[:], AF.Exp,
                                         scale=f32(0.5),
                                         bias=f32(math.log(40.0))), "S")
                    nrow_done = C["S"]
                    VV.wait_ge(H["S"], nrow_done)
                    sm.inc(VV.tensor_scalar_mul(nrow[:], nrow[:], -1.0), "V")
                    nneg = C["V"]
                    TT.wait_ge(H["V"], nneg)
                    if g_i > 0:
                        TT.wait_ge(H["V"], nm_read[g_i - 1])
                    for c in range(4):
                        _mm = TT.matmul(nm_ps[:, c:c + 1],
                                  nrow[0:1, c * P:(c + 1) * P],
                                  one1[:], start=(c == 0), stop=(c == 3))
                    sm.inc(_mm, "T")
                    nm_mm = C["T"]
                    VV.wait_ge(H["T"], nm_mm)
                    sm.inc(VV.tensor_copy(negm[:, g_i * 4:(g_i + 1) * 4],
                                          nm_ps[:]), "V")
                    nm_read[g_i] = C["V"]

                    k_ev = [0, 0]
                    for dc_i in range(DC):
                        TT.wait_ge(H["S"], q_done)
                        if dc_i >= 2:
                            TT.wait_ge(H["S"], k_ev[dc_i % 2])
                        for ec in range(DC):
                            _mm = TT.matmul(pj_ps[:, (dc_i % 2) * 512:
                                            (dc_i % 2 + 1) * 512],
                                      wk_s[:, (ec * 8 + dc_i) * P:
                                           (ec * 8 + dc_i + 1) * P],
                                      hf_s[:, ec * 512:(ec + 1) * 512],
                                      start=(ec == 0), stop=(ec == DC - 1))
                        sm.inc(_mm, "T")
                        kmm = C["T"]
                        SS.wait_ge(H["T"], kmm)
                        if g_i > 0 and dc_i == 0:
                            SS.wait_ge(H["dST"], kst_store)
                        sm.inc(SS.activation(
                            kst[:, dc_i * 512:(dc_i + 1) * 512],
                            pj_ps[:, (dc_i % 2) * 512:(dc_i % 2 + 1) * 512],
                            AF.Identity, bias=bk_s[:, dc_i:dc_i + 1],
                            scale=f32(1.0)), "S")
                        k_ev[dc_i % 2] = C["S"]
                    qk_evac[g_i] = C["S"]
                    SY.wait_ge(H["S"], qk_evac[g_i])
                    sm.inc(SY.dma_start(
                        out=kin_kg[g_i].rearrange("(dc p) t -> p dc t", p=P),
                        in_=kst[:].rearrange("p (dc t) -> p dc t", t=512)),
                        "dST", 16)
                    kst_store = C["dST"]

                    GG.wait_ge(H["dST"], kst_store)
                    sm.inc(GG.collective_compute(
                        "AllGather", ALU.bypass,
                        replica_groups=[list(range(NCORES))],
                        ins=[kin_kg[g_i]], outs=[kout_kg[g_i]]), "cc")
                    sm.inc(GG.collective_compute(
                        "AllGather", ALU.bypass,
                        replica_groups=[list(range(NCORES))],
                        ins=[kin_vg[g_i]], outs=[kout_vg[g_i]]), "cc")
                    cc_cnt[g_i] = C["cc"]

            p1_T, p1_V, p1_S = C["T"], C["V"], C["S"]

            if trunc <= 2 or trunc in (11, 12):
                GG.wait_ge(H["cc"], C["cc"])
                SY.wait_ge(H["dST"], C["dST"])
                SY.wait_ge(H["dX"], C["dX"])
                SY.wait_ge(H["dW"], C["dW"])
                p1sb_stack.close()
                return nc

            p1sb_stack.close()
            with contextlib.ExitStack() as p2ps:
                def ps2(name, shape, dt=F32):
                    return p2ps.enter_context(nc.psum_tensor(name, shape, dt))

                def sb2(name, shape, dt):
                    return p2ps.enter_context(nc.sbuf_tensor(name, shape, dt))

                kbf = sb2("kbf", [P, 2 * DC * KB], BF16)
                vbf = sb2("vbf", [P, 2 * 4 * D], BF16)
                p_sb = sb2("p_sb", [P, 2 * KB], BF16)
                ptb = sb2("ptb", [P, 2 * KB], BF16)
                rho = sb2("rho", [P, 2], F32)
                x0r = sb2("x0r", [P, 2 * D], F32)
                z_n = sb2("z_n", [P, D], F32)
                x1st = sb2("x1st", [P, 2 * D], F32)

                s_ps = ps2("s_ps", [P, 2 * 512])
                pt_ps = ps2("pt_ps", [P, 2 * 512])
                o_ps = ps2("o_ps", [P, 2 * D])

                GG.wait_ge(H["T"], p1_T)
                GG.wait_ge(H["V"], p1_V)
                GG.wait_ge(H["S"], p1_S)

                VV.memset(o_acc[:], 0.0)
                sm.inc(VV.memset(l_s[:], 0.0), "V")
                o_zeroed = C["V"]

                tS = {}
                tPt = {}
                tPV = {}
                sE = {}
                sPt = {}
                vO = {}
                ld = {}
                TOT = NKB * NT

                def emit_loads(j):
                    w_blk, r_blk = j // 8, j % 8
                    if j >= 2:
                        GG.wait_ge(H["T"], tPV[(j - 2) * NT + NT - 1])
                    GG.wait_ge(H["cc"], cc_cnt[w_blk])
                    sm.inc(GG.dma_start(
                        out=kbf[:, (j % 2) * DC * KB:
                                (j % 2 + 1) * DC * KB]
                        .rearrange("p (dc t) -> p dc t", t=KB),
                        in_=kout_kg[w_blk][r_blk * D:(r_blk + 1) * D, :]
                        .rearrange("(dc p) t -> p dc t", p=P)), "dKV", 16)
                    sm.inc(GG.dma_start(
                        out=vbf[:, (j % 2) * 4 * D:(j % 2 + 1) * 4 * D]
                        .rearrange("p (ks d) -> p ks d", d=D),
                        in_=kout_vg[w_blk][r_blk * KB:(r_blk + 1) * KB, :]
                        .rearrange("(ks p) d -> p ks d", p=P)), "dKV", 16)
                    ld[j] = C["dKV"]

                def emit_S(k):
                    j, q, slot = k // NT, k % NT, k % 2
                    TT.wait_ge(H["dKV"], ld[j])
                    if k == 0:
                        TT.wait_ge(H["S"], p1_S)
                        TT.wait_ge(H["V"], o_zeroed)
                    if k >= 2:
                        TT.wait_ge(H["S"], sE[k - 2])
                    for ec in range(DC):
                        _mm = TT.matmul(
                            s_ps[:, slot * 512:(slot + 1) * 512],
                            q_s[:, ec * TC + q * P:ec * TC + (q + 1) * P],
                            kbf[:, (j % 2) * DC * KB + ec * KB:
                                (j % 2) * DC * KB + (ec + 1) * KB],
                            start=(ec == 0), stop=(ec == DC - 1))
                    sm.inc(_mm, "T")
                    tS[k] = C["T"]

                def emit_exp(k):
                    q, slot = k % NT, k % 2
                    SS.wait_ge(H["T"], tS[k])
                    if k >= 2:
                        SS.wait_ge(H["T"], tPt[k - 2])
                        SS.wait_ge(H["V"], vO[k - 2])
                    sm.inc(SS.activation(
                        p_sb[:, slot * KB:(slot + 1) * KB],
                        s_ps[:, slot * 512:(slot + 1) * 512],
                        AF.Exp, bias=negm[:, q:q + 1], scale=f32(1.0),
                        accum_out=rho[:, slot:slot + 1]), "S")
                    sE[k] = C["S"]

                def emit_Pt(k):
                    slot = k % 2
                    TT.wait_ge(H["S"], sE[k])
                    if k >= 2:
                        TT.wait_ge(H["S"], sPt[k - 2])
                    for c in range(4):
                        _mm = TT.matmul(
                            pt_ps[:, slot * 512 + c * P:
                                  slot * 512 + (c + 1) * P],
                            p_sb[:, slot * KB + c * P:
                                 slot * KB + (c + 1) * P],
                            identb[:], start=(c == 0), stop=(c == 3))
                    sm.inc(_mm, "T")
                    tPt[k] = C["T"]

                def emit_ptcopy(k):
                    slot = k % 2
                    SS.wait_ge(H["T"], tPt[k])
                    if k >= 2:
                        SS.wait_ge(H["T"], tPV[k - 2])
                    sm.inc(SS.copy(
                        ptb[:, slot * KB:(slot + 1) * KB],
                        pt_ps[:, slot * 512:(slot + 1) * 512]), "S")
                    sPt[k] = C["S"]

                def emit_PV(k):
                    j, slot = k // NT, k % 2
                    TT.wait_ge(H["S"], sPt[k])
                    if k >= 2:
                        TT.wait_ge(H["V"], vO[k - 2])
                    for ks in range(4):
                        for hh in range(2):
                            _mm = TT.matmul(
                                o_ps[:, slot * D + hh * 512:
                                     slot * D + (hh + 1) * 512],
                                ptb[:, slot * KB + ks * P:
                                    slot * KB + (ks + 1) * P],
                                vbf[:, (j % 2) * 4 * D + ks * D + hh * 512:
                                    (j % 2) * 4 * D + ks * D +
                                    (hh + 1) * 512],
                                start=(ks == 0), stop=(ks == 3))
                    sm.inc(_mm, "T")
                    tPV[k] = C["T"]

                def emit_oadd(k):
                    q, slot = k % NT, k % 2
                    VV.wait_ge(H["T"], tPV[k])
                    VV.tensor_tensor(
                        o_acc[:, q * D:(q + 1) * D],
                        o_acc[:, q * D:(q + 1) * D],
                        o_ps[:, slot * D:(slot + 1) * D], ALU.add)
                    sm.inc(VV.tensor_tensor(
                        l_s[:, q:q + 1], l_s[:, q:q + 1],
                        rho[:, slot:slot + 1], ALU.add), "V")
                    vO[k] = C["V"]

                x0r_load = {}
                x1_store = {}

                def emit_P3(q):
                    sl = q % 2
                    zs = o_acc[:, q * D:(q + 1) * D]
                    sm.inc(VV.reciprocal(rstd[:], l_s[:, q:q + 1]), "V")
                    VV.wait_ge(H["V"], C["V"])
                    VV.tensor_scalar_mul(zs, zs, rstd[:])
                    VV.bn_stats(stt[:, 0:6], zs[:, 0:512])
                    sm.inc(VV.bn_stats(stt[:, 6:12], zs[:, 512:1024]), "V")
                    VV.wait_ge(H["V"], C["V"])
                    sm.inc(VV.bn_aggr(agg[:], stt[:]), "V")
                    bnz = C["V"]
                    SS.wait_ge(H["V"], bnz)
                    SS.activation(rstd[:], agg[:, 1:2], AF.Ln,
                                  bias=f32(1e-5), scale=f32(1.0))
                    sm.inc(SS.activation(rstd[:], rstd[:], AF.Exp,
                                         scale=f32(-0.5)), "S")
                    rsz = C["S"]
                    VV.wait_ge(H["S"], rsz)
                    VV.tensor_scalar(z_n[:], zs, agg[:, 0:1], rstd[:],
                                     ALU.subtract, ALU.mult)
                    VV.wait_ge(H["dX"], x0r_load[q])
                    if q >= 2:
                        VV.wait_ge(H["dST"], x1_store[q - 2])
                    sm.inc(VV.tensor_tensor(
                        x1st[:, sl * D:(sl + 1) * D],
                        z_n[:], x0r[:, sl * D:(sl + 1) * D],
                        ALU.add), "V")
                    x1_done = C["V"]
                    if q + 2 < NT:
                        SY.wait_ge(H["V"], x1_done)
                        sm.inc(SY.dma_start(
                            out=x0r[:, sl * D:(sl + 1) * D],
                            in_=x0_d[(q + 2) * P:(q + 3) * P, :]),
                            "dX", 16)
                        x0r_load[q + 2] = C["dX"]
                    SY.wait_ge(H["V"], x1_done)
                    sm.inc(SY.dma_start(
                        out=x1_scr[q * P:(q + 1) * P, :],
                        in_=x1st[:, sl * D:(sl + 1) * D]),
                        "dST", 16)
                    if q >= 5:
                        sm.inc(SY.dma_start(
                            out=x1t_in[(q - 5) * P:(q - 4) * P, :],
                            in_=x1st[:, sl * D:(sl + 1) * D]),
                            "dST", 16)
                    x1_store[q] = C["dST"]
                    if STAGE2_STUB:
                        sm.inc(SY.dma_start(
                            out=out_d[q * P:(q + 1) * P, :],
                            in_=x1st[:, sl * D:(sl + 1) * D]),
                            "dOU", 16)

                emit_loads(0)
                emit_loads(1)
                for k in range(TOT):
                    if k >= 1:
                        emit_Pt(k - 1)
                        emit_ptcopy(k - 1)
                    emit_S(k)
                    if k >= 1:
                        emit_PV(k - 1)
                    emit_exp(k)
                    if k >= 1:
                        emit_oadd(k - 1)
                    if k % NT == 0 and k > 0 and k // NT + 1 < NKB:
                        emit_loads(k // NT + 1)
                    if k == (NKB - 2) * NT:
                        for qq in range(2):
                            sm.inc(SY.dma_start(
                                out=x0r[:, qq * D:(qq + 1) * D],
                                in_=x0_d[qq * P:(qq + 1) * P, :]),
                                "dX", 16)
                            x0r_load[qq] = C["dX"]
                    if k >= 1 and k - 1 >= (NKB - 1) * NT:
                        emit_P3(k - 1 - (NKB - 1) * NT)
                emit_Pt(TOT - 1)
                emit_ptcopy(TOT - 1)
                emit_PV(TOT - 1)
                emit_oadd(TOT - 1)
                emit_P3(NT - 1)

                flash_V = C["V"]

                if dbg:
                    SY.wait_ge(H["V"], flash_V)
                    sm.inc(SY.dma_start(out=dbg_o, in_=o_acc[:]), "dST", 16)
                    sm.inc(SY.dma_start(out=dbg_l, in_=l_s[:]), "dST", 16)
                    sm.inc(SY.dma_start(out=dbg_m, in_=negm[:]), "dST", 16)
                    sm.inc(SY.dma_start(out=dbg_q, in_=q_s[:]), "dST", 16)

                if trunc <= 3:
                    VV.wait_ge(H["V"], C["V"])
                    SY.wait_ge(H["dKV"], C["dKV"])
                    return nc

                if trunc > 4:
                    tail_done = C["dST"]
                    GG.wait_ge(H["dST"], tail_done)
                    sm.inc(GG.collective_compute(
                        "AllGather", ALU.bypass,
                        replica_groups=[list(range(NCORES))],
                        ins=[x1t_in], outs=[x1t_out]), "cc")
                    x1_gathered = C["cc"]
                    with GG.register("hr") as hr:
                        GG.wait_ge(H["cc"], x1_gathered)
                        GG.cc_rank_ld(
                            hr, replica_groups=[list(range(NCORES))])
                        GG.reg_add(hr, hr, NCORES - 1)
                        GG.reg_mod(hr, hr, NCORES)
                        GG.reg_mul(hr, hr, HALO_G)
                        offh = GG.snap(hr, min_val=0,
                                       max_val=(NCORES - 1) * HALO_G)
                        sm.inc(GG.dma_start(
                            out=x1h_scr,
                            in_=x1t_out[bass.ds(offh, HALO_G), :]),
                            "dH", 16)
                    halo_rdy = C["dH"]

                if trunc <= 4:
                    SY.wait_ge(H["dST"], C["dST"])
                    SY.wait_ge(H["dOU"], C["dOU"])
                    return nc

            stage2_state.update(dict(halo_rdy=halo_rdy))

        if not STAGE2_STUB:
            emit_stage2 = _make_stage2(
                nc, sm, H, C, TT, VV, SS, GG, SY,
                identb, identf, ones_c, ones_r, one1, onesb,
                x1_scr, x1h_scr, out_d, rot_d, msk1_d, msk2_d,
                wqkv_d, wo_d, w1a_d, w1g_d, w2_d, b1a_d, b1g_d, b2_d,
                lng_d, lnb_d, qs_d, ks_d, b2r_d, hz_d, ctx, s2stop)
            emit_stage2()

        SY.wait_ge(H["dOU"], C["dOU"])
        SY.wait_ge(H["dM"], C["dM"])
        SY.wait_ge(H["dST"], C["dST"])
        GG.wait_ge(H["dH"], C["dH"])

    return nc


_NC_CACHE = {}


def kernel(**inputs):
    in_maps = make_inputs(inputs)
    if "prog" not in _NC_CACHE:
        _NC_CACHE["prog"] = build_program()
    nc = _NC_CACHE["prog"]
    res = run_bass_kernel_spmd(nc, in_maps, core_ids=list(range(NCORES)))
    out = np.concatenate([np.asarray(r["x2"]) for r in res.results], axis=0)
    return out.reshape(1, T, D).astype(np.float32)


def _make_stage2(nc, sm, H, C, TT, VV, SS, GG, SY,
                 identb, identf, ones_c, ones_r, one1, onesb,
                 x1_scr, x1h_scr, out_d, rot_d, msk1_d, msk2_d,
                 wqkv_d, wo_d, w1a_d, w1g_d, w2_d, b1a_d, b1g_d, b2_d,
                 lng_d, lnb_d, qs_d, ks_d, b2r_d, hz_d, ctx, s2stop=99):
    CW1 = 1 + (HT_A1 + NT) * P
    NTL = [HT_G + NT, HT_K2 + NT]
    QKV0 = [1, 0]
    ATT0 = [HT_G - HT_A1, HT_K2 - HT_A2]
    U1OF = [lambda mg: (mg - 5) * P, lambda mg: (mg + 1) * P]
    ROT0 = [lambda mg: (mg - 1) * P,
            lambda mg: (HT_K1 - HT_K2) * P + mg * P]
    GW = [512, 512, 512, 512, 512, 170]
    GOF = [0, 512, 1024, 1536, 2048, 2560]

    def emit():
        with contextlib.ExitStack() as s2:
            def sbo(name, shape, dt):
                return s2.enter_context(nc.sbuf_tensor(name, shape, dt))

            u1 = sbo("u1", [P, DC * CW1], F32)
            x1stg = sbo("x1stg2", [P, 2 * D], F32)
            stt2 = sbo("stt2", [P, 12], F32)
            agg2 = sbo("agg2", [P, 2], F32)
            rstd2 = sbo("rstd2", [P, 1], F32)
            onesb_r = sbo("onesb_r", [1, P], BF16)
            hz_s = sbo("hz_s", [P, DC], F32)

            def join():
                sm.inc(VV.engine_nop(), "V")
                sm.inc(SS.activation(rstd2[:], rstd2[:], AF.Identity), "S")
                sm.inc(GG.engine_nop(), "G")
                cV, cS, cG = C["V"], C["S"], C["G"]
                for eng in (TT, VV, SS, GG, SY):
                    eng.wait_ge(H["V"], cV)
                    eng.wait_ge(H["S"], cS)
                    eng.wait_ge(H["G"], cG)
                    eng.wait_ge(H["dM"], C["dM"])
                    eng.wait_ge(H["dW"], C["dW"])

            def zero_row_m1():
                col = 1 + 767
                u1c_ = u1[:].rearrange("p (ch n) -> p ch n", n=CW1)[
                    :, :, col:col + 1]
                sm.inc(VV.tensor_tensor(
                    u1c_, u1c_, hz_s[:].unsqueeze(2), ALU.mult), "V")
                cnt = C["V"]
                for eng in (TT, SS, GG, SY):
                    eng.wait_ge(H["V"], cnt)

            for eng in (TT, VV, SS, GG, SY):
                if eng is SY:
                    eng.wait_ge(H["dH"], C["dH"])
                eng.wait_ge(H["dM"], C["dM"])
                eng.wait_ge(H["dST"], C["dST"])
                eng.wait_ge(H["V"], C["V"])
                eng.wait_ge(H["S"], C["S"])
            GG.memset(onesb_r[:], 1.0)
            sm.inc(SY.dma_start(out=hz_s[:], in_=hz_d), "dM", 16)
            sm.inc(GG.memset(u1[:], 0.0), "G")
            u1_zero = C["G"]
            for eng in (TT, VV, SS, SY):
                eng.wait_ge(H["G"], u1_zero)

            for l in range(2):
                ntl = NTL[l]
                att0 = ATT0[l]
                u1of = U1OF[l]
                with contextlib.ExitStack() as sa:
                    def sba(name, shape, dt):
                        return sa.enter_context(
                            nc.sbuf_tensor(f"{name}_l{l}", shape, dt))

                    def psa(name, shape, dt=F32):
                        return sa.enter_context(
                            nc.psum_tensor(f"{name}_l{l}", shape, dt))

                    wqkvs = sba("wqkvs", [P, DC * 3 * INNER], BF16)
                    wos = sba("wos", [P, 4 * D], BF16)
                    QSb = sba("QSb", [P, INNER], F32)
                    KSb = sba("KSb", [P, INNER], F32)
                    qsr = sba("qsr", [1, INNER], F32)
                    ksr = sba("ksr", [1, INNER], F32)
                    gbf = sba("gbf", [P, D], BF16)
                    gt = sba("gt", [P, D], F32)
                    g1c = sba("g1c", [P, 2 * DC * 129], BF16)
                    qkv_s = sba("qkv_s", [P, 2 * 3 * INNER], F32)
                    scr = sba("scr", [P, 2 * KW], F32)
                    qsh = sba("qsh", [P, INNER], F32)
                    qbf = sba("qbf", [P, INNER], BF16)
                    kbf2 = sba("kbf2", [P, INNER], BF16)
                    q_fm = sba("q_fm", [P, 2 * 4 * P], BF16)
                    rsq = sba("rsq", [P, D], F32)
                    k_ring = sba("k_ring", [P, 4 * 8 * P], BF16)
                    v_ring = sba("v_ring", [P, 8 * INNER], BF16)
                    rtab = sba("rtab", [P, 2 * 4 * DH], F32)
                    mskS = sba("mskS", [P, 2 * KW], F32)
                    p2 = sba("p2", [P, 2 * KW], BF16)
                    ptb2 = sba("ptb2", [P, 2 * KW], BF16)
                    o_fm = sba("o_fm", [P, INNER], BF16)
                    otm_sb = sba("otm_sb", [P, INNER], BF16)
                    rn_s = sba("rn_s", [P, 2 * HEADS], F32)
                    lsum = sba("lsum", [P, HEADS], F32)
                    lrec = sba("lrec", [P, HEADS], F32)
                    nm2 = sba("nm2", [P, 2], F32)

                    psQ = psa("psQ", [P, 2 * 512])
                    psPV = psa("psPV", [P, 512])
                    psB = psa("psB", [P, 4 * 512])
                    psC = psa("psC", [P, 512])

                    g1cv = g1c[:].rearrange("p (s ch n) -> p (s ch) n",
                                            s=2, ch=DC, n=129)
                    u1v_ = u1[:].rearrange("p (ch n) -> p ch n", n=CW1)
                    wqv = wqkvs[:].rearrange("p (ec n) -> p ec n",
                                             n=3 * INNER)
                    wov = wos[:].rearrange("p (kc n) -> p kc n", n=D)
                    krv = k_ring[:].rearrange("p (hc n) -> p hc n",
                                              n=8 * P)

                    sm.inc(GG.dma_start(
                        out=wqkvs[:].rearrange("p (ec n) -> p ec n",
                                               n=3 * INNER),
                        in_=wqkv_d[l].rearrange("(ec p) n -> p ec n", p=P)),
                        "dW", 16)
                    sm.inc(GG.dma_start(
                        out=wos[:].rearrange("p (kc n) -> p kc n", n=D),
                        in_=wo_d[l].rearrange("(kc p) n -> p kc n", p=P)),
                        "dW", 16)
                    w_rdy2 = C["dW"]
                    sm.inc(SY.dma_start(out=qsr[:], in_=qs_d[l]), "dM", 16)
                    sm.inc(SY.dma_start(out=ksr[:], in_=ks_d[l]), "dM", 16)
                    qs_ld = C["dM"]
                    TT.wait_ge(H["dM"], qs_ld)
                    TT.matmul(psC[:], ones_r[:], qsr[:],
                              start=True, stop=True)
                    _mm = TT.matmul(psB[:, 0:512], ones_r[:], ksr[:],
                                    start=True, stop=True)
                    sm.inc(_mm, "T")
                    qsb_mm = C["T"]
                    VV.wait_ge(H["T"], qsb_mm)
                    VV.tensor_copy(QSb[:], psC[:])
                    sm.inc(VV.tensor_copy(KSb[:], psB[:, 0:512]), "V")
                    qsb_rdy = C["V"]
                    GG.memset(k_ring[:], 0.0)
                    sm.inc(GG.memset(v_ring[:], 0.0), "G")
                    ring_z = C["G"]
                    for eng in (TT, VV, SS):
                        eng.wait_ge(H["G"], ring_z)

                    x1_ld = {}
                    g_rdy = {}
                    cast_v = {}
                    qkv_mm_t = {}
                    qkv_ev = {}
                    rot_rdy = {}
                    qkfm = {}
                    att_done = {}
                    pv_done = {}
                    psA_vfree = {}
                    u1_init_sd = {}
                    recip_v = {}
                    msk_ld = {}
                    rot_ld = {}
                    psC_free = 0
                    s2_rd = {}
                    s2mm_t = {}
                    tPt2 = {}
                    tPV2 = {}
                    ptb_ev = {}
                    exp_dn = {}
                    g_tr_t = {}
                    u1tr_t = {}

                    def x1src(mg):
                        if mg < HT_G:
                            return x1h_scr[mg * P:(mg + 1) * P, :]
                        return x1_scr[(mg - HT_G) * P:
                                      (mg - HT_G + 1) * P, :]

                    def load_rot(mg):
                        SY.wait_ge(H["V"], rot_rdy.get(mg - 2, 0))
                        r0 = ROT0[l](mg)
                        sm.inc(SY.dma_start(
                            out=rtab[:, (mg % 2) * 256:(mg % 2 + 1) * 256],
                            in_=rot_d[r0:r0 + P, :]), "dM", 16)
                        rot_ld[mg] = C["dM"]

                    def load_msk(mg):
                        SY.wait_ge(H["V"], att_done.get(mg - 2, 0))
                        sm.inc(SY.dma_start(
                            out=mskS[:, (mg % 2) * KW:
                                     (mg % 2 + 1) * KW],
                            in_=MSKD[l][(mg - att0) * P:
                                        (mg - att0 + 1) * P, :]),
                            "dM", 16)
                        msk_ld[mg] = C["dM"]

                    MSKD = [msk1_d, msk2_d]

                    if l == 0:
                        for i in range(min(2, ntl)):
                            sm.inc(SY.dma_start(
                                out=x1stg[:, (i % 2) * D:(i % 2 + 1) * D],
                                in_=x1src(i)), "dM", 16)
                            x1_ld[i] = C["dM"]
                    for i in range(QKV0[l], min(QKV0[l] + 2, ntl)):
                        load_rot(i)

                    def cast_u1c(mg):
                        lt = u1of(mg)
                        if mg >= 2:
                            VV.wait_ge(H["T"], qkv_mm_t.get(mg - 2, 0))
                        src = u1v_[:, :, lt:lt + 129]
                        _in = VV.tensor_copy(
                            g1cv[:, (mg % 2) * DC:(mg % 2) * DC + DC, :],
                            src)
                        sm.inc(_in, "V")
                        cast_v[mg] = C["V"]

                    if l == 1:
                        cast_u1c(0)

                    def emit_G(m):
                        nonlocal psC_free
                        if m < QKV0[l]:
                            att_done[m] = att_done.get(m - 1, 0)
                            pv_done[m] = pv_done.get(m - 1, 0)
                            psA_vfree[m] = psA_vfree.get(m - 1, 0)
                            return
                        if m < att0:
                            att_done[m] = att_done.get(m - 1, 0)
                            pv_done[m] = qkv_mm_t[m]
                            psA_vfree[m] = psA_vfree.get(m - 1, 0)
                            return
                        lt = u1of(m)
                        qfb = (m % 2) * 4 * P
                        slots = [(m - 4 + i) % 8 for i in range(NW)]
                        runs = []
                        rs = 0
                        for i in range(1, NW + 1):
                            if i == NW or slots[i] != slots[i - 1] + 1:
                                runs.append((rs, i))
                                rs = i
                        runs2 = []
                        for a, b in runs:
                            while a < b:
                                nb = min(b, ((a // 4) + 1) * 4)
                                runs2.append((a, nb))
                                a = nb

                        def e_S2(h):
                            sl2 = h % 2
                            hb = sl2 * 64
                            hc = h // 2
                            if h == 0:
                                TT.wait_ge(H["S"], qkfm[(m, 1)])
                                TT.wait_ge(H["V"],
                                           att_done.get(m - 1, 0))
                            if (m, h - 2) in ptb_ev:
                                TT.wait_ge(H["S"], ptb_ev[(m, h - 2)])
                            elif (m - 1, h + 6) in ptb_ev:
                                TT.wait_ge(H["S"],
                                           ptb_ev[(m - 1, h + 6)])
                            for a, b in runs2:
                                _mm = TT.matmul(
                                    psB[:, sl2 * 1024 + a * P:
                                        sl2 * 1024 + b * P],
                                    q_fm[hb:hb + 64,
                                         qfb + hc * P:qfb + (hc + 1) * P],
                                    krv[hb:hb + 64, hc,
                                        slots[a] * P:
                                        slots[b - 1] * P + P],
                                    start=(a % 4 == 0),
                                    stop=(b % 4 == 0 or b == NW))
                            sm.inc(_mm, "T")
                            s2mm_t[(m, h)] = C["T"]

                        def e_mask(h):
                            sl2 = h % 2
                            VV.wait_ge(H["T"], s2mm_t[(m, h)])
                            if h == 0:
                                VV.wait_ge(H["dM"], msk_ld[m])
                            if (m, h - 2) in exp_dn:
                                VV.wait_ge(H["S"], exp_dn[(m, h - 2)])
                            elif (m - 1, h + 6) in exp_dn:
                                VV.wait_ge(H["S"],
                                           exp_dn[(m - 1, h + 6)])
                            VV.tensor_tensor(
                                scr[:, sl2 * KW:(sl2 + 1) * KW],
                                psB[:, sl2 * 1024:sl2 * 1024 + KW],
                                mskS[:, (m % 2) * KW:
                                     (m % 2 + 1) * KW], ALU.add)
                            _in = VV.reduce_max(
                                nm2[:, sl2:sl2 + 1],
                                scr[:, sl2 * KW:(sl2 + 1) * KW],
                                axis=mybir.AxisListType.X, negate=True)
                            sm.inc(_in, "V")
                            s2_rd[(m, h)] = C["V"]

                        def e_exp(h):
                            sl2 = h % 2
                            SS.wait_ge(H["V"], s2_rd[(m, h)])
                            if h == 0:
                                SS.wait_ge(H["V"],
                                           recip_v.get(m - 1, 0))
                            if (m, h - 2) in tPt2:
                                SS.wait_ge(H["T"], tPt2[(m, h - 2)])
                            elif (m - 1, h + 6) in tPt2:
                                SS.wait_ge(H["T"],
                                           tPt2[(m - 1, h + 6)])
                            _in = SS.activation(
                                p2[:, sl2 * KW:(sl2 + 1) * KW],
                                scr[:, sl2 * KW:(sl2 + 1) * KW],
                                AF.Exp, bias=nm2[:, sl2:sl2 + 1],
                                scale=f32(1.0),
                                accum_out=lsum[:, h:h + 1])
                            sm.inc(_in, "S")
                            exp_dn[(m, h)] = C["S"]

                        def e_Pt(h):
                            sl2 = h % 2
                            TT.wait_ge(H["S"], exp_dn[(m, h)])
                            for c in range(NW):
                                _mm = TT.matmul(
                                    psB[:, sl2 * 1024 + c * P:
                                        sl2 * 1024 + (c + 1) * P],
                                    p2[:, sl2 * KW + c * P:
                                        sl2 * KW + (c + 1) * P],
                                    identb[:],
                                    start=(c == 0 or c == 4),
                                    stop=(c == 3 or c == 4))
                            sm.inc(_mm, "T")
                            tPt2[(m, h)] = C["T"]

                        def e_ptcopy(h):
                            sl2 = h % 2
                            SS.wait_ge(H["T"], tPt2[(m, h)])
                            if (m, h - 2) in tPV2:
                                SS.wait_ge(H["T"], tPV2[(m, h - 2)])
                            elif (m - 1, h + 6) in tPV2:
                                SS.wait_ge(H["T"],
                                           tPV2[(m - 1, h + 6)])
                            _in = SS.copy(
                                ptb2[:, sl2 * KW:(sl2 + 1) * KW],
                                psB[:, sl2 * 1024:sl2 * 1024 + KW])
                            sm.inc(_in, "S")
                            ptb_ev[(m, h)] = C["S"]

                        def e_PV(h):
                            sl2 = h % 2
                            TT.wait_ge(H["S"], ptb_ev[(m, h)])
                            if h == 0:
                                TT.wait_ge(H["V"],
                                           psA_vfree.get(m - 1, 0))
                            for i in range(NW):
                                stv = slots[i]
                                _mm = TT.matmul(
                                    psPV[:, h * DH:(h + 1) * DH],
                                    ptb2[:, sl2 * KW + i * P:
                                         sl2 * KW + (i + 1) * P],
                                    v_ring[:, stv * INNER + h * DH:
                                           stv * INNER +
                                           (h + 1) * DH],
                                    start=(h == 0 and i == 0),
                                    stop=(h == HEADS - 1 and
                                          i == NW - 1))
                            sm.inc(_mm, "T")
                            tPV2[(m, h)] = C["T"]

                        e_S2(0)
                        e_mask(0)
                        e_exp(0)
                        for h in range(1, HEADS):
                            e_S2(h)
                            e_mask(h)
                            e_Pt(h - 1)
                            e_ptcopy(h - 1)
                            e_PV(h - 1)
                            e_exp(h)
                        e_Pt(HEADS - 1)
                        e_ptcopy(HEADS - 1)
                        e_PV(HEADS - 1)
                        pv_mm = C["T"]

                        VV.wait_ge(H["S"], exp_dn[(m, HEADS - 1)])
                        sm.inc(VV.reciprocal(lrec[:], lsum[:]), "V")
                        recip_v[m] = C["V"]
                        VV.wait_ge(H["V"], recip_v[m])
                        VV.wait_ge(H["T"], pv_mm)
                        _in = VV.tensor_tensor(
                            otm_sb[:].rearrange("p (h c) -> p h c",
                                                c=DH),
                            psPV[:].rearrange("p (h c) -> p h c", c=DH),
                            lrec[:].unsqueeze(2)
                            .broadcast_to([P, HEADS, DH]), ALU.mult)
                        sm.inc(_in, "V")
                        psA_vfree[m] = C["V"]
                        otm_ev = C["V"]
                        TT.wait_ge(H["V"], otm_ev)
                        TT.wait_ge(H["S"], psC_free)
                        for hc in range(4):
                            _mm = TT.matmul(
                                psC[:, hc * P:(hc + 1) * P],
                                otm_sb[:, hc * P:(hc + 1) * P],
                                identb[:], start=(hc == 0),
                                stop=(hc == 3))
                        sm.inc(_mm, "T")
                        otr = C["T"]
                        SS.wait_ge(H["T"], otr)
                        _in = SS.copy(o_fm[:], psC[:])
                        sm.inc(_in, "S")
                        psC_free = C["S"]
                        o_ev = C["S"]
                        TT.wait_ge(H["S"], o_ev)
                        TT.wait_ge(H["S"], ptb_ev[(m, 6)])
                        for kc in range(4):
                            for mc in range(8):
                                _mm = TT.matmul(
                                    psB[:, mc * P:(mc + 1) * P],
                                    wov[:, kc, mc * P:(mc + 1) * P],
                                    o_fm[:, kc * P:(kc + 1) * P],
                                    start=(kc == 0 and mc % 4 == 0),
                                    stop=(kc == 3 and mc % 4 == 3))
                        sm.inc(_mm, "T")
                        wo_mm = C["T"]
                        pv_done[m] = wo_mm
                        VV.wait_ge(H["T"], wo_mm)
                        if l == 0:
                            VV.wait_ge(H["S"], u1_init_sd.get(m, 0))
                        u1t = u1v_[:, :, 1 + lt:1 + lt + P]
                        _in = VV.tensor_tensor(
                            u1t, u1t,
                            psB[:, 0:1024].rearrange(
                                "p (ch n) -> p ch n", n=P), ALU.add)
                        sm.inc(_in, "V")
                        att_done[m] = C["V"]

                    def emit_E(m):
                        if m < QKV0[l]:
                            return
                        qb2 = (m % 2) * 1536
                        SS.wait_ge(H["V"], rot_rdy.get(m - 1, 0))
                        _in = SS.activation(rsq[:],
                                            qkv_s[:, qb2:qb2 + 1024],
                                            AF.Square)
                        sm.inc(_in, "S")
                        sqd = C["S"]
                        VV.wait_ge(H["S"], sqd)
                        sm.inc(VV.reduce_sum(
                            rn_s[:].unsqueeze(2),
                            rsq[:].rearrange("p (h c) -> p h c", c=DH),
                            axis=mybir.AxisListType.X), "V")
                        VV.wait_ge(H["V"], C["V"])
                        ssqd = C["V"]
                        SS.wait_ge(H["V"], ssqd)
                        SS.activation(rn_s[:], rn_s[:], AF.Ln,
                                      bias=f32(1e-38), scale=f32(1.0))
                        sm.inc(SS.activation(rn_s[:], rn_s[:], AF.Exp,
                                             scale=f32(-0.5)), "S")
                        rnd = C["S"]
                        VV.wait_ge(H["S"], rnd)
                        VV.tensor_scalar_min(rn_s[:], rn_s[:], 1.0e12)
                        VV.wait_ge(H["dM"], rot_ld[m])
                        VV.wait_ge(H["V"], qsb_rdy)
                        VV.wait_ge(H["T"], pv_done.get(m - 2, 0))
                        VV.wait_ge(H["S"], qkfm.get((m - 1, 1), 0))
                        rt = rtab[:, (m % 2) * 256:(m % 2 + 1) * 256]
                        for side in range(2):
                            base = qb2 + side * 512
                            part = qkv_s[:, base:base + 512]
                            part3 = part.rearrange("p (h c) -> p h c",
                                                   c=DH)
                            VV.tensor_tensor(
                                part3, part3,
                                rn_s[:, side * 8:side * 8 + 8]
                                .unsqueeze(2).broadcast_to([P, 8, DH]),
                                ALU.mult)
                            VV.tensor_tensor(
                                part, part,
                                (QSb if side == 0 else KSb)[:], ALU.mult)
                            sh4 = qsh[:].rearrange("p (h s c) -> p h s c",
                                                   s=2, c=32)
                            pt4 = part.rearrange("p (h s c) -> p h s c",
                                                 s=2, c=32)
                            VV.tensor_copy(sh4[:, :, 0:1, :],
                                           pt4[:, :, 1:2, :])
                            VV.tensor_copy(sh4[:, :, 1:2, :],
                                           pt4[:, :, 0:1, :])
                            cq = rt[:, side * 128:side * 128 + DH]
                            bq = rt[:, side * 128 + DH:
                                    side * 128 + 2 * DH]
                            VV.tensor_tensor(
                                part3, part3,
                                cq.unsqueeze(1).broadcast_to([P, 8, DH]),
                                ALU.mult)
                            VV.tensor_tensor(
                                qsh[:].rearrange("p (h c) -> p h c",
                                                 c=DH),
                                qsh[:].rearrange("p (h c) -> p h c",
                                                 c=DH),
                                bq.unsqueeze(1).broadcast_to([P, 8, DH]),
                                ALU.mult)
                            _in = VV.tensor_tensor(
                                (qbf if side == 0 else kbf2)[:],
                                part, qsh[:], ALU.add)
                        st = m % 8
                        _in = VV.tensor_copy(
                            v_ring[:, st * INNER:(st + 1) * INNER],
                            qkv_s[:, qb2 + 1024:qb2 + 1536])
                        sm.inc(_in, "V")
                        rot_rdy[m] = C["V"]

                    def emit_F(m):
                        nonlocal psC_free
                        if m < QKV0[l]:
                            return
                        st = m % 8
                        TT.wait_ge(H["V"], rot_rdy[m])
                        for which in range(2):
                            srcb = qbf if which == 0 else kbf2
                            TT.wait_ge(H["S"], psC_free)
                            for hc in range(4):
                                _mm = TT.matmul(
                                    psC[:, hc * P:(hc + 1) * P],
                                    srcb[:, hc * P:(hc + 1) * P],
                                    identb[:], start=(hc == 0),
                                    stop=(hc == 3))
                            sm.inc(_mm, "T")
                            trq = C["T"]
                            SS.wait_ge(H["T"], trq)
                            if which == 0:
                                dst = q_fm[:, (m % 2) * 4 * P:
                                           (m % 2 + 1) * 4 * P].rearrange(
                                    "p (hc n) -> p hc n", n=P)
                            else:
                                dst = krv[:, :, st * P:(st + 1) * P]
                            sm.inc(SS.copy(dst, psC[:].rearrange(
                                "p (hc n) -> p hc n", n=P)), "S")
                            psC_free = C["S"]
                            qkfm[(m, which)] = C["S"]

                    for mg in range(ntl):
                        lt = u1of(mg)
                        if l == 0:
                            VV.wait_ge(H["dM"], x1_ld[mg])
                            xs = x1stg[:, (mg % 2) * D:(mg % 2 + 1) * D]
                            VV.bn_stats(stt2[:, 0:6], xs[:, 0:512])
                            sm.inc(VV.bn_stats(stt2[:, 6:12],
                                               xs[:, 512:1024]), "V")
                            VV.wait_ge(H["V"], C["V"])
                            sm.inc(VV.bn_aggr(agg2[:], stt2[:]), "V")
                            bnd = C["V"]
                            SS.wait_ge(H["V"], bnd)
                            SS.activation(rstd2[:], agg2[:, 1:2], AF.Ln,
                                          bias=f32(1e-5), scale=f32(1.0))
                            sm.inc(SS.activation(rstd2[:], rstd2[:],
                                                 AF.Exp, scale=f32(-0.5)),
                                   "S")
                            rsd = C["S"]
                            VV.wait_ge(H["S"], rsd)
                            VV.wait_ge(H["T"], g_tr_t.get(mg - 1, 0))
                            VV.wait_ge(H["T"], u1tr_t.get(mg - 1, 0))
                            VV.tensor_scalar(
                                gt[:], xs,
                                agg2[:, 0:1], rstd2[:],
                                ALU.subtract, ALU.mult)
                            sm.inc(VV.tensor_copy(
                                gbf[:], gt[:]), "V")
                            g_rdy[mg] = C["V"]
                            TT.wait_ge(H["V"], g_rdy[mg])
                            if mg >= 2:
                                TT.wait_ge(H["T"], qkv_mm_t.get(mg - 2, 0))
                            for c2 in range(2):
                                TT.wait_ge(H["S"], psC_free)
                                for c in range(4):
                                    ch = c2 * 4 + c
                                    _mm = TT.matmul(
                                        psC[:, c * P:(c + 1) * P],
                                        gbf[:, ch * P:(ch + 1) * P],
                                        identb[:], start=(c == 0),
                                        stop=(c == 3))
                                sm.inc(_mm, "T")
                                trc = C["T"]
                                SS.wait_ge(H["T"], trc)
                                dst = g1cv[:, (mg % 2) * DC + c2 * 4:
                                           (mg % 2) * DC + (c2 + 1) * 4,
                                           1:129]
                                sm.inc(SS.copy(dst, psC[:].rearrange(
                                    "p (c q) -> p c q", c=4)), "S")
                                psC_free = C["S"]
                            g_tr_t[mg] = C["T"]
                            g1c_s = psC_free
                            if mg > 0:
                                VV.wait_ge(H["S"], g1c_s)
                                _in = VV.tensor_copy(
                                    g1cv[:, (mg % 2) * DC:
                                         (mg % 2) * DC + DC, 0:1],
                                    g1cv[:, ((mg - 1) % 2) * DC:
                                         ((mg - 1) % 2) * DC + DC,
                                         128:129])
                                if mg == HT_G:
                                    _in = VV.tensor_tensor(
                                        g1cv[:, (mg % 2) * DC:
                                             (mg % 2) * DC + DC, 0:1],
                                        g1cv[:, (mg % 2) * DC:
                                             (mg % 2) * DC + DC, 0:1],
                                        hz_s[:].unsqueeze(2), ALU.mult)
                            else:
                                _in = VV.memset(
                                    g1cv[:, (mg % 2) * DC:
                                         (mg % 2) * DC + DC, 0:1], 0.0)
                            sm.inc(_in, "V")
                            g_full_v = C["V"]
                            g_full_s = g1c_s
                        else:
                            if mg + 1 < ntl:
                                cast_u1c(mg + 1)
                            g_full_v = cast_v[mg]
                            g_full_s = 0

                        if l == 0 and mg >= att0:
                            TT.wait_ge(H["V"], g_rdy[mg])
                            for c2 in range(2):
                                TT.wait_ge(H["S"], psC_free)
                                for c in range(4):
                                    ch = c2 * 4 + c
                                    _mm = TT.matmul(
                                        psC[:, c * P:(c + 1) * P],
                                        gt[:, ch * P:(ch + 1) * P],
                                        identf[:], start=(c == 0),
                                        stop=(c == 3))
                                sm.inc(_mm, "T")
                                trc = C["T"]
                                SS.wait_ge(H["T"], trc)
                                dst = u1v_[:, c2 * 4:(c2 + 1) * 4,
                                           1 + lt:1 + lt + P]
                                sm.inc(SS.copy(dst, psC[:].rearrange(
                                    "p (c q) -> p c q", c=4)), "S")
                                psC_free = C["S"]
                            u1tr_t[mg] = C["T"]
                            u1_init_sd[mg] = psC_free
                        if l == 0 and mg + 2 < ntl:
                            SY.wait_ge(H["V"], g_rdy[mg])
                            SY.wait_ge(H["T"], u1tr_t.get(mg, 0))
                            sm.inc(SY.dma_start(
                                out=x1stg[:, (mg % 2) * D:
                                          (mg % 2 + 1) * D],
                                in_=x1src(mg + 2)), "dM", 16)
                            x1_ld[mg + 2] = C["dM"]

                        if mg < QKV0[l]:
                            emit_G(mg - 2)
                            continue

                        emit_E(mg - 1)

                        qb = (mg % 2) * 1536
                        TT.wait_ge(H["V"], g_full_v)
                        if l == 0:
                            TT.wait_ge(H["S"], g_full_s)
                        TT.wait_ge(H["dW"], w_rdy2)
                        TT.wait_ge(H["S"], qkv_ev.get(mg - 1, 0))
                        for ec in range(DC):
                            off = 1 if ec < 4 else 0
                            lhs = g1cv[:, (mg % 2) * DC + ec, off:off + P]
                            for np_ in range(2):
                                _mm = TT.matmul(
                                    psQ[:, np_ * 512:(np_ + 1) * 512],
                                    lhs,
                                    wqv[:, ec, np_ * 512:(np_ + 1) * 512],
                                    start=(ec == 0), stop=(ec == DC - 1))
                        sm.inc(_mm, "T")
                        qkmm = C["T"]
                        SS.wait_ge(H["T"], qkmm)
                        SS.wait_ge(H["V"], rot_rdy.get(mg - 2, 0))
                        sm.inc(SS.copy(qkv_s[:, qb:qb + 1024],
                                       psQ[:, 0:1024]), "S")
                        qk_ev1 = C["S"]
                        TT.wait_ge(H["S"], qk_ev1)
                        for ec in range(DC):
                            off = 1 if ec < 4 else 0
                            lhs = g1cv[:, (mg % 2) * DC + ec, off:off + P]
                            _mm = TT.matmul(
                                psQ[:, 0:512], lhs,
                                wqv[:, ec, 1024:1536],
                                start=(ec == 0), stop=(ec == DC - 1))
                        sm.inc(_mm, "T")
                        qkv_mm_t[mg] = C["T"]
                        SS.wait_ge(H["T"], qkv_mm_t[mg])
                        sm.inc(SS.copy(qkv_s[:, qb + 1024:qb + 1536],
                                       psQ[:, 0:512]), "S")
                        qkv_ev[mg] = C["S"]

                        emit_F(mg - 1)
                        emit_G(mg - 2)
                        if mg + 1 < ntl and mg + 1 >= QKV0[l] + 2:
                            load_rot(mg + 1)
                        if mg >= att0:
                            load_msk(mg)

                    emit_E(ntl - 1)
                    emit_F(ntl - 1)
                    emit_G(ntl - 2)
                    emit_G(ntl - 1)
                    join()
                    zero_row_m1()
                if s2stop <= 1 + 2 * l:
                    return

                with contextlib.ExitStack() as sb_:
                    def sbb(name, shape, dt):
                        return sb_.enter_context(
                            nc.sbuf_tensor(f"{name}b_l{l}", shape, dt))

                    def psb_(name, shape, dt=F32):
                        return sb_.enter_context(
                            nc.psum_tensor(f"{name}b_l{l}", shape, dt))

                    nft = (HT_A1 + NT) if l == 0 else NT
                    ft0 = ATT0[0] if l == 0 else HT_K2
                    h_ln = sbb("h_ln", [P, DC * nft * P], BF16)
                    sqf = sbb("sqf", [P, D], F32)
                    srow = sbb("srow", [1, 2 * P], F32)
                    bcs = sbb("bcs", [P, 2 * P], F32)
                    lngb = sbb("lngb", [P, DC], F32)
                    lnbb = sbb("lnbb", [P, DC], F32)
                    b2row = sbb("b2row", [1, D], F32)
                    w1a_p = sbb("w1a_p", [P, DC * 512], BF16)
                    w1g_p = sbb("w1g_p", [P, DC * 512], BF16)
                    w2_p = sbb("w2_p", [P, 4 * D], BF16)
                    b1a_p = sbb("b1a_p", [1, 512], BF16)
                    b1g_p = sbb("b1g_p", [1, 512], BF16)
                    gel = sbb("gel", [P, 512], BF16)
                    pm_sb = sbb("pm_sb", [P, 512], BF16)
                    pmT4 = sbb("pmT4", [P, 4 * 4 * P], BF16)
                    ones5 = sbb("ones5", [1, 512], F32)

                    prep_ps = contextlib.ExitStack()
                    hs_ps = prep_ps.enter_context(
                        nc.psum_tensor(f"hs_psb_l{l}", [1, D], F32))
                    bc_ps = prep_ps.enter_context(
                        nc.psum_tensor(f"bc_psb_l{l}", [P, 2 * P], F32))

                    hv_ = h_ln[:].rearrange("p (ch n) -> p ch n",
                                            n=nft * P)
                    w1av = w1a_p[:].rearrange("p (ec n) -> p ec n", n=512)
                    w1gv = w1g_p[:].rearrange("p (ec n) -> p ec n", n=512)

                    sm.inc(SY.dma_start(out=lngb[:], in_=lng_d[l]),
                           "dM", 16)
                    sm.inc(SY.dma_start(out=lnbb[:], in_=lnb_d[l]),
                           "dM", 16)
                    sm.inc(SY.dma_start(out=b2row[:], in_=b2r_d[l]),
                           "dM", 16)
                    lnp_ld = C["dM"]

                    h_app = {}
                    bc_ev = {}
                    for ti in range(nft):
                        mgu = ft0 + ti
                        lt = u1of(mgu)
                        uv = u1[:].rearrange("p (ch n) -> p ch n", n=CW1)
                        vA = uv[:, 0:4, 1 + lt:1 + lt + P]
                        vB = uv[:, 4:8, lt:lt + P]
                        TT.wait_ge(H["V"], h_app.get(ti - 1, 0))
                        for gi, vv_ in ((0, vA), (1, vB)):
                            _mm = TT.matmul(
                                hs_ps[:, gi * 512:(gi + 1) * 512]
                                .rearrange("o (c n) -> o c n", n=P),
                                ones_c[:], vv_, start=True, stop=True)
                        sm.inc(_mm, "T")
                        smm = C["T"]
                        VV.wait_ge(H["T"], smm)
                        sm.inc(VV.reduce_sum(
                            srow[0:1, 0:P].unsqueeze(2),
                            hs_ps[0:1, 0:D].rearrange(
                                "o (g c n) -> o n (g c)", g=2, n=P),
                            axis=mybir.AxisListType.X), "V")
                        ssum = C["V"]
                        SS.wait_ge(H["V"], h_app.get(ti - 1, 0))
                        SS.activation(
                            sqf[:, 0:512].rearrange(
                                "p (c n) -> p c n", n=P), vA, AF.Square)
                        _in = SS.activation(
                            sqf[:, 512:1024].rearrange(
                                "p (c n) -> p c n", n=P), vB, AF.Square)
                        sm.inc(_in, "S")
                        sqd2 = C["S"]
                        TT.wait_ge(H["S"], sqd2)
                        TT.wait_ge(H["V"], ssum)
                        for gi in range(2):
                            _mm = TT.matmul(
                                hs_ps[:, gi * 512:(gi + 1) * 512],
                                ones_c[:],
                                sqf[:, gi * 512:(gi + 1) * 512],
                                start=True, stop=True)
                        sm.inc(_mm, "T")
                        qmm = C["T"]
                        VV.wait_ge(H["T"], qmm)
                        sm.inc(VV.reduce_sum(
                            srow[0:1, P:2 * P].unsqueeze(2),
                            hs_ps[0:1, 0:D].rearrange(
                                "o (g c n) -> o n (g c)", g=2, n=P),
                            axis=mybir.AxisListType.X), "V")
                        VV.wait_ge(H["V"], C["V"])
                        VV.tensor_scalar_mul(srow[0:1, 0:P],
                                             srow[0:1, 0:P], 1.0 / D)
                        sm.inc(VV.tensor_scalar_mul(
                            srow[0:1, P:2 * P], srow[0:1, P:2 * P],
                            1.0 / D), "V")
                        mud = C["V"]
                        SS.wait_ge(H["V"], mud)
                        _in = SS.activation(sqf[0:1, 0:P],
                                            srow[0:1, 0:P], AF.Square)
                        sm.inc(_in, "S")
                        mu2 = C["S"]
                        VV.wait_ge(H["S"], mu2)
                        sm.inc(VV.tensor_tensor(
                            srow[0:1, P:2 * P], srow[0:1, P:2 * P],
                            sqf[0:1, 0:P], ALU.subtract), "V")
                        vard = C["V"]
                        SS.wait_ge(H["V"], vard)
                        SS.activation(srow[0:1, P:2 * P],
                                      srow[0:1, P:2 * P], AF.Ln,
                                      bias=f32(1e-5), scale=f32(1.0))
                        _in = SS.activation(srow[0:1, P:2 * P],
                                            srow[0:1, P:2 * P], AF.Exp,
                                            scale=f32(-0.5))
                        sm.inc(_in, "S")
                        rrd = C["S"]
                        TT.wait_ge(H["S"], rrd)
                        TT.wait_ge(H["S"], bc_ev.get(ti - 1, 0))
                        for gi in range(2):
                            _mm = TT.matmul(
                                bc_ps[:, gi * P:(gi + 1) * P], ones_r[:],
                                srow[0:1, gi * P:(gi + 1) * P],
                                start=(gi == 0), stop=(gi == 1))
                        sm.inc(_mm, "T")
                        bcm = C["T"]
                        SS.wait_ge(H["T"], bcm)
                        sm.inc(SS.copy(bcs[:], bc_ps[:]), "S")
                        bc_ev[ti] = C["S"]
                        VV.wait_ge(H["S"], bc_ev[ti])
                        VV.wait_ge(H["dM"], lnp_ld)
                        sq3 = sqf[:].rearrange("p (c n) -> p c n", n=P)[
                            :, 0:4, :]
                        for gi, vv_ in ((0, vA), (1, vB)):
                            dst = hv_[:, gi * 4:(gi + 1) * 4,
                                      ti * P:(ti + 1) * P]
                            mub = bcs[:, 0:P].unsqueeze(1) \
                                .broadcast_to([P, 4, P])
                            rb = bcs[:, P:2 * P].unsqueeze(1) \
                                .broadcast_to([P, 4, P])
                            VV.tensor_tensor(sq3, vv_, mub, ALU.subtract)
                            VV.tensor_tensor(sq3, sq3, rb, ALU.mult)
                            VV.tensor_tensor(
                                sq3, sq3,
                                lngb[:, gi * 4:(gi + 1) * 4].unsqueeze(2)
                                .broadcast_to([P, 4, P]), ALU.mult)
                            _in = VV.tensor_tensor(
                                dst, sq3,
                                lnbb[:, gi * 4:(gi + 1) * 4].unsqueeze(2)
                                .broadcast_to([P, 4, P]), ALU.add)
                        sm.inc(_in, "V")
                        h_app[ti] = C["V"]
                    join()

                    prep_ps.close()
                    ab_ps = psb_("ab_ps", [P, 2 * 512])
                    pt_ps2 = psb_("pt_ps2", [P, 512])
                    y_ps4 = psb_("y_ps4", [P, 4 * 512])
                    sm.inc(GG.memset(ones5[:], 1.0), "G")
                    ones5_rdy = C["G"]

                    w_ld = {}
                    gate_dn = {}
                    geld = {}
                    ptv = {}
                    last_w2m = 0
                    prev_gate = 0
                    prev_geld = 0
                    prev_ptev = 0
                    prev_y4 = 0
                    groups = [list(range(g, min(g + 4, nft)))
                              for g in range(0, nft, 4)]
                    pm4 = pmT4[:].rearrange("p (kc n) -> p kc n", n=4 * P)
                    y4v = y_ps4[:].rearrange("p (c n) -> p c n", n=512)
                    for pc in range(6):
                        w = GW[pc]
                        g0 = GOF[pc]
                        nkc = (w + P - 1) // P
                        GG.wait_ge(H["T"], last_w2m)
                        sm.inc(GG.dma_start(
                            out=w1av[:, :, 0:w],
                            in_=w1a_d[l].rearrange(
                                "(ec p) n -> p ec n", p=P)[
                                :, :, g0:g0 + w]), "dW", 16)
                        sm.inc(GG.dma_start(
                            out=w1gv[:, :, 0:w],
                            in_=w1g_d[l].rearrange(
                                "(ec p) n -> p ec n", p=P)[
                                :, :, g0:g0 + w]), "dW", 16)
                        for kc in range(nkc):
                            r0 = g0 + kc * P
                            r1 = min(g0 + w, r0 + P)
                            sm.inc(GG.dma_start(
                                out=w2_p[0:r1 - r0, kc * D:(kc + 1) * D],
                                in_=w2_d[l][r0:r1, :]), "dW", 16)
                        sm.inc(GG.dma_start(out=b1a_p[0:1, 0:w],
                                            in_=b1a_d[l][0:1, g0:g0 + w]),
                               "dW", 16)
                        sm.inc(GG.dma_start(out=b1g_p[0:1, 0:w],
                                            in_=b1g_d[l][0:1, g0:g0 + w]),
                               "dW", 16)
                        w_ld[pc] = C["dW"]
                        last_is_b2 = (pc == 5)

                        for grp in groups:
                            ntok = len(grp) * P
                            lt0 = u1of(ft0 + grp[0])
                            for si, ti in enumerate(grp):
                                TT.wait_ge(H["dW"], w_ld[pc])
                                TT.wait_ge(H["V"], prev_gate)
                                TT.wait_ge(H["S"], prev_geld)
                                for ec in range(DC):
                                    lhs = hv_[:, ec, ti * P:(ti + 1) * P]
                                    TT.matmul(ab_ps[:, 0:w], lhs,
                                              w1av[:, ec, 0:w],
                                              start=(ec == 0), stop=False)
                                    TT.matmul(ab_ps[:, 512:512 + w], lhs,
                                              w1gv[:, ec, 0:w],
                                              start=(ec == 0), stop=False)
                                TT.matmul(ab_ps[:, 0:w], onesb_r[:],
                                          b1a_p[0:1, 0:w], start=False,
                                          stop=True)
                                _mm = TT.matmul(ab_ps[:, 512:512 + w],
                                                onesb_r[:], b1g_p[0:1, 0:w],
                                                start=False, stop=True)
                                sm.inc(_mm, "T")
                                agm = C["T"]
                                SS.wait_ge(H["T"], agm)
                                _in = SS.activation(gel[:, 0:w],
                                                    ab_ps[:, 512:512 + w],
                                                    AF.Gelu)
                                sm.inc(_in, "S")
                                geld[(pc, ti)] = C["S"]
                                prev_geld = C["S"]
                                VV.wait_ge(H["S"], geld[(pc, ti)])
                                _in = VV.tensor_tensor(pm_sb[:, 0:w],
                                                       ab_ps[:, 0:w],
                                                       gel[:, 0:w],
                                                       ALU.mult)
                                sm.inc(_in, "V")
                                gate_dn[(pc, ti)] = C["V"]
                                prev_gate = C["V"]
                                TT.wait_ge(H["V"], gate_dn[(pc, ti)])
                                TT.wait_ge(H["S"], prev_ptev)
                                for kc in range(nkc):
                                    c0 = kc * P
                                    c1 = min(w, c0 + P)
                                    _mm = TT.matmul(
                                        pt_ps2[0:c1 - c0,
                                               kc * P:kc * P + P],
                                        pm_sb[:, c0:c1], identb[:],
                                        start=True, stop=True)
                                sm.inc(_mm, "T")
                                ptv[(pc, ti)] = C["T"]
                                SS.wait_ge(H["T"], ptv[(pc, ti)])
                                SS.wait_ge(H["T"], last_w2m)
                                _in = SS.copy(
                                    pm4[:, :, si * P:(si + 1) * P],
                                    pt_ps2[:].rearrange(
                                        "p (kc n) -> p kc n", n=P))
                                sm.inc(_in, "S")
                                ptv[('ev', pc, ti)] = C["S"]
                                prev_ptev = C["S"]
                            grp_ptev = prev_ptev
                            for mh in range(2):
                                TT.wait_ge(H["S"], grp_ptev)
                                TT.wait_ge(H["V"], prev_y4)
                                if last_is_b2 and mh == 0:
                                    TT.wait_ge(H["G"], ones5_rdy)
                                for kc in range(nkc):
                                    r0 = g0 + kc * P
                                    rows = min(g0 + w, r0 + P) - r0
                                    for mc4 in range(4):
                                        mc = mh * 4 + mc4
                                        _mm = TT.matmul(
                                            y_ps4[:, mc4 * 512:
                                                  mc4 * 512 + ntok],
                                            w2_p[0:rows,
                                                 kc * D + mc * P:
                                                 kc * D + (mc + 1) * P],
                                            pm4[0:rows, kc, 0:ntok],
                                            start=(kc == 0),
                                            stop=(kc == nkc - 1 and
                                                  not last_is_b2))
                                if last_is_b2:
                                    for mc4 in range(4):
                                        mc = mh * 4 + mc4
                                        _mm = TT.matmul(
                                            y_ps4[:, mc4 * 512:
                                                  mc4 * 512 + ntok],
                                            b2row[0:1, mc * P:(mc + 1) * P],
                                            ones5[0:1, 0:ntok],
                                            start=False, stop=True)
                                sm.inc(_mm, "T")
                                w2m = C["T"]
                                last_w2m = w2m
                                VV.wait_ge(H["T"], w2m)
                                u1t = u1[:].rearrange(
                                    "p (ch n) -> p ch n", n=CW1)[
                                    :, mh * 4:(mh + 1) * 4,
                                    1 + lt0:1 + lt0 + ntok]
                                _in = VV.tensor_tensor(
                                    u1t, u1t, y4v[:, :, 0:ntok], ALU.add)
                                sm.inc(_in, "V")
                                prev_y4 = C["V"]
                    join()
                    if l == 0:
                        zero_row_m1()
                if s2stop <= 2 + 2 * l:
                    return

            with contextlib.ExitStack() as sf:
                def sbf(name, shape, dt):
                    return sf.enter_context(nc.sbuf_tensor(name, shape, dt))

                x1rt = sbf("x1rt", [P, 2 * D], F32)
                x2st = sbf("x2st", [P, 2 * D], F32)
                zt = sbf("zt", [P, D], F32)
                xt_ps = sf.enter_context(nc.psum_tensor("xt_ps",
                                                        [P, 2 * D], F32))

                x1r_ld = {}
                for q in range(2):
                    sm.inc(SY.dma_start(out=x1rt[:, q * D:(q + 1) * D],
                                        in_=x1_scr[q * P:(q + 1) * P, :]),
                           "dM", 16)
                    x1r_ld[q] = C["dM"]
                tr_cons = {}
                x2_done = {}
                u1vf = u1[:].rearrange("p (ch n) -> p ch n", n=CW1)
                for q in range(NT):
                    lt = U1OF[1](HT_K2 + q)
                    slot = q % 2
                    TT.wait_ge(H["V"], tr_cons.get(q - 2, 0))
                    for c2 in range(2):
                        for c in range(4):
                            ch = c2 * 4 + c
                            _mm = TT.matmul(
                                xt_ps[:, slot * D + c2 * 512 + c * P:
                                      slot * D + c2 * 512 + (c + 1) * P],
                                u1vf[:, ch, 1 + lt:1 + lt + P],
                                identf[:], start=(c == 0),
                                stop=(c == 3))
                    sm.inc(_mm, "T")
                    trm = C["T"]
                    VV.wait_ge(H["T"], trm)
                    VV.bn_stats(stt2[:, 0:6],
                                xt_ps[:, slot * D:slot * D + 512])
                    sm.inc(VV.bn_stats(stt2[:, 6:12],
                                       xt_ps[:, slot * D + 512:
                                             slot * D + D]), "V")
                    VV.wait_ge(H["V"], C["V"])
                    sm.inc(VV.bn_aggr(agg2[:], stt2[:]), "V")
                    bnf = C["V"]
                    SS.wait_ge(H["V"], bnf)
                    SS.activation(rstd2[:], agg2[:, 1:2], AF.Ln,
                                  bias=f32(1e-5), scale=f32(1.0))
                    sm.inc(SS.activation(rstd2[:], rstd2[:], AF.Exp,
                                         scale=f32(-0.5)), "S")
                    rsf = C["S"]
                    VV.wait_ge(H["S"], rsf)
                    _in = VV.tensor_scalar(
                        zt[:], xt_ps[:, slot * D:slot * D + D],
                        agg2[:, 0:1], rstd2[:], ALU.subtract, ALU.mult)
                    sm.inc(_in, "V")
                    tr_cons[q] = C["V"]
                    VV.wait_ge(H["dM"], x1r_ld[q])
                    if q >= 2:
                        VV.wait_ge(H["dOU"], x2_done[q - 2])
                    _in = VV.tensor_tensor(
                        x2st[:, slot * D:(slot + 1) * D], zt[:],
                        x1rt[:, slot * D:(slot + 1) * D], ALU.add)
                    sm.inc(_in, "V")
                    x2v = C["V"]
                    if q + 2 < NT:
                        SY.wait_ge(H["V"], x2v)
                        sm.inc(SY.dma_start(
                            out=x1rt[:, slot * D:(slot + 1) * D],
                            in_=x1_scr[(q + 2) * P:(q + 3) * P, :]),
                            "dM", 16)
                        x1r_ld[q + 2] = C["dM"]
                    SY.wait_ge(H["V"], x2v)
                    sm.inc(SY.dma_start(
                        out=out_d[q * P:(q + 1) * P, :],
                        in_=x2st[:, slot * D:(slot + 1) * D]), "dOU", 16)
                    x2_done[q] = C["dOU"]

    return emit


# version 13
# speedup vs baseline: 1.1050x; 1.1050x over previous
import contextlib
import math
import numpy as np
import ml_dtypes
import concourse.bass as bass
import concourse.mybir as mybir
from concourse.bass_utils import run_bass_kernel_spmd

F32 = mybir.dt.float32
BF16 = mybir.dt.bfloat16
ALU = mybir.AluOpType
AF = mybir.ActivationFunctionType
f32 = np.float32

P = 128
D = 1024
DC = 8
T = 16384
NCORES = 8
TC = 2048
NT = 16
KB = 512
NKB = T // KB
HALO_G = 1408
NEG = -1.0e30

HEADS = 8
DH = 64
INNER = 512
HID = 2730
WIN = 512
KW = 640
NW = 5
QK_SCALE = 8.0

HT_G = 11
HT_K1 = 10
HT_A1 = 6
HT_K2 = 5
HT_A2 = 1
NFT = HT_K1 + NT

STAGE2_STUB = False



def _rotary_tables(pos):
    pos = np.asarray(pos, np.float64)
    half = np.arange(0, DH, 2, dtype=np.float64)
    inv_freq = 1.0 / (10000.0 ** (half / DH))
    freqs = pos[:, None] * inv_freq[None, :]
    freqs = np.concatenate([freqs, freqs], axis=-1)
    base = (half + 0.4 * DH) / (1.4 * DH)
    power = (pos - T // 2) / float(WIN // 2)
    scale = base[None, :] ** power[:, None]
    scale = np.concatenate([scale, scale], axis=-1)
    cos, sin = np.cos(freqs), np.sin(freqs)
    sgn = np.ones((1, DH)); sgn[0, :32] = -1.0
    cq = cos * scale * QK_SCALE
    bq = sin * scale * sgn * QK_SCALE
    ck = cos / scale
    bk = sin / scale * sgn
    return np.concatenate([cq, bq, ck, bk], axis=1).astype(np.float32)


def _local_masks(tile_starts):
    out = np.zeros((len(tile_starts), P, KW), np.float32)
    pp = np.arange(P)[:, None]
    cj = np.arange(KW)[None, :]
    for n, t0 in enumerate(tile_starts):
        i = t0 + pp
        j = t0 - WIN + cj
        allowed = (j <= i) & (j >= i - WIN) & (j >= 0)
        out[n][~allowed] = NEG
    return out


def _chunk_cols(vec):
    return np.ascontiguousarray(np.asarray(vec, np.float32).reshape(DC, P).T)


def make_inputs(full):
    x = np.asarray(full["L0_attn_x"], np.float32).reshape(T, D)
    wq = np.asarray(full["L0_attn_Wq"], np.float32)
    wk = np.asarray(full["L0_attn_Wk"], np.float32)
    wv = np.asarray(full["L0_attn_Wv"], np.float32)
    com = dict(
        wqT=np.ascontiguousarray(wq.T), wkT=np.ascontiguousarray(wk.T),
        wvT=np.ascontiguousarray(wv.T),
        bq_c=_chunk_cols(full["L0_attn_bq"]),
        bk_c=_chunk_cols(full["L0_attn_bk"]),
        bv_r=np.ascontiguousarray(np.asarray(full["L0_attn_bv"],
                                             np.float32)[None, :]),
        identb_in=np.eye(P, dtype=ml_dtypes.bfloat16),
        identf_in=np.eye(P, dtype=np.float32),
    )
    wqkv = np.asarray(full["L0_ffn_wqkv"], np.float32)
    wo = np.asarray(full["L0_ffn_wo"], np.float32)
    w1 = np.asarray(full["L0_ffn_w1"], np.float32)
    w2 = np.asarray(full["L0_ffn_w2"], np.float32)
    b1 = np.asarray(full["L0_ffn_b1"], np.float32)
    b2 = np.asarray(full["L0_ffn_b2"], np.float32)
    lng = np.asarray(full["L0_ffn_ln_g"], np.float32)
    lnb = np.asarray(full["L0_ffn_ln_b"], np.float32)
    qsc = np.asarray(full["L0_ffn_q_scale"], np.float32)
    ksc = np.asarray(full["L0_ffn_k_scale"], np.float32)
    for l in range(2):
        com[f"wqkv{l}"] = np.ascontiguousarray(wqkv[l])
        com[f"wo{l}"] = np.ascontiguousarray(wo[l])
        com[f"w1a{l}"] = np.ascontiguousarray(w1[l][:, :HID])
        com[f"w1g{l}"] = np.ascontiguousarray(w1[l][:, HID:])
        com[f"w2{l}"] = np.ascontiguousarray(w2[l])
        com[f"b1a{l}"] = np.ascontiguousarray(b1[l][None, :HID])
        com[f"b1g{l}"] = np.ascontiguousarray(b1[l][None, HID:])
        com[f"b2c{l}"] = _chunk_cols(b2[l])
        com[f"b2r{l}"] = np.ascontiguousarray(b2[l][None, :])
        com[f"lngc{l}"] = _chunk_cols(lng[l])
        com[f"lnbc{l}"] = _chunk_cols(lnb[l])
        com[f"qs{l}"] = np.ascontiguousarray(np.tile(qsc[l], HEADS)[None, :])
        com[f"ks{l}"] = np.ascontiguousarray(np.tile(ksc[l], HEADS)[None, :])
    in_maps = []
    for c in range(NCORES):
        s = c * TC
        m = dict(com)
        m["x0"] = np.ascontiguousarray(x[s:s + TC])
        m["hz"] = np.full((P, DC), 0.0 if c == 0 else 1.0, np.float32)
        pos = np.arange(s - HT_K1 * P, s + TC)
        m["rot"] = _rotary_tables(pos)
        m["msk1"] = _local_masks(
            [s - HT_A1 * P + P * n for n in range(HT_A1 + NT)]
        ).reshape((HT_A1 + NT) * P, KW)
        m["msk2"] = _local_masks(
            [s - HT_A2 * P + P * n for n in range(HT_A2 + NT)]
        ).reshape((HT_A2 + NT) * P, KW)
        in_maps.append(m)
    return in_maps



class Sems:
    def __init__(self, nc, ctx, names):
        self.h = {n: ctx.enter_context(nc.semaphore(n)) for n in names}
        self.c = {n: 0 for n in names}

    def inc(self, inst, name, n=1):
        inst.then_inc(self.h[name], n)
        self.c[name] += n
        return self.c[name]


def build_program(trunc=99, dbg=False, s2stop=99):
    nc = bass.Bass("TRN2", target_bir_lowering=False,
                   detect_race_conditions=False)

    def ein(name, shape, dt=F32):
        return nc.dram_tensor(name, shape, dt, kind="ExternalInput").ap()

    x0_d = ein("x0", [TC, D])
    wqT_d = ein("wqT", [D, D])
    wkT_d = ein("wkT", [D, D])
    wvT_d = ein("wvT", [D, D])
    bq_d = ein("bq_c", [P, DC])
    bk_d = ein("bk_c", [P, DC])
    bv_d = ein("bv_r", [1, D])
    identb_d = ein("identb_in", [P, P], BF16)
    identf_d = ein("identf_in", [P, P])
    rot_d = ein("rot", [NFT * P, 4 * DH])
    msk1_d = ein("msk1", [(HT_A1 + NT) * P, KW])
    msk2_d = ein("msk2", [(HT_A2 + NT) * P, KW])
    wqkv_d = [ein(f"wqkv{l}", [D, 3 * INNER]) for l in range(2)]
    wo_d = [ein(f"wo{l}", [INNER, D]) for l in range(2)]
    w1a_d = [ein(f"w1a{l}", [D, HID]) for l in range(2)]
    w1g_d = [ein(f"w1g{l}", [D, HID]) for l in range(2)]
    w2_d = [ein(f"w2{l}", [HID, D]) for l in range(2)]
    b1a_d = [ein(f"b1a{l}", [1, HID]) for l in range(2)]
    b1g_d = [ein(f"b1g{l}", [1, HID]) for l in range(2)]
    b2_d = [ein(f"b2c{l}", [P, DC]) for l in range(2)]
    b2r_d = [ein(f"b2r{l}", [1, D]) for l in range(2)]
    hz_d = ein("hz", [P, DC])
    lng_d = [ein(f"lngc{l}", [P, DC]) for l in range(2)]
    lnb_d = [ein(f"lnbc{l}", [P, DC]) for l in range(2)]
    qs_d = [ein(f"qs{l}", [1, INNER]) for l in range(2)]
    ks_d = [ein(f"ks{l}", [1, INNER]) for l in range(2)]

    out_d = nc.dram_tensor("x2", [TC, D], F32, kind="ExternalOutput").ap()

    kin_kg = [nc.dram_tensor(f"kin_k{g}", [D, KB], BF16).ap()
              for g in range(4)]
    kin_vg = [nc.dram_tensor(f"kin_v{g}", [4 * P, D], BF16).ap()
              for g in range(4)]
    kout_kg = [nc.dram_tensor(f"kout_k{g}", [NCORES * D, KB], BF16).ap()
               for g in range(4)]
    kout_vg = [nc.dram_tensor(f"kout_v{g}", [NCORES * 4 * P, D], BF16).ap()
               for g in range(4)]
    x1_scr = nc.dram_tensor("x1_scr", [TC, D], F32).ap()
    x1t_in = nc.dram_tensor("x1t_in", [HALO_G, D], F32).ap()
    x1t_out = nc.dram_tensor("x1t_out", [NCORES * HALO_G, D], F32).ap()
    x1h_scr = nc.dram_tensor("x1h_scr", [HALO_G, D], F32).ap()
    if dbg:
        dbg_o = nc.dram_tensor("dbg_o", [P, NT * D], F32,
                               kind="ExternalOutput").ap()
        dbg_l = nc.dram_tensor("dbg_l", [P, NT], F32,
                               kind="ExternalOutput").ap()
        dbg_m = nc.dram_tensor("dbg_m", [P, NT], F32,
                               kind="ExternalOutput").ap()
        dbg_q = nc.dram_tensor("dbg_q", [P, DC * TC], BF16,
                               kind="ExternalOutput").ap()

    with contextlib.ExitStack() as ctx:
        sm = Sems(nc, ctx, ["dX", "dW", "dKV", "dST", "dOU", "cc",
                            "T", "V", "S", "G", "dM", "dH"])
        H, C = sm.h, sm.c

        def sb(name, shape, dt):
            return ctx.enter_context(nc.sbuf_tensor(name, shape, dt))

        identb = sb("identb", [P, P], BF16)
        identf = sb("identf", [P, P], F32)
        ones_c = sb("ones_c", [P, 1], F32)
        ones_r = sb("ones_r", [1, P], F32)
        one1 = sb("one1", [1, 1], F32)
        onesb = sb("onesb", [P, 1], BF16)

        block = ctx.enter_context(nc.Block())
        TT, VV, SS, GG, SY = nc.tensor, nc.vector, nc.scalar, nc.gpsimd, nc.sync

        stage2_state = {}

        with contextlib.ExitStack() as s1:
            def sb1(name, shape, dt):
                return s1.enter_context(nc.sbuf_tensor(name, shape, dt))

            stt = sb1("stt", [P, 12], F32)
            agg = sb1("agg", [P, 2], F32)
            rstd = sb1("rstd", [P, 1], F32)
            q_s = sb1("q_s", [P, DC * TC], BF16)
            negm = sb1("negm", [P, NT], F32)
            l_s = sb1("l_s", [P, NT], F32)
            o_acc = sb1("o_acc", [P, NT * D], F32)

            p1sb_stack = contextlib.ExitStack()

            def sbp1(name, shape, dt):
                return p1sb_stack.enter_context(nc.sbuf_tensor(name, shape, dt))

            wq_s = sbp1("wq_s", [P, 64 * P], BF16)
            wk_s = sbp1("wk_s", [P, 64 * P], BF16)
            wv_s = sbp1("wv_s", [P, 64 * P], BF16)
            bq32 = sbp1("bq32", [P, DC], F32)
            bk_s = sbp1("bk_s", [P, DC], F32)
            bv_r = sbp1("bv_rs", [1, D], F32)
            BV = sbp1("BV", [P, D], F32)
            x0st = sbp1("x0st", [P, 2 * D], F32)
            h_tm = sbp1("h_tm", [P, D], F32)
            h_bf = sbp1("h_bf", [P, D], BF16)
            hf_s = sbp1("hf_s", [P, DC * 512], BF16)
            sq_s = sbp1("sq_s", [P, 512], F32)
            nrow = sbp1("nrow", [1, 512], F32)
            kst = sbp1("kst", [P, DC * 512], BF16)
            vst = sbp1("vst", [P, D], BF16)

            GG.memset(ones_c[:], 1.0)
            GG.memset(ones_r[:], 1.0)
            GG.memset(one1[:], 1.0)
            GG.memset(onesb[:], 1.0)
            sm.inc(GG.dma_start(out=identb[:], in_=identb_d), "dW", 16)
            sm.inc(GG.dma_start(out=identf[:], in_=identf_d), "dW", 16)
            wview = lambda w: w.rearrange("(ec p) n -> p ec n", p=P)
            sview = lambda t: t[:].rearrange("p (ec n) -> p ec n", n=DC * P)
            sm.inc(GG.dma_start(out=sview(wq_s), in_=wview(wqT_d)), "dW", 16)
            sm.inc(GG.dma_start(out=sview(wk_s), in_=wview(wkT_d)), "dW", 16)
            sm.inc(GG.dma_start(out=sview(wv_s), in_=wview(wvT_d)), "dW", 16)
            sm.inc(GG.dma_start(out=bq32[:], in_=bq_d), "dW", 16)
            sm.inc(GG.dma_start(out=bk_s[:], in_=bk_d), "dW", 16)
            sm.inc(GG.dma_start(out=bv_r[:], in_=bv_d), "dW", 16)
            w_rdy = C["dW"]

            with contextlib.ExitStack() as p1ps:
                def ps1(name, shape, dt=F32):
                    return p1ps.enter_context(nc.psum_tensor(name, shape, dt))

                pj_ps = ps1("pj_ps", [P, 2 * 512])
                v_ps = ps1("v_ps", [P, D])
                tr_ps = ps1("tr_ps", [P, 512])
                nrm_ps = ps1("nrm_ps", [1, 512])
                nm_ps = ps1("nm_ps", [P, 4])

                VV.wait_ge(H["dW"], w_rdy)
                sm.inc(VV.tensor_scalar_mul(bq32[:], bq32[:], 1.0 / 32.0),
                       "V")
                bq32_rdy = C["V"]

                TT.wait_ge(H["dW"], w_rdy)
                _mm = TT.matmul(v_ps[:, 0:512], ones_r[:], bv_r[:, 0:512],
                          start=True, stop=True)
                _mm = TT.matmul(v_ps[:, 512:1024], ones_r[:], bv_r[:, 512:1024],
                          start=True, stop=True)
                sm.inc(_mm, "T")
                bvp = C["T"]
                VV.wait_ge(H["T"], bvp)
                sm.inc(VV.tensor_copy(BV[:], v_ps[:]), "V")
                bv_rdy = C["V"]

                for i in range(2):
                    sm.inc(SY.dma_start(out=x0st[:, i * D:(i + 1) * D],
                                        in_=x0_d[i * P:(i + 1) * P, :]),
                           "dX", 16)

                qk_evac = [0] * 4
                v_evac_last = bv_rdy
                tr_evac_last = 0
                nm_read = [0] * 4
                nrow_done = 0
                kst_store = 0
                cc_cnt = {}
                x0_cons = [0] * NT
                vst_store = [0] * NT

                ngrp = {0: 0, 11: 1, 12: 2}.get(trunc, 4)
                for g_i in range(ngrp):
                    for ii in range(4):
                        i = g_i * 4 + ii
                        xs = x0st[:, (i % 2) * D:(i % 2 + 1) * D]
                        VV.wait_ge(H["dX"], 16 * (i + 1))
                        VV.bn_stats(stt[:, 0:6], xs[:, 0:512])
                        sm.inc(VV.bn_stats(stt[:, 6:12], xs[:, 512:1024]),
                               "V")
                        VV.wait_ge(H["V"], C["V"])
                        sm.inc(VV.bn_aggr(agg[:], stt[:]), "V")
                        bn_done = C["V"]
                        SS.wait_ge(H["V"], bn_done)
                        SS.activation(rstd[:], agg[:, 1:2], AF.Ln,
                                      bias=f32(1e-5), scale=f32(1.0))
                        sm.inc(SS.activation(rstd[:], rstd[:], AF.Exp,
                                             scale=f32(-0.5)), "S")
                        rstd_done = C["S"]
                        VV.wait_ge(H["S"], rstd_done)
                        VV.tensor_scalar(h_tm[:], xs, agg[:, 0:1], rstd[:],
                                         ALU.subtract, ALU.mult)
                        sm.inc(VV.tensor_copy(h_bf[:], h_tm[:]), "V")
                        x0_cons[i] = C["V"]
                        cast_done = C["V"]
                        if i + 2 < NT:
                            SY.wait_ge(H["V"], x0_cons[i])
                            sm.inc(SY.dma_start(
                                out=x0st[:, (i % 2) * D:(i % 2 + 1) * D],
                                in_=x0_d[(i + 2) * P:(i + 3) * P, :]),
                                "dX", 16)

                        TT.wait_ge(H["V"], cast_done)
                        if g_i > 0 and ii == 0:
                            TT.wait_ge(H["S"], qk_evac[g_i - 1])
                        for c2 in range(2):
                            TT.wait_ge(H["S"], tr_evac_last)
                            for c in range(4):
                                cc2 = c2 * 4 + c
                                _mm = TT.matmul(tr_ps[:, c * P:(c + 1) * P],
                                          h_bf[:, cc2 * P:(cc2 + 1) * P],
                                          identb[:],
                                          start=(c == 0), stop=(c == 3))
                            sm.inc(_mm, "T")
                            tcnt = C["T"]
                            SS.wait_ge(H["T"], tcnt)
                            dst = hf_s[:].rearrange(
                                "p (ec g) -> p ec g", g=512)[
                                :, c2 * 4:(c2 + 1) * 4, ii * P:(ii + 1) * P]
                            sm.inc(SS.copy(dst, tr_ps[:].rearrange(
                                "p (c q) -> p c q", c=4)), "S")
                            tr_evac_last = C["S"]

                        TT.wait_ge(H["S"], tr_evac_last)
                        TT.wait_ge(H["V"], v_evac_last)
                        for ec in range(DC):
                            lhs = hf_s[:, ec * 512 + ii * P:
                                       ec * 512 + (ii + 1) * P]
                            for hh in range(2):
                                _mm = TT.matmul(v_ps[:, hh * 512:(hh + 1) * 512],
                                          lhs,
                                          wv_s[:, ec * 8 * P + hh * 512:
                                               ec * 8 * P + (hh + 1) * 512],
                                          start=(ec == 0),
                                          stop=(ec == DC - 1))
                        sm.inc(_mm, "T")
                        vmm = C["T"]
                        VV.wait_ge(H["T"], vmm)
                        if i > 0:
                            VV.wait_ge(H["dST"], vst_store[i - 1])
                        sm.inc(VV.tensor_tensor(vst[:], v_ps[:], BV[:],
                                                ALU.add), "V")
                        v_evac_last = C["V"]
                        SY.wait_ge(H["V"], v_evac_last)
                        sm.inc(SY.dma_start(
                            out=kin_vg[i // 4][(i % 4) * P:(i % 4 + 1) * P, :],
                            in_=vst[:]), "dST", 16)
                        vst_store[i] = C["dST"]

                    q_ev = [0, 0]
                    for dc_i in range(DC):
                        TT.wait_ge(H["S"], tr_evac_last)
                        if dc_i >= 2:
                            TT.wait_ge(H["S"], q_ev[dc_i % 2])
                        for ec in range(DC):
                            _mm = TT.matmul(pj_ps[:, (dc_i % 2) * 512:
                                            (dc_i % 2 + 1) * 512],
                                      wq_s[:, (ec * 8 + dc_i) * P:
                                           (ec * 8 + dc_i + 1) * P],
                                      hf_s[:, ec * 512:(ec + 1) * 512],
                                      start=(ec == 0), stop=(ec == DC - 1))
                        sm.inc(_mm, "T")
                        qmm = C["T"]
                        SS.wait_ge(H["T"], qmm)
                        SS.wait_ge(H["V"], bq32_rdy)
                        sm.inc(SS.activation(
                            q_s[:, dc_i * TC + g_i * 512:
                                dc_i * TC + (g_i + 1) * 512],
                            pj_ps[:, (dc_i % 2) * 512:(dc_i % 2 + 1) * 512],
                            AF.Identity, bias=bq32[:, dc_i:dc_i + 1],
                            scale=f32(1.0 / 32.0)), "S")
                        q_ev[dc_i % 2] = C["S"]
                    q_done = C["S"]

                    nrm_mm = 0
                    for dc_i in range(DC):
                        SS.wait_ge(H["S"], q_done)
                        if dc_i > 0:
                            SS.wait_ge(H["T"], nrm_mm)
                        sm.inc(SS.activation(
                            sq_s[:],
                            q_s[:, dc_i * TC + g_i * 512:
                                dc_i * TC + (g_i + 1) * 512],
                            AF.Square), "S")
                        sqd = C["S"]
                        TT.wait_ge(H["S"], sqd)
                        if dc_i == 0 and g_i > 0:
                            TT.wait_ge(H["S"], nrow_done)
                        _mm = TT.matmul(nrm_ps[:], ones_c[:], sq_s[:],
                                  start=(dc_i == 0), stop=(dc_i == DC - 1))
                        sm.inc(_mm, "T")
                        nrm_mm = C["T"]
                    SS.wait_ge(H["T"], nrm_mm)
                    SS.activation(nrow[:], nrm_ps[:], AF.Ln, bias=f32(1e-30),
                                  scale=f32(1.0))
                    sm.inc(SS.activation(nrow[:], nrow[:], AF.Exp,
                                         scale=f32(0.5),
                                         bias=f32(math.log(40.0))), "S")
                    nrow_done = C["S"]
                    VV.wait_ge(H["S"], nrow_done)
                    sm.inc(VV.tensor_scalar_mul(nrow[:], nrow[:], -1.0), "V")
                    nneg = C["V"]
                    TT.wait_ge(H["V"], nneg)
                    if g_i > 0:
                        TT.wait_ge(H["V"], nm_read[g_i - 1])
                    for c in range(4):
                        _mm = TT.matmul(nm_ps[:, c:c + 1],
                                  nrow[0:1, c * P:(c + 1) * P],
                                  one1[:], start=(c == 0), stop=(c == 3))
                    sm.inc(_mm, "T")
                    nm_mm = C["T"]
                    VV.wait_ge(H["T"], nm_mm)
                    sm.inc(VV.tensor_copy(negm[:, g_i * 4:(g_i + 1) * 4],
                                          nm_ps[:]), "V")
                    nm_read[g_i] = C["V"]

                    k_ev = [0, 0]
                    for dc_i in range(DC):
                        TT.wait_ge(H["S"], q_done)
                        if dc_i >= 2:
                            TT.wait_ge(H["S"], k_ev[dc_i % 2])
                        for ec in range(DC):
                            _mm = TT.matmul(pj_ps[:, (dc_i % 2) * 512:
                                            (dc_i % 2 + 1) * 512],
                                      wk_s[:, (ec * 8 + dc_i) * P:
                                           (ec * 8 + dc_i + 1) * P],
                                      hf_s[:, ec * 512:(ec + 1) * 512],
                                      start=(ec == 0), stop=(ec == DC - 1))
                        sm.inc(_mm, "T")
                        kmm = C["T"]
                        SS.wait_ge(H["T"], kmm)
                        if g_i > 0 and dc_i == 0:
                            SS.wait_ge(H["dST"], kst_store)
                        sm.inc(SS.activation(
                            kst[:, dc_i * 512:(dc_i + 1) * 512],
                            pj_ps[:, (dc_i % 2) * 512:(dc_i % 2 + 1) * 512],
                            AF.Identity, bias=bk_s[:, dc_i:dc_i + 1],
                            scale=f32(1.0)), "S")
                        k_ev[dc_i % 2] = C["S"]
                    qk_evac[g_i] = C["S"]
                    SY.wait_ge(H["S"], qk_evac[g_i])
                    sm.inc(SY.dma_start(
                        out=kin_kg[g_i].rearrange("(dc p) t -> p dc t", p=P),
                        in_=kst[:].rearrange("p (dc t) -> p dc t", t=512)),
                        "dST", 16)
                    kst_store = C["dST"]

                    GG.wait_ge(H["dST"], kst_store)
                    sm.inc(GG.collective_compute(
                        "AllGather", ALU.bypass,
                        replica_groups=[list(range(NCORES))],
                        ins=[kin_kg[g_i]], outs=[kout_kg[g_i]]), "cc")
                    sm.inc(GG.collective_compute(
                        "AllGather", ALU.bypass,
                        replica_groups=[list(range(NCORES))],
                        ins=[kin_vg[g_i]], outs=[kout_vg[g_i]]), "cc")
                    cc_cnt[g_i] = C["cc"]

            p1_T, p1_V, p1_S = C["T"], C["V"], C["S"]

            if trunc <= 2 or trunc in (11, 12):
                GG.wait_ge(H["cc"], C["cc"])
                SY.wait_ge(H["dST"], C["dST"])
                SY.wait_ge(H["dX"], C["dX"])
                SY.wait_ge(H["dW"], C["dW"])
                p1sb_stack.close()
                return nc

            p1sb_stack.close()
            with contextlib.ExitStack() as p2ps:
                def ps2(name, shape, dt=F32):
                    return p2ps.enter_context(nc.psum_tensor(name, shape, dt))

                def sb2(name, shape, dt):
                    return p2ps.enter_context(nc.sbuf_tensor(name, shape, dt))

                kbf = sb2("kbf", [P, 2 * DC * KB], BF16)
                vbf = sb2("vbf", [P, 2 * 4 * D], BF16)
                p_sb = sb2("p_sb", [P, 2 * KB], BF16)
                ptb = sb2("ptb", [P, 2 * KB], BF16)
                rho = sb2("rho", [P, 2], F32)
                x0r = sb2("x0r", [P, 2 * D], F32)
                z_n = sb2("z_n", [P, D], F32)
                x1st = sb2("x1st", [P, 2 * D], F32)

                s_ps = ps2("s_ps", [P, 2 * 512])
                pt_ps = ps2("pt_ps", [P, 2 * 512])
                o_ps = ps2("o_ps", [P, 2 * D])

                GG.wait_ge(H["T"], p1_T)
                GG.wait_ge(H["V"], p1_V)
                GG.wait_ge(H["S"], p1_S)

                VV.memset(o_acc[:], 0.0)
                sm.inc(VV.memset(l_s[:], 0.0), "V")
                o_zeroed = C["V"]

                tS = {}
                tPt = {}
                tPV = {}
                sE = {}
                sPt = {}
                vO = {}
                ld = {}
                TOT = NKB * NT

                def emit_loads(j):
                    w_blk, r_blk = j // 8, j % 8
                    if j >= 2:
                        GG.wait_ge(H["T"], tPV[(j - 2) * NT + NT - 1])
                    GG.wait_ge(H["cc"], cc_cnt[w_blk])
                    sm.inc(GG.dma_start(
                        out=kbf[:, (j % 2) * DC * KB:
                                (j % 2 + 1) * DC * KB]
                        .rearrange("p (dc t) -> p dc t", t=KB),
                        in_=kout_kg[w_blk][r_blk * D:(r_blk + 1) * D, :]
                        .rearrange("(dc p) t -> p dc t", p=P)), "dKV", 16)
                    sm.inc(GG.dma_start(
                        out=vbf[:, (j % 2) * 4 * D:(j % 2 + 1) * 4 * D]
                        .rearrange("p (ks d) -> p ks d", d=D),
                        in_=kout_vg[w_blk][r_blk * KB:(r_blk + 1) * KB, :]
                        .rearrange("(ks p) d -> p ks d", p=P)), "dKV", 16)
                    ld[j] = C["dKV"]

                def emit_S(k):
                    j, q, slot = k // NT, k % NT, k % 2
                    TT.wait_ge(H["dKV"], ld[j])
                    if k == 0:
                        TT.wait_ge(H["S"], p1_S)
                        TT.wait_ge(H["V"], o_zeroed)
                    if k >= 2:
                        TT.wait_ge(H["S"], sE[k - 2])
                    for ec in range(DC):
                        _mm = TT.matmul(
                            s_ps[:, slot * 512:(slot + 1) * 512],
                            q_s[:, ec * TC + q * P:ec * TC + (q + 1) * P],
                            kbf[:, (j % 2) * DC * KB + ec * KB:
                                (j % 2) * DC * KB + (ec + 1) * KB],
                            start=(ec == 0), stop=(ec == DC - 1))
                    sm.inc(_mm, "T")
                    tS[k] = C["T"]

                def emit_exp(k):
                    q, slot = k % NT, k % 2
                    SS.wait_ge(H["T"], tS[k])
                    if k >= 2:
                        SS.wait_ge(H["T"], tPt[k - 2])
                        SS.wait_ge(H["V"], vO[k - 2])
                    sm.inc(SS.activation(
                        p_sb[:, slot * KB:(slot + 1) * KB],
                        s_ps[:, slot * 512:(slot + 1) * 512],
                        AF.Exp, bias=negm[:, q:q + 1], scale=f32(1.0),
                        accum_out=rho[:, slot:slot + 1]), "S")
                    sE[k] = C["S"]

                def emit_Pt(k):
                    slot = k % 2
                    TT.wait_ge(H["S"], sE[k])
                    if k >= 2:
                        TT.wait_ge(H["S"], sPt[k - 2])
                    for c in range(4):
                        _mm = TT.matmul(
                            pt_ps[:, slot * 512 + c * P:
                                  slot * 512 + (c + 1) * P],
                            p_sb[:, slot * KB + c * P:
                                 slot * KB + (c + 1) * P],
                            identb[:], start=(c == 0), stop=(c == 3))
                    sm.inc(_mm, "T")
                    tPt[k] = C["T"]

                def emit_ptcopy(k):
                    slot = k % 2
                    SS.wait_ge(H["T"], tPt[k])
                    if k >= 2:
                        SS.wait_ge(H["T"], tPV[k - 2])
                    sm.inc(SS.copy(
                        ptb[:, slot * KB:(slot + 1) * KB],
                        pt_ps[:, slot * 512:(slot + 1) * 512]), "S")
                    sPt[k] = C["S"]

                def emit_PV(k):
                    j, slot = k // NT, k % 2
                    TT.wait_ge(H["S"], sPt[k])
                    if k >= 2:
                        TT.wait_ge(H["V"], vO[k - 2])
                    for ks in range(4):
                        for hh in range(2):
                            _mm = TT.matmul(
                                o_ps[:, slot * D + hh * 512:
                                     slot * D + (hh + 1) * 512],
                                ptb[:, slot * KB + ks * P:
                                    slot * KB + (ks + 1) * P],
                                vbf[:, (j % 2) * 4 * D + ks * D + hh * 512:
                                    (j % 2) * 4 * D + ks * D +
                                    (hh + 1) * 512],
                                start=(ks == 0), stop=(ks == 3))
                    sm.inc(_mm, "T")
                    tPV[k] = C["T"]

                def emit_oadd(k):
                    q, slot = k % NT, k % 2
                    VV.wait_ge(H["T"], tPV[k])
                    VV.tensor_tensor(
                        o_acc[:, q * D:(q + 1) * D],
                        o_acc[:, q * D:(q + 1) * D],
                        o_ps[:, slot * D:(slot + 1) * D], ALU.add)
                    sm.inc(VV.tensor_tensor(
                        l_s[:, q:q + 1], l_s[:, q:q + 1],
                        rho[:, slot:slot + 1], ALU.add), "V")
                    vO[k] = C["V"]

                x0r_load = {}
                x1_store = {}

                def emit_P3(q):
                    sl = q % 2
                    zs = o_acc[:, q * D:(q + 1) * D]
                    sm.inc(VV.reciprocal(rstd[:], l_s[:, q:q + 1]), "V")
                    VV.wait_ge(H["V"], C["V"])
                    VV.tensor_scalar_mul(zs, zs, rstd[:])
                    VV.bn_stats(stt[:, 0:6], zs[:, 0:512])
                    sm.inc(VV.bn_stats(stt[:, 6:12], zs[:, 512:1024]), "V")
                    VV.wait_ge(H["V"], C["V"])
                    sm.inc(VV.bn_aggr(agg[:], stt[:]), "V")
                    bnz = C["V"]
                    SS.wait_ge(H["V"], bnz)
                    SS.activation(rstd[:], agg[:, 1:2], AF.Ln,
                                  bias=f32(1e-5), scale=f32(1.0))
                    sm.inc(SS.activation(rstd[:], rstd[:], AF.Exp,
                                         scale=f32(-0.5)), "S")
                    rsz = C["S"]
                    VV.wait_ge(H["S"], rsz)
                    VV.tensor_scalar(z_n[:], zs, agg[:, 0:1], rstd[:],
                                     ALU.subtract, ALU.mult)
                    VV.wait_ge(H["dX"], x0r_load[q])
                    if q >= 2:
                        VV.wait_ge(H["dST"], x1_store[q - 2])
                    sm.inc(VV.tensor_tensor(
                        x1st[:, sl * D:(sl + 1) * D],
                        z_n[:], x0r[:, sl * D:(sl + 1) * D],
                        ALU.add), "V")
                    x1_done = C["V"]
                    if q + 2 < NT:
                        SY.wait_ge(H["V"], x1_done)
                        sm.inc(SY.dma_start(
                            out=x0r[:, sl * D:(sl + 1) * D],
                            in_=x0_d[(q + 2) * P:(q + 3) * P, :]),
                            "dX", 16)
                        x0r_load[q + 2] = C["dX"]
                    SY.wait_ge(H["V"], x1_done)
                    sm.inc(SY.dma_start(
                        out=x1_scr[q * P:(q + 1) * P, :],
                        in_=x1st[:, sl * D:(sl + 1) * D]),
                        "dST", 16)
                    if q >= 5:
                        sm.inc(SY.dma_start(
                            out=x1t_in[(q - 5) * P:(q - 4) * P, :],
                            in_=x1st[:, sl * D:(sl + 1) * D]),
                            "dST", 16)
                    x1_store[q] = C["dST"]
                    if STAGE2_STUB:
                        sm.inc(SY.dma_start(
                            out=out_d[q * P:(q + 1) * P, :],
                            in_=x1st[:, sl * D:(sl + 1) * D]),
                            "dOU", 16)

                emit_loads(0)
                emit_loads(1)
                for k in range(TOT):
                    if k >= 1:
                        emit_Pt(k - 1)
                        emit_ptcopy(k - 1)
                    emit_S(k)
                    if k >= 1:
                        emit_PV(k - 1)
                    emit_exp(k)
                    if k >= 1:
                        emit_oadd(k - 1)
                    if k % NT == 0 and k > 0 and k // NT + 1 < NKB:
                        emit_loads(k // NT + 1)
                    if k == (NKB - 2) * NT:
                        for qq in range(2):
                            sm.inc(SY.dma_start(
                                out=x0r[:, qq * D:(qq + 1) * D],
                                in_=x0_d[qq * P:(qq + 1) * P, :]),
                                "dX", 16)
                            x0r_load[qq] = C["dX"]
                    if k >= 1 and k - 1 >= (NKB - 1) * NT:
                        emit_P3(k - 1 - (NKB - 1) * NT)
                emit_Pt(TOT - 1)
                emit_ptcopy(TOT - 1)
                emit_PV(TOT - 1)
                emit_oadd(TOT - 1)
                emit_P3(NT - 1)

                flash_V = C["V"]

                if dbg:
                    SY.wait_ge(H["V"], flash_V)
                    sm.inc(SY.dma_start(out=dbg_o, in_=o_acc[:]), "dST", 16)
                    sm.inc(SY.dma_start(out=dbg_l, in_=l_s[:]), "dST", 16)
                    sm.inc(SY.dma_start(out=dbg_m, in_=negm[:]), "dST", 16)
                    sm.inc(SY.dma_start(out=dbg_q, in_=q_s[:]), "dST", 16)

                if trunc <= 3:
                    VV.wait_ge(H["V"], C["V"])
                    SY.wait_ge(H["dKV"], C["dKV"])
                    return nc

                if trunc > 4:
                    tail_done = C["dST"]
                    GG.wait_ge(H["dST"], tail_done)
                    sm.inc(GG.collective_compute(
                        "AllGather", ALU.bypass,
                        replica_groups=[list(range(NCORES))],
                        ins=[x1t_in], outs=[x1t_out]), "cc")
                    x1_gathered = C["cc"]
                    with GG.register("hr") as hr:
                        GG.wait_ge(H["cc"], x1_gathered)
                        GG.cc_rank_ld(
                            hr, replica_groups=[list(range(NCORES))])
                        GG.reg_add(hr, hr, NCORES - 1)
                        GG.reg_mod(hr, hr, NCORES)
                        GG.reg_mul(hr, hr, HALO_G)
                        offh = GG.snap(hr, min_val=0,
                                       max_val=(NCORES - 1) * HALO_G)
                        sm.inc(GG.dma_start(
                            out=x1h_scr,
                            in_=x1t_out[bass.ds(offh, HALO_G), :]),
                            "dH", 16)
                    halo_rdy = C["dH"]

                if trunc <= 4:
                    SY.wait_ge(H["dST"], C["dST"])
                    SY.wait_ge(H["dOU"], C["dOU"])
                    return nc

            stage2_state.update(dict(halo_rdy=halo_rdy))

        if not STAGE2_STUB:
            emit_stage2 = _make_stage2(
                nc, sm, H, C, TT, VV, SS, GG, SY,
                identb, identf, ones_c, ones_r, one1, onesb,
                x1_scr, x1h_scr, out_d, rot_d, msk1_d, msk2_d,
                wqkv_d, wo_d, w1a_d, w1g_d, w2_d, b1a_d, b1g_d, b2_d,
                lng_d, lnb_d, qs_d, ks_d, b2r_d, hz_d, ctx, s2stop)
            emit_stage2()

        SY.wait_ge(H["dOU"], C["dOU"])
        SY.wait_ge(H["dM"], C["dM"])
        SY.wait_ge(H["dST"], C["dST"])
        GG.wait_ge(H["dH"], C["dH"])

    return nc


_NC_CACHE = {}


def kernel(**inputs):
    in_maps = make_inputs(inputs)
    if "prog" not in _NC_CACHE:
        _NC_CACHE["prog"] = build_program()
    nc = _NC_CACHE["prog"]
    res = run_bass_kernel_spmd(nc, in_maps, core_ids=list(range(NCORES)))
    out = np.concatenate([np.asarray(r["x2"]) for r in res.results], axis=0)
    return out.reshape(1, T, D).astype(np.float32)


def _make_stage2(nc, sm, H, C, TT, VV, SS, GG, SY,
                 identb, identf, ones_c, ones_r, one1, onesb,
                 x1_scr, x1h_scr, out_d, rot_d, msk1_d, msk2_d,
                 wqkv_d, wo_d, w1a_d, w1g_d, w2_d, b1a_d, b1g_d, b2_d,
                 lng_d, lnb_d, qs_d, ks_d, b2r_d, hz_d, ctx, s2stop=99):
    CW1 = 1 + (HT_A1 + NT) * P
    NTL = [HT_G + NT, HT_K2 + NT]
    QKV0 = [1, 0]
    ATT0 = [HT_G - HT_A1, HT_K2 - HT_A2]
    U1OF = [lambda mg: (mg - 5) * P, lambda mg: (mg + 1) * P]
    ROT0 = [lambda mg: (mg - 1) * P,
            lambda mg: (HT_K1 - HT_K2) * P + mg * P]
    GW = [512, 512, 512, 512, 512, 170]
    GOF = [0, 512, 1024, 1536, 2048, 2560]

    def emit():
        with contextlib.ExitStack() as s2:
            def sbo(name, shape, dt):
                return s2.enter_context(nc.sbuf_tensor(name, shape, dt))

            u1 = sbo("u1", [P, DC * CW1], F32)
            x1stg = sbo("x1stg2", [P, 2 * D], F32)
            stt2 = sbo("stt2", [P, 12], F32)
            agg2 = sbo("agg2", [P, 2], F32)
            rstd2 = sbo("rstd2", [P, 1], F32)
            onesb_r = sbo("onesb_r", [1, P], BF16)
            hz_s = sbo("hz_s", [P, DC], F32)

            def join():
                sm.inc(VV.engine_nop(), "V")
                sm.inc(SS.activation(rstd2[:], rstd2[:], AF.Identity), "S")
                sm.inc(GG.engine_nop(), "G")
                cV, cS, cG = C["V"], C["S"], C["G"]
                for eng in (TT, VV, SS, GG, SY):
                    eng.wait_ge(H["V"], cV)
                    eng.wait_ge(H["S"], cS)
                    eng.wait_ge(H["G"], cG)
                    eng.wait_ge(H["dM"], C["dM"])
                    eng.wait_ge(H["dW"], C["dW"])

            def zero_row_m1():
                col = 1 + 767
                u1c_ = u1[:].rearrange("p (ch n) -> p ch n", n=CW1)[
                    :, :, col:col + 1]
                sm.inc(VV.tensor_tensor(
                    u1c_, u1c_, hz_s[:].unsqueeze(2), ALU.mult), "V")
                cnt = C["V"]
                for eng in (TT, SS, GG, SY):
                    eng.wait_ge(H["V"], cnt)

            for eng in (TT, VV, SS, GG, SY):
                eng.wait_ge(H["dH"], C["dH"])
                eng.wait_ge(H["dM"], C["dM"])
                eng.wait_ge(H["dST"], C["dST"])
                eng.wait_ge(H["V"], C["V"])
                eng.wait_ge(H["S"], C["S"])
            GG.memset(onesb_r[:], 1.0)
            sm.inc(SY.dma_start(out=hz_s[:], in_=hz_d), "dM", 16)
            sm.inc(GG.memset(u1[:], 0.0), "G")
            u1_zero = C["G"]
            for eng in (TT, VV, SS, SY):
                eng.wait_ge(H["G"], u1_zero)

            for l in range(2):
                ntl = NTL[l]
                att0 = ATT0[l]
                u1of = U1OF[l]
                with contextlib.ExitStack() as sa:
                    def sba(name, shape, dt):
                        return sa.enter_context(
                            nc.sbuf_tensor(f"{name}_l{l}", shape, dt))

                    def psa(name, shape, dt=F32):
                        return sa.enter_context(
                            nc.psum_tensor(f"{name}_l{l}", shape, dt))

                    wqkvs = sba("wqkvs", [P, DC * 3 * INNER], BF16)
                    wos = sba("wos", [P, 4 * D], BF16)
                    QSb = sba("QSb", [P, INNER], F32)
                    KSb = sba("KSb", [P, INNER], F32)
                    qsr = sba("qsr", [1, INNER], F32)
                    ksr = sba("ksr", [1, INNER], F32)
                    gbf = sba("gbf", [P, D], BF16)
                    gt = sba("gt", [P, D], F32)
                    g1c = sba("g1c", [P, 2 * DC * 129], BF16)
                    qkv_s = sba("qkv_s", [P, 2 * 3 * INNER], F32)
                    scr = sba("scr", [P, 2 * KW], F32)
                    qsh = sba("qsh", [P, INNER], F32)
                    qbf = sba("qbf", [P, INNER], BF16)
                    kbf2 = sba("kbf2", [P, INNER], BF16)
                    q_fm = sba("q_fm", [P, 2 * 4 * P], BF16)
                    rsq = sba("rsq", [P, D], F32)
                    k_ring = sba("k_ring", [P, 4 * 8 * P], BF16)
                    v_ring = sba("v_ring", [P, 8 * INNER], BF16)
                    rtab = sba("rtab", [P, 2 * 4 * DH], F32)
                    mskS = sba("mskS", [P, 2 * KW], F32)
                    p2 = sba("p2", [P, 2 * KW], BF16)
                    ptb2 = sba("ptb2", [P, 2 * KW], BF16)
                    o_fm = sba("o_fm", [P, INNER], BF16)
                    otm_sb = sba("otm_sb", [P, INNER], BF16)
                    rn_s = sba("rn_s", [P, 2 * HEADS], F32)
                    lsum = sba("lsum", [P, HEADS], F32)
                    lrec = sba("lrec", [P, HEADS], F32)
                    nm2 = sba("nm2", [P, 2], F32)

                    psQ = psa("psQ", [P, 2 * 512])
                    psPV = psa("psPV", [P, 512])
                    psB = psa("psB", [P, 4 * 512])
                    psC = psa("psC", [P, 512])

                    g1cv = g1c[:].rearrange("p (s ch n) -> p (s ch) n",
                                            s=2, ch=DC, n=129)
                    u1v_ = u1[:].rearrange("p (ch n) -> p ch n", n=CW1)
                    wqv = wqkvs[:].rearrange("p (ec n) -> p ec n",
                                             n=3 * INNER)
                    wov = wos[:].rearrange("p (kc n) -> p kc n", n=D)
                    krv = k_ring[:].rearrange("p (hc n) -> p hc n",
                                              n=8 * P)

                    sm.inc(GG.dma_start(
                        out=wqkvs[:].rearrange("p (ec n) -> p ec n",
                                               n=3 * INNER),
                        in_=wqkv_d[l].rearrange("(ec p) n -> p ec n", p=P)),
                        "dW", 16)
                    sm.inc(GG.dma_start(
                        out=wos[:].rearrange("p (kc n) -> p kc n", n=D),
                        in_=wo_d[l].rearrange("(kc p) n -> p kc n", p=P)),
                        "dW", 16)
                    w_rdy2 = C["dW"]
                    sm.inc(SY.dma_start(out=qsr[:], in_=qs_d[l]), "dM", 16)
                    sm.inc(SY.dma_start(out=ksr[:], in_=ks_d[l]), "dM", 16)
                    qs_ld = C["dM"]
                    TT.wait_ge(H["dM"], qs_ld)
                    TT.matmul(psC[:], ones_r[:], qsr[:],
                              start=True, stop=True)
                    _mm = TT.matmul(psB[:, 0:512], ones_r[:], ksr[:],
                                    start=True, stop=True)
                    sm.inc(_mm, "T")
                    qsb_mm = C["T"]
                    VV.wait_ge(H["T"], qsb_mm)
                    VV.tensor_copy(QSb[:], psC[:])
                    sm.inc(VV.tensor_copy(KSb[:], psB[:, 0:512]), "V")
                    qsb_rdy = C["V"]
                    GG.memset(k_ring[:], 0.0)
                    sm.inc(GG.memset(v_ring[:], 0.0), "G")
                    ring_z = C["G"]
                    for eng in (TT, VV, SS):
                        eng.wait_ge(H["G"], ring_z)

                    x1_ld = {}
                    g_rdy = {}
                    cast_v = {}
                    qkv_mm_t = {}
                    qkv_ev = {}
                    rot_rdy = {}
                    qkfm = {}
                    att_done = {}
                    pv_done = {}
                    psA_vfree = {}
                    u1_init_sd = {}
                    recip_v = {}
                    msk_ld = {}
                    rot_ld = {}
                    psC_free = 0
                    s2_rd = {}
                    s2mm_t = {}
                    tPt2 = {}
                    tPV2 = {}
                    ptb_ev = {}
                    exp_dn = {}
                    g_tr_t = {}
                    u1tr_t = {}

                    def x1src(mg):
                        if mg < HT_G:
                            return x1h_scr[mg * P:(mg + 1) * P, :]
                        return x1_scr[(mg - HT_G) * P:
                                      (mg - HT_G + 1) * P, :]

                    def load_rot(mg):
                        SY.wait_ge(H["V"], rot_rdy.get(mg - 2, 0))
                        r0 = ROT0[l](mg)
                        sm.inc(SY.dma_start(
                            out=rtab[:, (mg % 2) * 256:(mg % 2 + 1) * 256],
                            in_=rot_d[r0:r0 + P, :]), "dM", 16)
                        rot_ld[mg] = C["dM"]

                    def load_msk(mg):
                        SY.wait_ge(H["V"], att_done.get(mg - 2, 0))
                        sm.inc(SY.dma_start(
                            out=mskS[:, (mg % 2) * KW:
                                     (mg % 2 + 1) * KW],
                            in_=MSKD[l][(mg - att0) * P:
                                        (mg - att0 + 1) * P, :]),
                            "dM", 16)
                        msk_ld[mg] = C["dM"]

                    MSKD = [msk1_d, msk2_d]

                    if l == 0:
                        for i in range(min(2, ntl)):
                            sm.inc(SY.dma_start(
                                out=x1stg[:, (i % 2) * D:(i % 2 + 1) * D],
                                in_=x1src(i)), "dM", 16)
                            x1_ld[i] = C["dM"]
                    for i in range(QKV0[l], min(QKV0[l] + 2, ntl)):
                        load_rot(i)

                    def cast_u1c(mg):
                        lt = u1of(mg)
                        if mg >= 2:
                            VV.wait_ge(H["T"], qkv_mm_t.get(mg - 2, 0))
                        src = u1v_[:, :, lt:lt + 129]
                        _in = VV.tensor_copy(
                            g1cv[:, (mg % 2) * DC:(mg % 2) * DC + DC, :],
                            src)
                        sm.inc(_in, "V")
                        cast_v[mg] = C["V"]

                    if l == 1:
                        cast_u1c(0)

                    def emit_G(m):
                        nonlocal psC_free
                        if m < QKV0[l]:
                            att_done[m] = att_done.get(m - 1, 0)
                            pv_done[m] = pv_done.get(m - 1, 0)
                            psA_vfree[m] = psA_vfree.get(m - 1, 0)
                            return
                        if m < att0:
                            att_done[m] = att_done.get(m - 1, 0)
                            pv_done[m] = qkv_mm_t[m]
                            psA_vfree[m] = psA_vfree.get(m - 1, 0)
                            return
                        lt = u1of(m)
                        qfb = (m % 2) * 4 * P
                        slots = [(m - 4 + i) % 8 for i in range(NW)]
                        runs = []
                        rs = 0
                        for i in range(1, NW + 1):
                            if i == NW or slots[i] != slots[i - 1] + 1:
                                runs.append((rs, i))
                                rs = i
                        runs2 = []
                        for a, b in runs:
                            while a < b:
                                nb = min(b, ((a // 4) + 1) * 4)
                                runs2.append((a, nb))
                                a = nb

                        def e_S2(h):
                            sl2 = h % 2
                            hb = sl2 * 64
                            hc = h // 2
                            if h == 0:
                                TT.wait_ge(H["S"], qkfm[(m, 1)])
                                TT.wait_ge(H["V"],
                                           att_done.get(m - 1, 0))
                            if (m, h - 2) in ptb_ev:
                                TT.wait_ge(H["S"], ptb_ev[(m, h - 2)])
                            elif (m - 1, h + 6) in ptb_ev:
                                TT.wait_ge(H["S"],
                                           ptb_ev[(m - 1, h + 6)])
                            for a, b in runs2:
                                _mm = TT.matmul(
                                    psB[:, sl2 * 1024 + a * P:
                                        sl2 * 1024 + b * P],
                                    q_fm[hb:hb + 64,
                                         qfb + hc * P:qfb + (hc + 1) * P],
                                    krv[hb:hb + 64, hc,
                                        slots[a] * P:
                                        slots[b - 1] * P + P],
                                    start=(a % 4 == 0),
                                    stop=(b % 4 == 0 or b == NW))
                            sm.inc(_mm, "T")
                            s2mm_t[(m, h)] = C["T"]

                        def e_mask(h):
                            sl2 = h % 2
                            VV.wait_ge(H["T"], s2mm_t[(m, h)])
                            if h == 0:
                                VV.wait_ge(H["dM"], msk_ld[m])
                            if (m, h - 2) in exp_dn:
                                VV.wait_ge(H["S"], exp_dn[(m, h - 2)])
                            elif (m - 1, h + 6) in exp_dn:
                                VV.wait_ge(H["S"],
                                           exp_dn[(m - 1, h + 6)])
                            VV.tensor_tensor(
                                scr[:, sl2 * KW:(sl2 + 1) * KW],
                                psB[:, sl2 * 1024:sl2 * 1024 + KW],
                                mskS[:, (m % 2) * KW:
                                     (m % 2 + 1) * KW], ALU.add)
                            _in = VV.reduce_max(
                                nm2[:, sl2:sl2 + 1],
                                scr[:, sl2 * KW:(sl2 + 1) * KW],
                                axis=mybir.AxisListType.X, negate=True)
                            sm.inc(_in, "V")
                            s2_rd[(m, h)] = C["V"]

                        def e_exp(h):
                            sl2 = h % 2
                            SS.wait_ge(H["V"], s2_rd[(m, h)])
                            if h == 0:
                                SS.wait_ge(H["V"],
                                           recip_v.get(m - 1, 0))
                            if (m, h - 2) in tPt2:
                                SS.wait_ge(H["T"], tPt2[(m, h - 2)])
                            elif (m - 1, h + 6) in tPt2:
                                SS.wait_ge(H["T"],
                                           tPt2[(m - 1, h + 6)])
                            _in = SS.activation(
                                p2[:, sl2 * KW:(sl2 + 1) * KW],
                                scr[:, sl2 * KW:(sl2 + 1) * KW],
                                AF.Exp, bias=nm2[:, sl2:sl2 + 1],
                                scale=f32(1.0),
                                accum_out=lsum[:, h:h + 1])
                            sm.inc(_in, "S")
                            exp_dn[(m, h)] = C["S"]

                        def e_Pt(h):
                            sl2 = h % 2
                            TT.wait_ge(H["S"], exp_dn[(m, h)])
                            for c in range(NW):
                                _mm = TT.matmul(
                                    psB[:, sl2 * 1024 + c * P:
                                        sl2 * 1024 + (c + 1) * P],
                                    p2[:, sl2 * KW + c * P:
                                        sl2 * KW + (c + 1) * P],
                                    identb[:],
                                    start=(c == 0 or c == 4),
                                    stop=(c == 3 or c == 4))
                            sm.inc(_mm, "T")
                            tPt2[(m, h)] = C["T"]

                        def e_ptcopy(h):
                            sl2 = h % 2
                            SS.wait_ge(H["T"], tPt2[(m, h)])
                            if (m, h - 2) in tPV2:
                                SS.wait_ge(H["T"], tPV2[(m, h - 2)])
                            elif (m - 1, h + 6) in tPV2:
                                SS.wait_ge(H["T"],
                                           tPV2[(m - 1, h + 6)])
                            _in = SS.copy(
                                ptb2[:, sl2 * KW:(sl2 + 1) * KW],
                                psB[:, sl2 * 1024:sl2 * 1024 + KW])
                            sm.inc(_in, "S")
                            ptb_ev[(m, h)] = C["S"]

                        def e_PV(h):
                            sl2 = h % 2
                            TT.wait_ge(H["S"], ptb_ev[(m, h)])
                            if h == 0:
                                TT.wait_ge(H["V"],
                                           psA_vfree.get(m - 1, 0))
                            for i in range(NW):
                                stv = slots[i]
                                _mm = TT.matmul(
                                    psPV[:, h * DH:(h + 1) * DH],
                                    ptb2[:, sl2 * KW + i * P:
                                         sl2 * KW + (i + 1) * P],
                                    v_ring[:, stv * INNER + h * DH:
                                           stv * INNER +
                                           (h + 1) * DH],
                                    start=(h == 0 and i == 0),
                                    stop=(h == HEADS - 1 and
                                          i == NW - 1))
                            sm.inc(_mm, "T")
                            tPV2[(m, h)] = C["T"]

                        e_S2(0)
                        e_mask(0)
                        e_exp(0)
                        for h in range(1, HEADS):
                            e_S2(h)
                            e_mask(h)
                            e_Pt(h - 1)
                            e_ptcopy(h - 1)
                            e_PV(h - 1)
                            e_exp(h)
                        e_Pt(HEADS - 1)
                        e_ptcopy(HEADS - 1)
                        e_PV(HEADS - 1)
                        pv_mm = C["T"]

                        VV.wait_ge(H["S"], exp_dn[(m, HEADS - 1)])
                        sm.inc(VV.reciprocal(lrec[:], lsum[:]), "V")
                        recip_v[m] = C["V"]
                        VV.wait_ge(H["V"], recip_v[m])
                        VV.wait_ge(H["T"], pv_mm)
                        _in = VV.tensor_tensor(
                            otm_sb[:].rearrange("p (h c) -> p h c",
                                                c=DH),
                            psPV[:].rearrange("p (h c) -> p h c", c=DH),
                            lrec[:].unsqueeze(2)
                            .broadcast_to([P, HEADS, DH]), ALU.mult)
                        sm.inc(_in, "V")
                        psA_vfree[m] = C["V"]
                        otm_ev = C["V"]
                        TT.wait_ge(H["V"], otm_ev)
                        TT.wait_ge(H["S"], psC_free)
                        for hc in range(4):
                            _mm = TT.matmul(
                                psC[:, hc * P:(hc + 1) * P],
                                otm_sb[:, hc * P:(hc + 1) * P],
                                identb[:], start=(hc == 0),
                                stop=(hc == 3))
                        sm.inc(_mm, "T")
                        otr = C["T"]
                        SS.wait_ge(H["T"], otr)
                        _in = SS.copy(o_fm[:], psC[:])
                        sm.inc(_in, "S")
                        psC_free = C["S"]
                        o_ev = C["S"]
                        TT.wait_ge(H["S"], o_ev)
                        TT.wait_ge(H["S"], ptb_ev[(m, 6)])
                        for kc in range(4):
                            for mc in range(8):
                                _mm = TT.matmul(
                                    psB[:, mc * P:(mc + 1) * P],
                                    wov[:, kc, mc * P:(mc + 1) * P],
                                    o_fm[:, kc * P:(kc + 1) * P],
                                    start=(kc == 0 and mc % 4 == 0),
                                    stop=(kc == 3 and mc % 4 == 3))
                        sm.inc(_mm, "T")
                        wo_mm = C["T"]
                        pv_done[m] = wo_mm
                        VV.wait_ge(H["T"], wo_mm)
                        if l == 0:
                            VV.wait_ge(H["S"], u1_init_sd.get(m, 0))
                        u1t = u1v_[:, :, 1 + lt:1 + lt + P]
                        _in = VV.tensor_tensor(
                            u1t, u1t,
                            psB[:, 0:1024].rearrange(
                                "p (ch n) -> p ch n", n=P), ALU.add)
                        sm.inc(_in, "V")
                        att_done[m] = C["V"]

                    def emit_E(m):
                        if m < QKV0[l]:
                            return
                        qb2 = (m % 2) * 1536
                        SS.wait_ge(H["V"], rot_rdy.get(m - 1, 0))
                        _in = SS.activation(rsq[:],
                                            qkv_s[:, qb2:qb2 + 1024],
                                            AF.Square)
                        sm.inc(_in, "S")
                        sqd = C["S"]
                        VV.wait_ge(H["S"], sqd)
                        sm.inc(VV.reduce_sum(
                            rn_s[:].unsqueeze(2),
                            rsq[:].rearrange("p (h c) -> p h c", c=DH),
                            axis=mybir.AxisListType.X), "V")
                        VV.wait_ge(H["V"], C["V"])
                        ssqd = C["V"]
                        SS.wait_ge(H["V"], ssqd)
                        SS.activation(rn_s[:], rn_s[:], AF.Ln,
                                      bias=f32(1e-38), scale=f32(1.0))
                        sm.inc(SS.activation(rn_s[:], rn_s[:], AF.Exp,
                                             scale=f32(-0.5)), "S")
                        rnd = C["S"]
                        VV.wait_ge(H["S"], rnd)
                        VV.tensor_scalar_min(rn_s[:], rn_s[:], 1.0e12)
                        VV.wait_ge(H["dM"], rot_ld[m])
                        VV.wait_ge(H["V"], qsb_rdy)
                        VV.wait_ge(H["T"], pv_done.get(m - 2, 0))
                        VV.wait_ge(H["S"], qkfm.get((m - 1, 1), 0))
                        rt = rtab[:, (m % 2) * 256:(m % 2 + 1) * 256]
                        for side in range(2):
                            base = qb2 + side * 512
                            part = qkv_s[:, base:base + 512]
                            part3 = part.rearrange("p (h c) -> p h c",
                                                   c=DH)
                            VV.tensor_tensor(
                                part3, part3,
                                rn_s[:, side * 8:side * 8 + 8]
                                .unsqueeze(2).broadcast_to([P, 8, DH]),
                                ALU.mult)
                            VV.tensor_tensor(
                                part, part,
                                (QSb if side == 0 else KSb)[:], ALU.mult)
                            sh4 = qsh[:].rearrange("p (h s c) -> p h s c",
                                                   s=2, c=32)
                            pt4 = part.rearrange("p (h s c) -> p h s c",
                                                 s=2, c=32)
                            VV.tensor_copy(sh4[:, :, 0:1, :],
                                           pt4[:, :, 1:2, :])
                            VV.tensor_copy(sh4[:, :, 1:2, :],
                                           pt4[:, :, 0:1, :])
                            cq = rt[:, side * 128:side * 128 + DH]
                            bq = rt[:, side * 128 + DH:
                                    side * 128 + 2 * DH]
                            VV.tensor_tensor(
                                part3, part3,
                                cq.unsqueeze(1).broadcast_to([P, 8, DH]),
                                ALU.mult)
                            VV.tensor_tensor(
                                qsh[:].rearrange("p (h c) -> p h c",
                                                 c=DH),
                                qsh[:].rearrange("p (h c) -> p h c",
                                                 c=DH),
                                bq.unsqueeze(1).broadcast_to([P, 8, DH]),
                                ALU.mult)
                            _in = VV.tensor_tensor(
                                (qbf if side == 0 else kbf2)[:],
                                part, qsh[:], ALU.add)
                        st = m % 8
                        _in = VV.tensor_copy(
                            v_ring[:, st * INNER:(st + 1) * INNER],
                            qkv_s[:, qb2 + 1024:qb2 + 1536])
                        sm.inc(_in, "V")
                        rot_rdy[m] = C["V"]

                    def emit_F(m):
                        nonlocal psC_free
                        if m < QKV0[l]:
                            return
                        st = m % 8
                        TT.wait_ge(H["V"], rot_rdy[m])
                        for which in range(2):
                            srcb = qbf if which == 0 else kbf2
                            TT.wait_ge(H["S"], psC_free)
                            for hc in range(4):
                                _mm = TT.matmul(
                                    psC[:, hc * P:(hc + 1) * P],
                                    srcb[:, hc * P:(hc + 1) * P],
                                    identb[:], start=(hc == 0),
                                    stop=(hc == 3))
                            sm.inc(_mm, "T")
                            trq = C["T"]
                            SS.wait_ge(H["T"], trq)
                            if which == 0:
                                dst = q_fm[:, (m % 2) * 4 * P:
                                           (m % 2 + 1) * 4 * P].rearrange(
                                    "p (hc n) -> p hc n", n=P)
                            else:
                                dst = krv[:, :, st * P:(st + 1) * P]
                            sm.inc(SS.copy(dst, psC[:].rearrange(
                                "p (hc n) -> p hc n", n=P)), "S")
                            psC_free = C["S"]
                            qkfm[(m, which)] = C["S"]

                    for mg in range(ntl):
                        lt = u1of(mg)
                        if l == 0:
                            VV.wait_ge(H["dM"], x1_ld[mg])
                            xs = x1stg[:, (mg % 2) * D:(mg % 2 + 1) * D]
                            VV.bn_stats(stt2[:, 0:6], xs[:, 0:512])
                            sm.inc(VV.bn_stats(stt2[:, 6:12],
                                               xs[:, 512:1024]), "V")
                            VV.wait_ge(H["V"], C["V"])
                            sm.inc(VV.bn_aggr(agg2[:], stt2[:]), "V")
                            bnd = C["V"]
                            SS.wait_ge(H["V"], bnd)
                            SS.activation(rstd2[:], agg2[:, 1:2], AF.Ln,
                                          bias=f32(1e-5), scale=f32(1.0))
                            sm.inc(SS.activation(rstd2[:], rstd2[:],
                                                 AF.Exp, scale=f32(-0.5)),
                                   "S")
                            rsd = C["S"]
                            VV.wait_ge(H["S"], rsd)
                            VV.wait_ge(H["T"], g_tr_t.get(mg - 1, 0))
                            VV.wait_ge(H["T"], u1tr_t.get(mg - 1, 0))
                            VV.tensor_scalar(
                                gt[:], xs,
                                agg2[:, 0:1], rstd2[:],
                                ALU.subtract, ALU.mult)
                            sm.inc(VV.tensor_copy(
                                gbf[:], gt[:]), "V")
                            g_rdy[mg] = C["V"]
                            TT.wait_ge(H["V"], g_rdy[mg])
                            if mg >= 2:
                                TT.wait_ge(H["T"], qkv_mm_t.get(mg - 2, 0))
                            for c2 in range(2):
                                TT.wait_ge(H["S"], psC_free)
                                for c in range(4):
                                    ch = c2 * 4 + c
                                    _mm = TT.matmul(
                                        psC[:, c * P:(c + 1) * P],
                                        gbf[:, ch * P:(ch + 1) * P],
                                        identb[:], start=(c == 0),
                                        stop=(c == 3))
                                sm.inc(_mm, "T")
                                trc = C["T"]
                                SS.wait_ge(H["T"], trc)
                                dst = g1cv[:, (mg % 2) * DC + c2 * 4:
                                           (mg % 2) * DC + (c2 + 1) * 4,
                                           1:129]
                                sm.inc(SS.copy(dst, psC[:].rearrange(
                                    "p (c q) -> p c q", c=4)), "S")
                                psC_free = C["S"]
                            g_tr_t[mg] = C["T"]
                            g1c_s = psC_free
                            if mg > 0:
                                VV.wait_ge(H["S"], g1c_s)
                                _in = VV.tensor_copy(
                                    g1cv[:, (mg % 2) * DC:
                                         (mg % 2) * DC + DC, 0:1],
                                    g1cv[:, ((mg - 1) % 2) * DC:
                                         ((mg - 1) % 2) * DC + DC,
                                         128:129])
                                if mg == HT_G:
                                    _in = VV.tensor_tensor(
                                        g1cv[:, (mg % 2) * DC:
                                             (mg % 2) * DC + DC, 0:1],
                                        g1cv[:, (mg % 2) * DC:
                                             (mg % 2) * DC + DC, 0:1],
                                        hz_s[:].unsqueeze(2), ALU.mult)
                            else:
                                _in = VV.memset(
                                    g1cv[:, (mg % 2) * DC:
                                         (mg % 2) * DC + DC, 0:1], 0.0)
                            sm.inc(_in, "V")
                            g_full_v = C["V"]
                            g_full_s = g1c_s
                        else:
                            if mg + 1 < ntl:
                                cast_u1c(mg + 1)
                            g_full_v = cast_v[mg]
                            g_full_s = 0

                        if l == 0 and mg >= att0:
                            TT.wait_ge(H["V"], g_rdy[mg])
                            for c2 in range(2):
                                TT.wait_ge(H["S"], psC_free)
                                for c in range(4):
                                    ch = c2 * 4 + c
                                    _mm = TT.matmul(
                                        psC[:, c * P:(c + 1) * P],
                                        gt[:, ch * P:(ch + 1) * P],
                                        identf[:], start=(c == 0),
                                        stop=(c == 3))
                                sm.inc(_mm, "T")
                                trc = C["T"]
                                SS.wait_ge(H["T"], trc)
                                dst = u1v_[:, c2 * 4:(c2 + 1) * 4,
                                           1 + lt:1 + lt + P]
                                sm.inc(SS.copy(dst, psC[:].rearrange(
                                    "p (c q) -> p c q", c=4)), "S")
                                psC_free = C["S"]
                            u1tr_t[mg] = C["T"]
                            u1_init_sd[mg] = psC_free
                        if l == 0 and mg + 2 < ntl:
                            SY.wait_ge(H["V"], g_rdy[mg])
                            SY.wait_ge(H["T"], u1tr_t.get(mg, 0))
                            sm.inc(SY.dma_start(
                                out=x1stg[:, (mg % 2) * D:
                                          (mg % 2 + 1) * D],
                                in_=x1src(mg + 2)), "dM", 16)
                            x1_ld[mg + 2] = C["dM"]

                        if mg < QKV0[l]:
                            emit_G(mg - 2)
                            continue

                        emit_E(mg - 1)

                        qb = (mg % 2) * 1536
                        TT.wait_ge(H["V"], g_full_v)
                        if l == 0:
                            TT.wait_ge(H["S"], g_full_s)
                        TT.wait_ge(H["dW"], w_rdy2)
                        TT.wait_ge(H["S"], qkv_ev.get(mg - 1, 0))
                        for ec in range(DC):
                            off = 1 if ec < 4 else 0
                            lhs = g1cv[:, (mg % 2) * DC + ec, off:off + P]
                            for np_ in range(2):
                                _mm = TT.matmul(
                                    psQ[:, np_ * 512:(np_ + 1) * 512],
                                    lhs,
                                    wqv[:, ec, np_ * 512:(np_ + 1) * 512],
                                    start=(ec == 0), stop=(ec == DC - 1))
                        sm.inc(_mm, "T")
                        qkmm = C["T"]
                        SS.wait_ge(H["T"], qkmm)
                        SS.wait_ge(H["V"], rot_rdy.get(mg - 2, 0))
                        sm.inc(SS.copy(qkv_s[:, qb:qb + 1024],
                                       psQ[:, 0:1024]), "S")
                        qk_ev1 = C["S"]
                        TT.wait_ge(H["S"], qk_ev1)
                        for ec in range(DC):
                            off = 1 if ec < 4 else 0
                            lhs = g1cv[:, (mg % 2) * DC + ec, off:off + P]
                            _mm = TT.matmul(
                                psQ[:, 0:512], lhs,
                                wqv[:, ec, 1024:1536],
                                start=(ec == 0), stop=(ec == DC - 1))
                        sm.inc(_mm, "T")
                        qkv_mm_t[mg] = C["T"]
                        SS.wait_ge(H["T"], qkv_mm_t[mg])
                        sm.inc(SS.copy(qkv_s[:, qb + 1024:qb + 1536],
                                       psQ[:, 0:512]), "S")
                        qkv_ev[mg] = C["S"]

                        emit_F(mg - 1)
                        emit_G(mg - 2)
                        if mg + 1 < ntl and mg + 1 >= QKV0[l] + 2:
                            load_rot(mg + 1)
                        if mg >= att0:
                            load_msk(mg)

                    emit_E(ntl - 1)
                    emit_F(ntl - 1)
                    emit_G(ntl - 2)
                    emit_G(ntl - 1)
                    join()
                    zero_row_m1()
                if s2stop <= 1 + 2 * l:
                    return

                with contextlib.ExitStack() as sb_:
                    def sbb(name, shape, dt):
                        return sb_.enter_context(
                            nc.sbuf_tensor(f"{name}b_l{l}", shape, dt))

                    def psb_(name, shape, dt=F32):
                        return sb_.enter_context(
                            nc.psum_tensor(f"{name}b_l{l}", shape, dt))

                    nft = (HT_A1 + NT) if l == 0 else NT
                    ft0 = ATT0[0] if l == 0 else HT_K2
                    h_ln = sbb("h_ln", [P, DC * nft * P], BF16)
                    sqf = sbb("sqf", [P, D], F32)
                    srow = sbb("srow", [1, 2 * P], F32)
                    bcs = sbb("bcs", [P, 2 * P], F32)
                    lngb = sbb("lngb", [P, DC], F32)
                    lnbb = sbb("lnbb", [P, DC], F32)
                    b2row = sbb("b2row", [1, D], F32)
                    w1a_p = sbb("w1a_p", [P, DC * 512], BF16)
                    w1g_p = sbb("w1g_p", [P, DC * 512], BF16)
                    w2_p = sbb("w2_p", [P, 4 * D], BF16)
                    b1a_p = sbb("b1a_p", [1, 512], BF16)
                    b1g_p = sbb("b1g_p", [1, 512], BF16)
                    gel = sbb("gel", [P, 512], BF16)
                    pm_sb = sbb("pm_sb", [P, 512], BF16)
                    pmT4 = sbb("pmT4", [P, 4 * 4 * P], BF16)
                    ones5 = sbb("ones5", [1, 512], F32)

                    prep_ps = contextlib.ExitStack()
                    hs_ps = prep_ps.enter_context(
                        nc.psum_tensor(f"hs_psb_l{l}", [1, D], F32))
                    bc_ps = prep_ps.enter_context(
                        nc.psum_tensor(f"bc_psb_l{l}", [P, 2 * P], F32))

                    hv_ = h_ln[:].rearrange("p (ch n) -> p ch n",
                                            n=nft * P)
                    w1av = w1a_p[:].rearrange("p (ec n) -> p ec n", n=512)
                    w1gv = w1g_p[:].rearrange("p (ec n) -> p ec n", n=512)

                    sm.inc(SY.dma_start(out=lngb[:], in_=lng_d[l]),
                           "dM", 16)
                    sm.inc(SY.dma_start(out=lnbb[:], in_=lnb_d[l]),
                           "dM", 16)
                    sm.inc(SY.dma_start(out=b2row[:], in_=b2r_d[l]),
                           "dM", 16)
                    lnp_ld = C["dM"]

                    h_app = {}
                    bc_ev = {}
                    for ti in range(nft):
                        mgu = ft0 + ti
                        lt = u1of(mgu)
                        uv = u1[:].rearrange("p (ch n) -> p ch n", n=CW1)
                        vA = uv[:, 0:4, 1 + lt:1 + lt + P]
                        vB = uv[:, 4:8, lt:lt + P]
                        TT.wait_ge(H["V"], h_app.get(ti - 1, 0))
                        for gi, vv_ in ((0, vA), (1, vB)):
                            _mm = TT.matmul(
                                hs_ps[:, gi * 512:(gi + 1) * 512]
                                .rearrange("o (c n) -> o c n", n=P),
                                ones_c[:], vv_, start=True, stop=True)
                        sm.inc(_mm, "T")
                        smm = C["T"]
                        VV.wait_ge(H["T"], smm)
                        sm.inc(VV.reduce_sum(
                            srow[0:1, 0:P].unsqueeze(2),
                            hs_ps[0:1, 0:D].rearrange(
                                "o (g c n) -> o n (g c)", g=2, n=P),
                            axis=mybir.AxisListType.X), "V")
                        ssum = C["V"]
                        SS.wait_ge(H["V"], h_app.get(ti - 1, 0))
                        SS.activation(
                            sqf[:, 0:512].rearrange(
                                "p (c n) -> p c n", n=P), vA, AF.Square)
                        _in = SS.activation(
                            sqf[:, 512:1024].rearrange(
                                "p (c n) -> p c n", n=P), vB, AF.Square)
                        sm.inc(_in, "S")
                        sqd2 = C["S"]
                        TT.wait_ge(H["S"], sqd2)
                        TT.wait_ge(H["V"], ssum)
                        for gi in range(2):
                            _mm = TT.matmul(
                                hs_ps[:, gi * 512:(gi + 1) * 512],
                                ones_c[:],
                                sqf[:, gi * 512:(gi + 1) * 512],
                                start=True, stop=True)
                        sm.inc(_mm, "T")
                        qmm = C["T"]
                        VV.wait_ge(H["T"], qmm)
                        sm.inc(VV.reduce_sum(
                            srow[0:1, P:2 * P].unsqueeze(2),
                            hs_ps[0:1, 0:D].rearrange(
                                "o (g c n) -> o n (g c)", g=2, n=P),
                            axis=mybir.AxisListType.X), "V")
                        VV.wait_ge(H["V"], C["V"])
                        VV.tensor_scalar_mul(srow[0:1, 0:P],
                                             srow[0:1, 0:P], 1.0 / D)
                        sm.inc(VV.tensor_scalar_mul(
                            srow[0:1, P:2 * P], srow[0:1, P:2 * P],
                            1.0 / D), "V")
                        mud = C["V"]
                        SS.wait_ge(H["V"], mud)
                        _in = SS.activation(sqf[0:1, 0:P],
                                            srow[0:1, 0:P], AF.Square)
                        sm.inc(_in, "S")
                        mu2 = C["S"]
                        VV.wait_ge(H["S"], mu2)
                        sm.inc(VV.tensor_tensor(
                            srow[0:1, P:2 * P], srow[0:1, P:2 * P],
                            sqf[0:1, 0:P], ALU.subtract), "V")
                        vard = C["V"]
                        SS.wait_ge(H["V"], vard)
                        SS.activation(srow[0:1, P:2 * P],
                                      srow[0:1, P:2 * P], AF.Ln,
                                      bias=f32(1e-5), scale=f32(1.0))
                        _in = SS.activation(srow[0:1, P:2 * P],
                                            srow[0:1, P:2 * P], AF.Exp,
                                            scale=f32(-0.5))
                        sm.inc(_in, "S")
                        rrd = C["S"]
                        TT.wait_ge(H["S"], rrd)
                        TT.wait_ge(H["S"], bc_ev.get(ti - 1, 0))
                        for gi in range(2):
                            _mm = TT.matmul(
                                bc_ps[:, gi * P:(gi + 1) * P], ones_r[:],
                                srow[0:1, gi * P:(gi + 1) * P],
                                start=(gi == 0), stop=(gi == 1))
                        sm.inc(_mm, "T")
                        bcm = C["T"]
                        SS.wait_ge(H["T"], bcm)
                        sm.inc(SS.copy(bcs[:], bc_ps[:]), "S")
                        bc_ev[ti] = C["S"]
                        VV.wait_ge(H["S"], bc_ev[ti])
                        VV.wait_ge(H["dM"], lnp_ld)
                        sq3 = sqf[:].rearrange("p (c n) -> p c n", n=P)[
                            :, 0:4, :]
                        for gi, vv_ in ((0, vA), (1, vB)):
                            dst = hv_[:, gi * 4:(gi + 1) * 4,
                                      ti * P:(ti + 1) * P]
                            mub = bcs[:, 0:P].unsqueeze(1) \
                                .broadcast_to([P, 4, P])
                            rb = bcs[:, P:2 * P].unsqueeze(1) \
                                .broadcast_to([P, 4, P])
                            VV.tensor_tensor(sq3, vv_, mub, ALU.subtract)
                            VV.tensor_tensor(sq3, sq3, rb, ALU.mult)
                            VV.tensor_tensor(
                                sq3, sq3,
                                lngb[:, gi * 4:(gi + 1) * 4].unsqueeze(2)
                                .broadcast_to([P, 4, P]), ALU.mult)
                            _in = VV.tensor_tensor(
                                dst, sq3,
                                lnbb[:, gi * 4:(gi + 1) * 4].unsqueeze(2)
                                .broadcast_to([P, 4, P]), ALU.add)
                        sm.inc(_in, "V")
                        h_app[ti] = C["V"]
                    join()

                    prep_ps.close()
                    ab_ps = psb_("ab_ps", [P, 2 * 512])
                    pt_ps2 = psb_("pt_ps2", [P, 512])
                    y_ps4 = psb_("y_ps4", [P, 4 * 512])
                    sm.inc(GG.memset(ones5[:], 1.0), "G")
                    ones5_rdy = C["G"]

                    w_ld = {}
                    gate_dn = {}
                    geld = {}
                    ptv = {}
                    last_w2m = 0
                    prev_gate = 0
                    prev_geld = 0
                    prev_ptev = 0
                    prev_y4 = 0
                    groups = [list(range(g, min(g + 4, nft)))
                              for g in range(0, nft, 4)]
                    pm4 = pmT4[:].rearrange("p (kc n) -> p kc n", n=4 * P)
                    y4v = y_ps4[:].rearrange("p (c n) -> p c n", n=512)
                    for pc in range(6):
                        w = GW[pc]
                        g0 = GOF[pc]
                        nkc = (w + P - 1) // P
                        GG.wait_ge(H["T"], last_w2m)
                        sm.inc(GG.dma_start(
                            out=w1av[:, :, 0:w],
                            in_=w1a_d[l].rearrange(
                                "(ec p) n -> p ec n", p=P)[
                                :, :, g0:g0 + w]), "dW", 16)
                        sm.inc(GG.dma_start(
                            out=w1gv[:, :, 0:w],
                            in_=w1g_d[l].rearrange(
                                "(ec p) n -> p ec n", p=P)[
                                :, :, g0:g0 + w]), "dW", 16)
                        sm.inc(GG.dma_start(out=b1a_p[0:1, 0:w],
                                            in_=b1a_d[l][0:1, g0:g0 + w]),
                               "dW", 16)
                        sm.inc(GG.dma_start(out=b1g_p[0:1, 0:w],
                                            in_=b1g_d[l][0:1, g0:g0 + w]),
                               "dW", 16)
                        w1_ld = C["dW"]
                        for kc in range(nkc):
                            r0 = g0 + kc * P
                            r1 = min(g0 + w, r0 + P)
                            sm.inc(GG.dma_start(
                                out=w2_p[0:r1 - r0, kc * D:(kc + 1) * D],
                                in_=w2_d[l][r0:r1, :]), "dW", 16)
                        w_ld[pc] = C["dW"]
                        last_is_b2 = (pc == 5)

                        for grp in groups:
                            ntok = len(grp) * P
                            lt0 = u1of(ft0 + grp[0])
                            for si, ti in enumerate(grp):
                                TT.wait_ge(H["dW"], w1_ld)
                                TT.wait_ge(H["V"], prev_gate)
                                TT.wait_ge(H["S"], prev_geld)
                                for ec in range(DC):
                                    lhs = hv_[:, ec, ti * P:(ti + 1) * P]
                                    TT.matmul(ab_ps[:, 0:w], lhs,
                                              w1av[:, ec, 0:w],
                                              start=(ec == 0), stop=False)
                                    TT.matmul(ab_ps[:, 512:512 + w], lhs,
                                              w1gv[:, ec, 0:w],
                                              start=(ec == 0), stop=False)
                                TT.matmul(ab_ps[:, 0:w], onesb_r[:],
                                          b1a_p[0:1, 0:w], start=False,
                                          stop=True)
                                _mm = TT.matmul(ab_ps[:, 512:512 + w],
                                                onesb_r[:], b1g_p[0:1, 0:w],
                                                start=False, stop=True)
                                sm.inc(_mm, "T")
                                agm = C["T"]
                                SS.wait_ge(H["T"], agm)
                                _in = SS.activation(gel[:, 0:w],
                                                    ab_ps[:, 512:512 + w],
                                                    AF.Gelu)
                                sm.inc(_in, "S")
                                geld[(pc, ti)] = C["S"]
                                prev_geld = C["S"]
                                VV.wait_ge(H["S"], geld[(pc, ti)])
                                _in = VV.tensor_tensor(pm_sb[:, 0:w],
                                                       ab_ps[:, 0:w],
                                                       gel[:, 0:w],
                                                       ALU.mult)
                                sm.inc(_in, "V")
                                gate_dn[(pc, ti)] = C["V"]
                                prev_gate = C["V"]
                                TT.wait_ge(H["V"], gate_dn[(pc, ti)])
                                TT.wait_ge(H["S"], prev_ptev)
                                for kc in range(nkc):
                                    c0 = kc * P
                                    c1 = min(w, c0 + P)
                                    _mm = TT.matmul(
                                        pt_ps2[0:c1 - c0,
                                               kc * P:kc * P + P],
                                        pm_sb[:, c0:c1], identb[:],
                                        start=True, stop=True)
                                sm.inc(_mm, "T")
                                ptv[(pc, ti)] = C["T"]
                                SS.wait_ge(H["T"], ptv[(pc, ti)])
                                SS.wait_ge(H["T"], last_w2m)
                                _in = SS.copy(
                                    pm4[:, :, si * P:(si + 1) * P],
                                    pt_ps2[:].rearrange(
                                        "p (kc n) -> p kc n", n=P))
                                sm.inc(_in, "S")
                                ptv[('ev', pc, ti)] = C["S"]
                                prev_ptev = C["S"]
                            grp_ptev = prev_ptev
                            for mh in range(2):
                                TT.wait_ge(H["dW"], w_ld[pc])
                                TT.wait_ge(H["S"], grp_ptev)
                                TT.wait_ge(H["V"], prev_y4)
                                if last_is_b2 and mh == 0:
                                    TT.wait_ge(H["G"], ones5_rdy)
                                for kc in range(nkc):
                                    r0 = g0 + kc * P
                                    rows = min(g0 + w, r0 + P) - r0
                                    for mc4 in range(4):
                                        mc = mh * 4 + mc4
                                        _mm = TT.matmul(
                                            y_ps4[:, mc4 * 512:
                                                  mc4 * 512 + ntok],
                                            w2_p[0:rows,
                                                 kc * D + mc * P:
                                                 kc * D + (mc + 1) * P],
                                            pm4[0:rows, kc, 0:ntok],
                                            start=(kc == 0),
                                            stop=(kc == nkc - 1 and
                                                  not last_is_b2))
                                if last_is_b2:
                                    for mc4 in range(4):
                                        mc = mh * 4 + mc4
                                        _mm = TT.matmul(
                                            y_ps4[:, mc4 * 512:
                                                  mc4 * 512 + ntok],
                                            b2row[0:1, mc * P:(mc + 1) * P],
                                            ones5[0:1, 0:ntok],
                                            start=False, stop=True)
                                sm.inc(_mm, "T")
                                w2m = C["T"]
                                last_w2m = w2m
                                VV.wait_ge(H["T"], w2m)
                                u1t = u1[:].rearrange(
                                    "p (ch n) -> p ch n", n=CW1)[
                                    :, mh * 4:(mh + 1) * 4,
                                    1 + lt0:1 + lt0 + ntok]
                                _in = VV.tensor_tensor(
                                    u1t, u1t, y4v[:, :, 0:ntok], ALU.add)
                                sm.inc(_in, "V")
                                prev_y4 = C["V"]
                    join()
                    if l == 0:
                        zero_row_m1()
                if s2stop <= 2 + 2 * l:
                    return

            with contextlib.ExitStack() as sf:
                def sbf(name, shape, dt):
                    return sf.enter_context(nc.sbuf_tensor(name, shape, dt))

                x1rt = sbf("x1rt", [P, 2 * D], F32)
                x2st = sbf("x2st", [P, 2 * D], F32)
                zt = sbf("zt", [P, D], F32)
                xt_ps = sf.enter_context(nc.psum_tensor("xt_ps",
                                                        [P, 2 * D], F32))

                x1r_ld = {}
                for q in range(2):
                    sm.inc(SY.dma_start(out=x1rt[:, q * D:(q + 1) * D],
                                        in_=x1_scr[q * P:(q + 1) * P, :]),
                           "dM", 16)
                    x1r_ld[q] = C["dM"]
                tr_cons = {}
                x2_done = {}
                u1vf = u1[:].rearrange("p (ch n) -> p ch n", n=CW1)
                for q in range(NT):
                    lt = U1OF[1](HT_K2 + q)
                    slot = q % 2
                    TT.wait_ge(H["V"], tr_cons.get(q - 2, 0))
                    for c2 in range(2):
                        for c in range(4):
                            ch = c2 * 4 + c
                            _mm = TT.matmul(
                                xt_ps[:, slot * D + c2 * 512 + c * P:
                                      slot * D + c2 * 512 + (c + 1) * P],
                                u1vf[:, ch, 1 + lt:1 + lt + P],
                                identf[:], start=(c == 0),
                                stop=(c == 3))
                    sm.inc(_mm, "T")
                    trm = C["T"]
                    VV.wait_ge(H["T"], trm)
                    VV.bn_stats(stt2[:, 0:6],
                                xt_ps[:, slot * D:slot * D + 512])
                    sm.inc(VV.bn_stats(stt2[:, 6:12],
                                       xt_ps[:, slot * D + 512:
                                             slot * D + D]), "V")
                    VV.wait_ge(H["V"], C["V"])
                    sm.inc(VV.bn_aggr(agg2[:], stt2[:]), "V")
                    bnf = C["V"]
                    SS.wait_ge(H["V"], bnf)
                    SS.activation(rstd2[:], agg2[:, 1:2], AF.Ln,
                                  bias=f32(1e-5), scale=f32(1.0))
                    sm.inc(SS.activation(rstd2[:], rstd2[:], AF.Exp,
                                         scale=f32(-0.5)), "S")
                    rsf = C["S"]
                    VV.wait_ge(H["S"], rsf)
                    _in = VV.tensor_scalar(
                        zt[:], xt_ps[:, slot * D:slot * D + D],
                        agg2[:, 0:1], rstd2[:], ALU.subtract, ALU.mult)
                    sm.inc(_in, "V")
                    tr_cons[q] = C["V"]
                    VV.wait_ge(H["dM"], x1r_ld[q])
                    if q >= 2:
                        VV.wait_ge(H["dOU"], x2_done[q - 2])
                    _in = VV.tensor_tensor(
                        x2st[:, slot * D:(slot + 1) * D], zt[:],
                        x1rt[:, slot * D:(slot + 1) * D], ALU.add)
                    sm.inc(_in, "V")
                    x2v = C["V"]
                    if q + 2 < NT:
                        SY.wait_ge(H["V"], x2v)
                        sm.inc(SY.dma_start(
                            out=x1rt[:, slot * D:(slot + 1) * D],
                            in_=x1_scr[(q + 2) * P:(q + 3) * P, :]),
                            "dM", 16)
                        x1r_ld[q + 2] = C["dM"]
                    SY.wait_ge(H["V"], x2v)
                    sm.inc(SY.dma_start(
                        out=out_d[q * P:(q + 1) * P, :],
                        in_=x2st[:, slot * D:(slot + 1) * D]), "dOU", 16)
                    x2_done[q] = C["dOU"]

    return emit


# version 14 (final)
# speedup vs baseline: 1.1591x; 1.0489x over previous
import contextlib
import math
import numpy as np
import ml_dtypes
import concourse.bass as bass
import concourse.mybir as mybir
from concourse.bass_utils import run_bass_kernel_spmd

F32 = mybir.dt.float32
BF16 = mybir.dt.bfloat16
ALU = mybir.AluOpType
AF = mybir.ActivationFunctionType
f32 = np.float32

P = 128
D = 1024
DC = 8
T = 16384
NCORES = 8
TC = 2048
NT = 16
KB = 512
NKB = T // KB
HALO_G = 1408
NEG = -1.0e30

HEADS = 8
DH = 64
INNER = 512
HID = 2730
WIN = 512
KW = 640
NW = 5
QK_SCALE = 8.0

HT_G = 11
HT_K1 = 10
HT_A1 = 6
HT_K2 = 5
HT_A2 = 1
NFT = HT_K1 + NT

STAGE2_STUB = False



def _rotary_tables(pos):
    pos = np.asarray(pos, np.float64)
    half = np.arange(0, DH, 2, dtype=np.float64)
    inv_freq = 1.0 / (10000.0 ** (half / DH))
    freqs = pos[:, None] * inv_freq[None, :]
    freqs = np.concatenate([freqs, freqs], axis=-1)
    base = (half + 0.4 * DH) / (1.4 * DH)
    power = (pos - T // 2) / float(WIN // 2)
    scale = base[None, :] ** power[:, None]
    scale = np.concatenate([scale, scale], axis=-1)
    cos, sin = np.cos(freqs), np.sin(freqs)
    sgn = np.ones((1, DH)); sgn[0, :32] = -1.0
    cq = cos * scale * QK_SCALE
    bq = sin * scale * sgn * QK_SCALE
    ck = cos / scale
    bk = sin / scale * sgn
    return np.concatenate([cq, bq, ck, bk], axis=1).astype(np.float32)


def _local_masks(tile_starts):
    out = np.zeros((len(tile_starts), P, KW), np.float32)
    pp = np.arange(P)[:, None]
    cj = np.arange(KW)[None, :]
    for n, t0 in enumerate(tile_starts):
        i = t0 + pp
        j = t0 - WIN + cj
        allowed = (j <= i) & (j >= i - WIN) & (j >= 0)
        out[n][~allowed] = NEG
    return out


def _chunk_cols(vec):
    return np.ascontiguousarray(np.asarray(vec, np.float32).reshape(DC, P).T)


def make_inputs(full):
    x = np.asarray(full["L0_attn_x"], np.float32).reshape(T, D)
    wq = np.asarray(full["L0_attn_Wq"], np.float32)
    wk = np.asarray(full["L0_attn_Wk"], np.float32)
    wv = np.asarray(full["L0_attn_Wv"], np.float32)
    com = dict(
        wqT=np.ascontiguousarray(wq.T), wkT=np.ascontiguousarray(wk.T),
        wvT=np.ascontiguousarray(wv.T),
        bq_c=_chunk_cols(full["L0_attn_bq"]),
        bk_c=_chunk_cols(full["L0_attn_bk"]),
        bv_r=np.ascontiguousarray(np.asarray(full["L0_attn_bv"],
                                             np.float32)[None, :]),
        identb_in=np.eye(P, dtype=ml_dtypes.bfloat16),
        identf_in=np.eye(P, dtype=np.float32),
    )
    wqkv = np.asarray(full["L0_ffn_wqkv"], np.float32)
    wo = np.asarray(full["L0_ffn_wo"], np.float32)
    w1 = np.asarray(full["L0_ffn_w1"], np.float32)
    w2 = np.asarray(full["L0_ffn_w2"], np.float32)
    b1 = np.asarray(full["L0_ffn_b1"], np.float32)
    b2 = np.asarray(full["L0_ffn_b2"], np.float32)
    lng = np.asarray(full["L0_ffn_ln_g"], np.float32)
    lnb = np.asarray(full["L0_ffn_ln_b"], np.float32)
    qsc = np.asarray(full["L0_ffn_q_scale"], np.float32)
    ksc = np.asarray(full["L0_ffn_k_scale"], np.float32)
    for l in range(2):
        com[f"wqkv{l}"] = np.ascontiguousarray(wqkv[l])
        com[f"wo{l}"] = np.ascontiguousarray(wo[l])
        com[f"w1a{l}"] = np.ascontiguousarray(w1[l][:, :HID])
        com[f"w1g{l}"] = np.ascontiguousarray(w1[l][:, HID:])
        com[f"w2{l}"] = np.ascontiguousarray(w2[l])
        com[f"b1a{l}"] = np.ascontiguousarray(b1[l][None, :HID])
        com[f"b1g{l}"] = np.ascontiguousarray(b1[l][None, HID:])
        com[f"b2c{l}"] = _chunk_cols(b2[l])
        com[f"b2r{l}"] = np.ascontiguousarray(b2[l][None, :])
        com[f"lngc{l}"] = _chunk_cols(lng[l])
        com[f"lnbc{l}"] = _chunk_cols(lnb[l])
        com[f"qs{l}"] = np.ascontiguousarray(np.tile(qsc[l], HEADS)[None, :])
        com[f"ks{l}"] = np.ascontiguousarray(np.tile(ksc[l], HEADS)[None, :])
    in_maps = []
    for c in range(NCORES):
        s = c * TC
        m = dict(com)
        m["x0"] = np.ascontiguousarray(x[s:s + TC])
        m["hz"] = np.full((P, DC), 0.0 if c == 0 else 1.0, np.float32)
        pos = np.arange(s - HT_K1 * P, s + TC)
        m["rot"] = _rotary_tables(pos)
        m["msk1"] = _local_masks(
            [s - HT_A1 * P + P * n for n in range(HT_A1 + NT)]
        ).reshape((HT_A1 + NT) * P, KW)
        m["msk2"] = _local_masks(
            [s - HT_A2 * P + P * n for n in range(HT_A2 + NT)]
        ).reshape((HT_A2 + NT) * P, KW)
        in_maps.append(m)
    return in_maps



class Sems:
    def __init__(self, nc, ctx, names):
        self.h = {n: ctx.enter_context(nc.semaphore(n)) for n in names}
        self.c = {n: 0 for n in names}

    def inc(self, inst, name, n=1):
        inst.then_inc(self.h[name], n)
        self.c[name] += n
        return self.c[name]


def build_program(trunc=99, dbg=False, s2stop=99):
    nc = bass.Bass("TRN2", target_bir_lowering=False,
                   detect_race_conditions=False)

    def ein(name, shape, dt=F32):
        return nc.dram_tensor(name, shape, dt, kind="ExternalInput").ap()

    x0_d = ein("x0", [TC, D])
    wqT_d = ein("wqT", [D, D])
    wkT_d = ein("wkT", [D, D])
    wvT_d = ein("wvT", [D, D])
    bq_d = ein("bq_c", [P, DC])
    bk_d = ein("bk_c", [P, DC])
    bv_d = ein("bv_r", [1, D])
    identb_d = ein("identb_in", [P, P], BF16)
    identf_d = ein("identf_in", [P, P])
    rot_d = ein("rot", [NFT * P, 4 * DH])
    msk1_d = ein("msk1", [(HT_A1 + NT) * P, KW])
    msk2_d = ein("msk2", [(HT_A2 + NT) * P, KW])
    wqkv_d = [ein(f"wqkv{l}", [D, 3 * INNER]) for l in range(2)]
    wo_d = [ein(f"wo{l}", [INNER, D]) for l in range(2)]
    w1a_d = [ein(f"w1a{l}", [D, HID]) for l in range(2)]
    w1g_d = [ein(f"w1g{l}", [D, HID]) for l in range(2)]
    w2_d = [ein(f"w2{l}", [HID, D]) for l in range(2)]
    b1a_d = [ein(f"b1a{l}", [1, HID]) for l in range(2)]
    b1g_d = [ein(f"b1g{l}", [1, HID]) for l in range(2)]
    b2_d = [ein(f"b2c{l}", [P, DC]) for l in range(2)]
    b2r_d = [ein(f"b2r{l}", [1, D]) for l in range(2)]
    hz_d = ein("hz", [P, DC])
    lng_d = [ein(f"lngc{l}", [P, DC]) for l in range(2)]
    lnb_d = [ein(f"lnbc{l}", [P, DC]) for l in range(2)]
    qs_d = [ein(f"qs{l}", [1, INNER]) for l in range(2)]
    ks_d = [ein(f"ks{l}", [1, INNER]) for l in range(2)]

    out_d = nc.dram_tensor("x2", [TC, D], F32, kind="ExternalOutput").ap()

    kin_kg = [nc.dram_tensor(f"kin_k{g}", [D, KB], BF16).ap()
              for g in range(4)]
    kin_vg = [nc.dram_tensor(f"kin_v{g}", [4 * P, D], BF16).ap()
              for g in range(4)]
    kout_kg = [nc.dram_tensor(f"kout_k{g}", [NCORES * D, KB], BF16).ap()
               for g in range(4)]
    kout_vg = [nc.dram_tensor(f"kout_v{g}", [NCORES * 4 * P, D], BF16).ap()
               for g in range(4)]
    x1_scr = nc.dram_tensor("x1_scr", [TC, D], F32).ap()
    x1t_in = nc.dram_tensor("x1t_in", [HALO_G, D], F32).ap()
    x1t_out = nc.dram_tensor("x1t_out", [NCORES * HALO_G, D], F32).ap()
    x1h_scr = nc.dram_tensor("x1h_scr", [HALO_G, D], F32).ap()
    if dbg:
        dbg_o = nc.dram_tensor("dbg_o", [P, NT * D], F32,
                               kind="ExternalOutput").ap()
        dbg_l = nc.dram_tensor("dbg_l", [P, NT], F32,
                               kind="ExternalOutput").ap()
        dbg_m = nc.dram_tensor("dbg_m", [P, NT], F32,
                               kind="ExternalOutput").ap()
        dbg_q = nc.dram_tensor("dbg_q", [P, DC * TC], BF16,
                               kind="ExternalOutput").ap()

    with contextlib.ExitStack() as ctx:
        sm = Sems(nc, ctx, ["dX", "dW", "dKV", "dST", "dOU", "cc",
                            "T", "V", "S", "G", "dM", "dH"])
        H, C = sm.h, sm.c

        def sb(name, shape, dt):
            return ctx.enter_context(nc.sbuf_tensor(name, shape, dt))

        identb = sb("identb", [P, P], BF16)
        identf = sb("identf", [P, P], F32)
        ones_c = sb("ones_c", [P, 1], F32)
        ones_r = sb("ones_r", [1, P], F32)
        one1 = sb("one1", [1, 1], F32)
        onesb = sb("onesb", [P, 1], BF16)

        block = ctx.enter_context(nc.Block())
        TT, VV, SS, GG, SY = nc.tensor, nc.vector, nc.scalar, nc.gpsimd, nc.sync

        stage2_state = {}

        with contextlib.ExitStack() as s1:
            def sb1(name, shape, dt):
                return s1.enter_context(nc.sbuf_tensor(name, shape, dt))

            stt = sb1("stt", [P, 12], F32)
            agg = sb1("agg", [P, 2], F32)
            rstd = sb1("rstd", [P, 1], F32)
            q_s = sb1("q_s", [P, DC * TC], BF16)
            negm = sb1("negm", [P, NT], F32)
            l_s = sb1("l_s", [P, NT], F32)
            o_acc = sb1("o_acc", [P, NT * D], F32)

            p1sb_stack = contextlib.ExitStack()

            def sbp1(name, shape, dt):
                return p1sb_stack.enter_context(nc.sbuf_tensor(name, shape, dt))

            wq_s = sbp1("wq_s", [P, 64 * P], BF16)
            wk_s = sbp1("wk_s", [P, 64 * P], BF16)
            wv_s = sbp1("wv_s", [P, 64 * P], BF16)
            bq32 = sbp1("bq32", [P, DC], F32)
            bk_s = sbp1("bk_s", [P, DC], F32)
            bv_r = sbp1("bv_rs", [1, D], F32)
            BV = sbp1("BV", [P, D], F32)
            x0st = sbp1("x0st", [P, 2 * D], F32)
            h_tm = sbp1("h_tm", [P, D], F32)
            h_bf = sbp1("h_bf", [P, D], BF16)
            hf_s = sbp1("hf_s", [P, DC * 512], BF16)
            sq_s = sbp1("sq_s", [P, 512], F32)
            nrow = sbp1("nrow", [1, 512], F32)
            kst = sbp1("kst", [P, DC * 512], BF16)
            vst = sbp1("vst", [P, D], BF16)

            GG.memset(ones_c[:], 1.0)
            GG.memset(ones_r[:], 1.0)
            GG.memset(one1[:], 1.0)
            GG.memset(onesb[:], 1.0)
            sm.inc(GG.dma_start(out=identb[:], in_=identb_d), "dW", 16)
            sm.inc(GG.dma_start(out=identf[:], in_=identf_d), "dW", 16)
            wview = lambda w: w.rearrange("(ec p) n -> p ec n", p=P)
            sview = lambda t: t[:].rearrange("p (ec n) -> p ec n", n=DC * P)
            sm.inc(GG.dma_start(out=sview(wq_s), in_=wview(wqT_d)), "dW", 16)
            sm.inc(GG.dma_start(out=sview(wk_s), in_=wview(wkT_d)), "dW", 16)
            sm.inc(GG.dma_start(out=sview(wv_s), in_=wview(wvT_d)), "dW", 16)
            sm.inc(GG.dma_start(out=bq32[:], in_=bq_d), "dW", 16)
            sm.inc(GG.dma_start(out=bk_s[:], in_=bk_d), "dW", 16)
            sm.inc(GG.dma_start(out=bv_r[:], in_=bv_d), "dW", 16)
            w_rdy = C["dW"]

            with contextlib.ExitStack() as p1ps:
                def ps1(name, shape, dt=F32):
                    return p1ps.enter_context(nc.psum_tensor(name, shape, dt))

                pj_ps = ps1("pj_ps", [P, 2 * 512])
                v_ps = ps1("v_ps", [P, D])
                tr_ps = ps1("tr_ps", [P, 512])
                nrm_ps = ps1("nrm_ps", [1, 512])
                nm_ps = ps1("nm_ps", [P, 4])

                VV.wait_ge(H["dW"], w_rdy)
                sm.inc(VV.tensor_scalar_mul(bq32[:], bq32[:], 1.0 / 32.0),
                       "V")
                bq32_rdy = C["V"]

                TT.wait_ge(H["dW"], w_rdy)
                _mm = TT.matmul(v_ps[:, 0:512], ones_r[:], bv_r[:, 0:512],
                          start=True, stop=True)
                _mm = TT.matmul(v_ps[:, 512:1024], ones_r[:], bv_r[:, 512:1024],
                          start=True, stop=True)
                sm.inc(_mm, "T")
                bvp = C["T"]
                VV.wait_ge(H["T"], bvp)
                sm.inc(VV.tensor_copy(BV[:], v_ps[:]), "V")
                bv_rdy = C["V"]

                for i in range(2):
                    sm.inc(SY.dma_start(out=x0st[:, i * D:(i + 1) * D],
                                        in_=x0_d[i * P:(i + 1) * P, :]),
                           "dX", 16)

                qk_evac = [0] * 4
                v_evac_last = bv_rdy
                tr_evac_last = 0
                nm_read = [0] * 4
                nrow_done = 0
                kst_store = 0
                cc_cnt = {}
                x0_cons = [0] * NT
                vst_store = [0] * NT

                ngrp = {0: 0, 11: 1, 12: 2}.get(trunc, 4)
                for g_i in range(ngrp):
                    for ii in range(4):
                        i = g_i * 4 + ii
                        xs = x0st[:, (i % 2) * D:(i % 2 + 1) * D]
                        VV.wait_ge(H["dX"], 16 * (i + 1))
                        VV.bn_stats(stt[:, 0:6], xs[:, 0:512])
                        sm.inc(VV.bn_stats(stt[:, 6:12], xs[:, 512:1024]),
                               "V")
                        VV.wait_ge(H["V"], C["V"])
                        sm.inc(VV.bn_aggr(agg[:], stt[:]), "V")
                        bn_done = C["V"]
                        SS.wait_ge(H["V"], bn_done)
                        SS.activation(rstd[:], agg[:, 1:2], AF.Ln,
                                      bias=f32(1e-5), scale=f32(1.0))
                        sm.inc(SS.activation(rstd[:], rstd[:], AF.Exp,
                                             scale=f32(-0.5)), "S")
                        rstd_done = C["S"]
                        VV.wait_ge(H["S"], rstd_done)
                        VV.tensor_scalar(h_tm[:], xs, agg[:, 0:1], rstd[:],
                                         ALU.subtract, ALU.mult)
                        sm.inc(VV.tensor_copy(h_bf[:], h_tm[:]), "V")
                        x0_cons[i] = C["V"]
                        cast_done = C["V"]
                        if i + 2 < NT:
                            SY.wait_ge(H["V"], x0_cons[i])
                            sm.inc(SY.dma_start(
                                out=x0st[:, (i % 2) * D:(i % 2 + 1) * D],
                                in_=x0_d[(i + 2) * P:(i + 3) * P, :]),
                                "dX", 16)

                        TT.wait_ge(H["V"], cast_done)
                        if g_i > 0 and ii == 0:
                            TT.wait_ge(H["S"], qk_evac[g_i - 1])
                        for c2 in range(2):
                            TT.wait_ge(H["S"], tr_evac_last)
                            for c in range(4):
                                cc2 = c2 * 4 + c
                                _mm = TT.matmul(tr_ps[:, c * P:(c + 1) * P],
                                          h_bf[:, cc2 * P:(cc2 + 1) * P],
                                          identb[:],
                                          start=(c == 0), stop=(c == 3))
                            sm.inc(_mm, "T")
                            tcnt = C["T"]
                            SS.wait_ge(H["T"], tcnt)
                            dst = hf_s[:].rearrange(
                                "p (ec g) -> p ec g", g=512)[
                                :, c2 * 4:(c2 + 1) * 4, ii * P:(ii + 1) * P]
                            sm.inc(SS.copy(dst, tr_ps[:].rearrange(
                                "p (c q) -> p c q", c=4)), "S")
                            tr_evac_last = C["S"]

                        TT.wait_ge(H["S"], tr_evac_last)
                        TT.wait_ge(H["V"], v_evac_last)
                        for ec in range(DC):
                            lhs = hf_s[:, ec * 512 + ii * P:
                                       ec * 512 + (ii + 1) * P]
                            for hh in range(2):
                                _mm = TT.matmul(v_ps[:, hh * 512:(hh + 1) * 512],
                                          lhs,
                                          wv_s[:, ec * 8 * P + hh * 512:
                                               ec * 8 * P + (hh + 1) * 512],
                                          start=(ec == 0),
                                          stop=(ec == DC - 1))
                        sm.inc(_mm, "T")
                        vmm = C["T"]
                        VV.wait_ge(H["T"], vmm)
                        if i > 0:
                            VV.wait_ge(H["dST"], vst_store[i - 1])
                        sm.inc(VV.tensor_tensor(vst[:], v_ps[:], BV[:],
                                                ALU.add), "V")
                        v_evac_last = C["V"]
                        SY.wait_ge(H["V"], v_evac_last)
                        sm.inc(SY.dma_start(
                            out=kin_vg[i // 4][(i % 4) * P:(i % 4 + 1) * P, :],
                            in_=vst[:]), "dST", 16)
                        vst_store[i] = C["dST"]

                    q_ev = [0, 0]
                    for dc_i in range(DC):
                        TT.wait_ge(H["S"], tr_evac_last)
                        if dc_i >= 2:
                            TT.wait_ge(H["S"], q_ev[dc_i % 2])
                        for ec in range(DC):
                            _mm = TT.matmul(pj_ps[:, (dc_i % 2) * 512:
                                            (dc_i % 2 + 1) * 512],
                                      wq_s[:, (ec * 8 + dc_i) * P:
                                           (ec * 8 + dc_i + 1) * P],
                                      hf_s[:, ec * 512:(ec + 1) * 512],
                                      start=(ec == 0), stop=(ec == DC - 1))
                        sm.inc(_mm, "T")
                        qmm = C["T"]
                        SS.wait_ge(H["T"], qmm)
                        SS.wait_ge(H["V"], bq32_rdy)
                        sm.inc(SS.activation(
                            q_s[:, dc_i * TC + g_i * 512:
                                dc_i * TC + (g_i + 1) * 512],
                            pj_ps[:, (dc_i % 2) * 512:(dc_i % 2 + 1) * 512],
                            AF.Identity, bias=bq32[:, dc_i:dc_i + 1],
                            scale=f32(1.0 / 32.0)), "S")
                        q_ev[dc_i % 2] = C["S"]
                    q_done = C["S"]

                    nrm_mm = 0
                    for dc_i in range(DC):
                        SS.wait_ge(H["S"], q_done)
                        if dc_i > 0:
                            SS.wait_ge(H["T"], nrm_mm)
                        sm.inc(SS.activation(
                            sq_s[:],
                            q_s[:, dc_i * TC + g_i * 512:
                                dc_i * TC + (g_i + 1) * 512],
                            AF.Square), "S")
                        sqd = C["S"]
                        TT.wait_ge(H["S"], sqd)
                        if dc_i == 0 and g_i > 0:
                            TT.wait_ge(H["S"], nrow_done)
                        _mm = TT.matmul(nrm_ps[:], ones_c[:], sq_s[:],
                                  start=(dc_i == 0), stop=(dc_i == DC - 1))
                        sm.inc(_mm, "T")
                        nrm_mm = C["T"]
                    SS.wait_ge(H["T"], nrm_mm)
                    SS.activation(nrow[:], nrm_ps[:], AF.Ln, bias=f32(1e-30),
                                  scale=f32(1.0))
                    sm.inc(SS.activation(nrow[:], nrow[:], AF.Exp,
                                         scale=f32(0.5),
                                         bias=f32(math.log(40.0))), "S")
                    nrow_done = C["S"]
                    VV.wait_ge(H["S"], nrow_done)
                    sm.inc(VV.tensor_scalar_mul(nrow[:], nrow[:], -1.0), "V")
                    nneg = C["V"]
                    TT.wait_ge(H["V"], nneg)
                    if g_i > 0:
                        TT.wait_ge(H["V"], nm_read[g_i - 1])
                    for c in range(4):
                        _mm = TT.matmul(nm_ps[:, c:c + 1],
                                  nrow[0:1, c * P:(c + 1) * P],
                                  one1[:], start=(c == 0), stop=(c == 3))
                    sm.inc(_mm, "T")
                    nm_mm = C["T"]
                    VV.wait_ge(H["T"], nm_mm)
                    sm.inc(VV.tensor_copy(negm[:, g_i * 4:(g_i + 1) * 4],
                                          nm_ps[:]), "V")
                    nm_read[g_i] = C["V"]

                    k_ev = [0, 0]
                    for dc_i in range(DC):
                        TT.wait_ge(H["S"], q_done)
                        if dc_i >= 2:
                            TT.wait_ge(H["S"], k_ev[dc_i % 2])
                        for ec in range(DC):
                            _mm = TT.matmul(pj_ps[:, (dc_i % 2) * 512:
                                            (dc_i % 2 + 1) * 512],
                                      wk_s[:, (ec * 8 + dc_i) * P:
                                           (ec * 8 + dc_i + 1) * P],
                                      hf_s[:, ec * 512:(ec + 1) * 512],
                                      start=(ec == 0), stop=(ec == DC - 1))
                        sm.inc(_mm, "T")
                        kmm = C["T"]
                        SS.wait_ge(H["T"], kmm)
                        if g_i > 0 and dc_i == 0:
                            SS.wait_ge(H["dST"], kst_store)
                        sm.inc(SS.activation(
                            kst[:, dc_i * 512:(dc_i + 1) * 512],
                            pj_ps[:, (dc_i % 2) * 512:(dc_i % 2 + 1) * 512],
                            AF.Identity, bias=bk_s[:, dc_i:dc_i + 1],
                            scale=f32(1.0)), "S")
                        k_ev[dc_i % 2] = C["S"]
                    qk_evac[g_i] = C["S"]
                    SY.wait_ge(H["S"], qk_evac[g_i])
                    sm.inc(SY.dma_start(
                        out=kin_kg[g_i].rearrange("(dc p) t -> p dc t", p=P),
                        in_=kst[:].rearrange("p (dc t) -> p dc t", t=512)),
                        "dST", 16)
                    kst_store = C["dST"]

                    GG.wait_ge(H["dST"], kst_store)
                    sm.inc(GG.collective_compute(
                        "AllGather", ALU.bypass,
                        replica_groups=[list(range(NCORES))],
                        ins=[kin_kg[g_i]], outs=[kout_kg[g_i]]), "cc")
                    sm.inc(GG.collective_compute(
                        "AllGather", ALU.bypass,
                        replica_groups=[list(range(NCORES))],
                        ins=[kin_vg[g_i]], outs=[kout_vg[g_i]]), "cc")
                    cc_cnt[g_i] = C["cc"]

            p1_T, p1_V, p1_S = C["T"], C["V"], C["S"]

            if trunc <= 2 or trunc in (11, 12):
                GG.wait_ge(H["cc"], C["cc"])
                SY.wait_ge(H["dST"], C["dST"])
                SY.wait_ge(H["dX"], C["dX"])
                SY.wait_ge(H["dW"], C["dW"])
                p1sb_stack.close()
                return nc

            p1sb_stack.close()
            with contextlib.ExitStack() as p2ps:
                def ps2(name, shape, dt=F32):
                    return p2ps.enter_context(nc.psum_tensor(name, shape, dt))

                def sb2(name, shape, dt):
                    return p2ps.enter_context(nc.sbuf_tensor(name, shape, dt))

                kbf = sb2("kbf", [P, 2 * DC * KB], BF16)
                vbf = sb2("vbf", [P, 2 * 4 * D], BF16)
                p_sb = sb2("p_sb", [P, 2 * KB], BF16)
                ptb = sb2("ptb", [P, 2 * KB], BF16)
                rho = sb2("rho", [P, 2], F32)
                x0r = sb2("x0r", [P, 2 * D], F32)
                z_n = sb2("z_n", [P, D], F32)
                x1st = sb2("x1st", [P, 2 * D], F32)

                s_ps = ps2("s_ps", [P, 2 * 512])
                pt_ps = ps2("pt_ps", [P, 2 * 512])
                o_ps = ps2("o_ps", [P, 2 * D])

                GG.wait_ge(H["T"], p1_T)
                GG.wait_ge(H["V"], p1_V)
                GG.wait_ge(H["S"], p1_S)

                VV.memset(o_acc[:], 0.0)
                sm.inc(VV.memset(l_s[:], 0.0), "V")
                o_zeroed = C["V"]

                tS = {}
                tPt = {}
                tPV = {}
                sE = {}
                sPt = {}
                vO = {}
                ld = {}
                TOT = NKB * NT

                def emit_loads(j):
                    w_blk, r_blk = j // 8, j % 8
                    if j >= 2:
                        GG.wait_ge(H["T"], tPV[(j - 2) * NT + NT - 1])
                    GG.wait_ge(H["cc"], cc_cnt[w_blk])
                    sm.inc(GG.dma_start(
                        out=kbf[:, (j % 2) * DC * KB:
                                (j % 2 + 1) * DC * KB]
                        .rearrange("p (dc t) -> p dc t", t=KB),
                        in_=kout_kg[w_blk][r_blk * D:(r_blk + 1) * D, :]
                        .rearrange("(dc p) t -> p dc t", p=P)), "dKV", 16)
                    sm.inc(GG.dma_start(
                        out=vbf[:, (j % 2) * 4 * D:(j % 2 + 1) * 4 * D]
                        .rearrange("p (ks d) -> p ks d", d=D),
                        in_=kout_vg[w_blk][r_blk * KB:(r_blk + 1) * KB, :]
                        .rearrange("(ks p) d -> p ks d", p=P)), "dKV", 16)
                    ld[j] = C["dKV"]

                def emit_S(k):
                    j, q, slot = k // NT, k % NT, k % 2
                    TT.wait_ge(H["dKV"], ld[j])
                    if k == 0:
                        TT.wait_ge(H["S"], p1_S)
                        TT.wait_ge(H["V"], o_zeroed)
                    if k >= 2:
                        TT.wait_ge(H["S"], sE[k - 2])
                    for ec in range(DC):
                        _mm = TT.matmul(
                            s_ps[:, slot * 512:(slot + 1) * 512],
                            q_s[:, ec * TC + q * P:ec * TC + (q + 1) * P],
                            kbf[:, (j % 2) * DC * KB + ec * KB:
                                (j % 2) * DC * KB + (ec + 1) * KB],
                            start=(ec == 0), stop=(ec == DC - 1))
                    sm.inc(_mm, "T")
                    tS[k] = C["T"]

                def emit_exp(k):
                    q, slot = k % NT, k % 2
                    SS.wait_ge(H["T"], tS[k])
                    if k >= 2:
                        SS.wait_ge(H["T"], tPt[k - 2])
                        SS.wait_ge(H["V"], vO[k - 2])
                    sm.inc(SS.activation(
                        p_sb[:, slot * KB:(slot + 1) * KB],
                        s_ps[:, slot * 512:(slot + 1) * 512],
                        AF.Exp, bias=negm[:, q:q + 1], scale=f32(1.0),
                        accum_out=rho[:, slot:slot + 1]), "S")
                    sE[k] = C["S"]

                def emit_Pt(k):
                    slot = k % 2
                    TT.wait_ge(H["S"], sE[k])
                    if k >= 2:
                        TT.wait_ge(H["S"], sPt[k - 2])
                    for c in range(4):
                        _mm = TT.matmul(
                            pt_ps[:, slot * 512 + c * P:
                                  slot * 512 + (c + 1) * P],
                            p_sb[:, slot * KB + c * P:
                                 slot * KB + (c + 1) * P],
                            identb[:], start=(c == 0), stop=(c == 3))
                    sm.inc(_mm, "T")
                    tPt[k] = C["T"]

                def emit_ptcopy(k):
                    slot = k % 2
                    SS.wait_ge(H["T"], tPt[k])
                    if k >= 2:
                        SS.wait_ge(H["T"], tPV[k - 2])
                    sm.inc(SS.copy(
                        ptb[:, slot * KB:(slot + 1) * KB],
                        pt_ps[:, slot * 512:(slot + 1) * 512]), "S")
                    sPt[k] = C["S"]

                def emit_PV(k):
                    j, slot = k // NT, k % 2
                    TT.wait_ge(H["S"], sPt[k])
                    if k >= 2:
                        TT.wait_ge(H["V"], vO[k - 2])
                    for ks in range(4):
                        for hh in range(2):
                            _mm = TT.matmul(
                                o_ps[:, slot * D + hh * 512:
                                     slot * D + (hh + 1) * 512],
                                ptb[:, slot * KB + ks * P:
                                    slot * KB + (ks + 1) * P],
                                vbf[:, (j % 2) * 4 * D + ks * D + hh * 512:
                                    (j % 2) * 4 * D + ks * D +
                                    (hh + 1) * 512],
                                start=(ks == 0), stop=(ks == 3))
                    sm.inc(_mm, "T")
                    tPV[k] = C["T"]

                def emit_oadd(k):
                    q, slot = k % NT, k % 2
                    VV.wait_ge(H["T"], tPV[k])
                    VV.tensor_tensor(
                        o_acc[:, q * D:(q + 1) * D],
                        o_acc[:, q * D:(q + 1) * D],
                        o_ps[:, slot * D:(slot + 1) * D], ALU.add)
                    sm.inc(VV.tensor_tensor(
                        l_s[:, q:q + 1], l_s[:, q:q + 1],
                        rho[:, slot:slot + 1], ALU.add), "V")
                    vO[k] = C["V"]

                x0r_load = {}
                x1_store = {}

                def emit_P3(q):
                    sl = q % 2
                    zs = o_acc[:, q * D:(q + 1) * D]
                    sm.inc(VV.reciprocal(rstd[:], l_s[:, q:q + 1]), "V")
                    VV.wait_ge(H["V"], C["V"])
                    VV.tensor_scalar_mul(zs, zs, rstd[:])
                    VV.bn_stats(stt[:, 0:6], zs[:, 0:512])
                    sm.inc(VV.bn_stats(stt[:, 6:12], zs[:, 512:1024]), "V")
                    VV.wait_ge(H["V"], C["V"])
                    sm.inc(VV.bn_aggr(agg[:], stt[:]), "V")
                    bnz = C["V"]
                    SS.wait_ge(H["V"], bnz)
                    SS.activation(rstd[:], agg[:, 1:2], AF.Ln,
                                  bias=f32(1e-5), scale=f32(1.0))
                    sm.inc(SS.activation(rstd[:], rstd[:], AF.Exp,
                                         scale=f32(-0.5)), "S")
                    rsz = C["S"]
                    VV.wait_ge(H["S"], rsz)
                    VV.tensor_scalar(z_n[:], zs, agg[:, 0:1], rstd[:],
                                     ALU.subtract, ALU.mult)
                    VV.wait_ge(H["dX"], x0r_load[q])
                    if q >= 2:
                        VV.wait_ge(H["dST"], x1_store[q - 2])
                    sm.inc(VV.tensor_tensor(
                        x1st[:, sl * D:(sl + 1) * D],
                        z_n[:], x0r[:, sl * D:(sl + 1) * D],
                        ALU.add), "V")
                    x1_done = C["V"]
                    if q + 2 < NT:
                        SY.wait_ge(H["V"], x1_done)
                        sm.inc(SY.dma_start(
                            out=x0r[:, sl * D:(sl + 1) * D],
                            in_=x0_d[(q + 2) * P:(q + 3) * P, :]),
                            "dX", 16)
                        x0r_load[q + 2] = C["dX"]
                    SY.wait_ge(H["V"], x1_done)
                    sm.inc(SY.dma_start(
                        out=x1_scr[q * P:(q + 1) * P, :],
                        in_=x1st[:, sl * D:(sl + 1) * D]),
                        "dST", 16)
                    if q >= 5:
                        sm.inc(SY.dma_start(
                            out=x1t_in[(q - 5) * P:(q - 4) * P, :],
                            in_=x1st[:, sl * D:(sl + 1) * D]),
                            "dST", 16)
                    x1_store[q] = C["dST"]
                    if STAGE2_STUB:
                        sm.inc(SY.dma_start(
                            out=out_d[q * P:(q + 1) * P, :],
                            in_=x1st[:, sl * D:(sl + 1) * D]),
                            "dOU", 16)

                emit_loads(0)
                emit_loads(1)
                for k in range(TOT):
                    if k >= 1:
                        emit_Pt(k - 1)
                        emit_ptcopy(k - 1)
                    emit_S(k)
                    if k >= 1:
                        emit_PV(k - 1)
                    emit_exp(k)
                    if k >= 1:
                        emit_oadd(k - 1)
                    if k % NT == 0 and k > 0 and k // NT + 1 < NKB:
                        emit_loads(k // NT + 1)
                    if k == (NKB - 2) * NT:
                        for qq in range(2):
                            sm.inc(SY.dma_start(
                                out=x0r[:, qq * D:(qq + 1) * D],
                                in_=x0_d[qq * P:(qq + 1) * P, :]),
                                "dX", 16)
                            x0r_load[qq] = C["dX"]
                    if k >= 1 and k - 1 >= (NKB - 1) * NT:
                        emit_P3(k - 1 - (NKB - 1) * NT)
                emit_Pt(TOT - 1)
                emit_ptcopy(TOT - 1)
                emit_PV(TOT - 1)
                emit_oadd(TOT - 1)
                emit_P3(NT - 1)

                flash_V = C["V"]

                if dbg:
                    SY.wait_ge(H["V"], flash_V)
                    sm.inc(SY.dma_start(out=dbg_o, in_=o_acc[:]), "dST", 16)
                    sm.inc(SY.dma_start(out=dbg_l, in_=l_s[:]), "dST", 16)
                    sm.inc(SY.dma_start(out=dbg_m, in_=negm[:]), "dST", 16)
                    sm.inc(SY.dma_start(out=dbg_q, in_=q_s[:]), "dST", 16)

                if trunc <= 3:
                    VV.wait_ge(H["V"], C["V"])
                    SY.wait_ge(H["dKV"], C["dKV"])
                    return nc

                if trunc > 4:
                    tail_done = C["dST"]
                    GG.wait_ge(H["dST"], tail_done)
                    sm.inc(GG.collective_compute(
                        "AllGather", ALU.bypass,
                        replica_groups=[list(range(NCORES))],
                        ins=[x1t_in], outs=[x1t_out]), "cc")
                    x1_gathered = C["cc"]
                    with GG.register("hr") as hr:
                        GG.wait_ge(H["cc"], x1_gathered)
                        GG.cc_rank_ld(
                            hr, replica_groups=[list(range(NCORES))])
                        GG.reg_add(hr, hr, NCORES - 1)
                        GG.reg_mod(hr, hr, NCORES)
                        GG.reg_mul(hr, hr, HALO_G)
                        offh = GG.snap(hr, min_val=0,
                                       max_val=(NCORES - 1) * HALO_G)
                        sm.inc(GG.dma_start(
                            out=x1h_scr,
                            in_=x1t_out[bass.ds(offh, HALO_G), :]),
                            "dH", 16)
                    halo_rdy = C["dH"]

                if trunc <= 4:
                    SY.wait_ge(H["dST"], C["dST"])
                    SY.wait_ge(H["dOU"], C["dOU"])
                    return nc

            stage2_state.update(dict(halo_rdy=halo_rdy))

        if not STAGE2_STUB:
            emit_stage2 = _make_stage2(
                nc, sm, H, C, TT, VV, SS, GG, SY,
                identb, identf, ones_c, ones_r, one1, onesb,
                x1_scr, x1h_scr, out_d, rot_d, msk1_d, msk2_d,
                wqkv_d, wo_d, w1a_d, w1g_d, w2_d, b1a_d, b1g_d, b2_d,
                lng_d, lnb_d, qs_d, ks_d, b2r_d, hz_d, ctx, s2stop)
            emit_stage2()

        SY.wait_ge(H["dOU"], C["dOU"])
        SY.wait_ge(H["dM"], C["dM"])
        SY.wait_ge(H["dST"], C["dST"])
        GG.wait_ge(H["dH"], C["dH"])

    return nc


_NC_CACHE = {}


def kernel(**inputs):
    in_maps = make_inputs(inputs)
    if "prog" not in _NC_CACHE:
        _NC_CACHE["prog"] = build_program()
    nc = _NC_CACHE["prog"]
    res = run_bass_kernel_spmd(nc, in_maps, core_ids=list(range(NCORES)))
    out = np.concatenate([np.asarray(r["x2"]) for r in res.results], axis=0)
    return out.reshape(1, T, D).astype(np.float32)


def _make_stage2(nc, sm, H, C, TT, VV, SS, GG, SY,
                 identb, identf, ones_c, ones_r, one1, onesb,
                 x1_scr, x1h_scr, out_d, rot_d, msk1_d, msk2_d,
                 wqkv_d, wo_d, w1a_d, w1g_d, w2_d, b1a_d, b1g_d, b2_d,
                 lng_d, lnb_d, qs_d, ks_d, b2r_d, hz_d, ctx, s2stop=99):
    CW1 = 1 + (HT_A1 + NT) * P
    NTL = [HT_G + NT, HT_K2 + NT]
    QKV0 = [1, 0]
    ATT0 = [HT_G - HT_A1, HT_K2 - HT_A2]
    U1OF = [lambda mg: (mg - 5) * P, lambda mg: (mg + 1) * P]
    ROT0 = [lambda mg: (mg - 1) * P,
            lambda mg: (HT_K1 - HT_K2) * P + mg * P]
    GW = [512, 512, 512, 512, 512, 170]
    GOF = [0, 512, 1024, 1536, 2048, 2560]

    def emit():
        with contextlib.ExitStack() as s2:
            def sbo(name, shape, dt):
                return s2.enter_context(nc.sbuf_tensor(name, shape, dt))

            u1 = sbo("u1", [P, DC * CW1], F32)
            x1stg = sbo("x1stg2", [P, 2 * D], F32)
            stt2 = sbo("stt2", [P, 12], F32)
            agg2 = sbo("agg2", [P, 2], F32)
            rstd2 = sbo("rstd2", [P, 1], F32)
            onesb_r = sbo("onesb_r", [1, P], BF16)
            hz_s = sbo("hz_s", [P, DC], F32)

            def join():
                sm.inc(VV.engine_nop(), "V")
                sm.inc(SS.activation(rstd2[:], rstd2[:], AF.Identity), "S")
                sm.inc(GG.engine_nop(), "G")
                cV, cS, cG = C["V"], C["S"], C["G"]
                for eng in (TT, VV, SS, GG, SY):
                    eng.wait_ge(H["V"], cV)
                    eng.wait_ge(H["S"], cS)
                    eng.wait_ge(H["G"], cG)
                    eng.wait_ge(H["dM"], C["dM"])
                    eng.wait_ge(H["dW"], C["dW"])

            def zero_row_m1():
                col = 1 + 767
                u1c_ = u1[:].rearrange("p (ch n) -> p ch n", n=CW1)[
                    :, :, col:col + 1]
                sm.inc(VV.tensor_tensor(
                    u1c_, u1c_, hz_s[:].unsqueeze(2), ALU.mult), "V")
                cnt = C["V"]
                for eng in (TT, SS, GG, SY):
                    eng.wait_ge(H["V"], cnt)

            for eng in (TT, VV, SS, GG, SY):
                eng.wait_ge(H["dH"], C["dH"])
                eng.wait_ge(H["dM"], C["dM"])
                eng.wait_ge(H["dST"], C["dST"])
                eng.wait_ge(H["V"], C["V"])
                eng.wait_ge(H["S"], C["S"])
            GG.memset(onesb_r[:], 1.0)
            sm.inc(SY.dma_start(out=hz_s[:], in_=hz_d), "dM", 16)
            sm.inc(GG.memset(u1[:], 0.0), "G")
            u1_zero = C["G"]
            for eng in (TT, VV, SS, SY):
                eng.wait_ge(H["G"], u1_zero)

            for l in range(2):
                ntl = NTL[l]
                att0 = ATT0[l]
                u1of = U1OF[l]
                with contextlib.ExitStack() as sa:
                    def sba(name, shape, dt):
                        return sa.enter_context(
                            nc.sbuf_tensor(f"{name}_l{l}", shape, dt))

                    def psa(name, shape, dt=F32):
                        return sa.enter_context(
                            nc.psum_tensor(f"{name}_l{l}", shape, dt))

                    wqkvs = sba("wqkvs", [P, DC * 3 * INNER], BF16)
                    wos = sba("wos", [P, 4 * D], BF16)
                    QSb = sba("QSb", [P, INNER], F32)
                    KSb = sba("KSb", [P, INNER], F32)
                    qsr = sba("qsr", [1, INNER], F32)
                    ksr = sba("ksr", [1, INNER], F32)
                    gbf = sba("gbf", [P, D], BF16)
                    gt = sba("gt", [P, D], F32)
                    g1c = sba("g1c", [P, 2 * DC * 129], BF16)
                    qkv_s = sba("qkv_s", [P, 2 * 3 * INNER], F32)
                    scr = sba("scr", [P, 2 * KW], F32)
                    qsh = sba("qsh", [P, INNER], F32)
                    qbf = sba("qbf", [P, INNER], BF16)
                    kbf2 = sba("kbf2", [P, INNER], BF16)
                    q_fm = sba("q_fm", [P, 2 * 4 * P], BF16)
                    rsq = sba("rsq", [P, D], F32)
                    k_ring = sba("k_ring", [P, 4 * 8 * P], BF16)
                    v_ring = sba("v_ring", [P, 8 * INNER], BF16)
                    rtab = sba("rtab", [P, 2 * 4 * DH], F32)
                    mskS = sba("mskS", [P, 2 * KW], F32)
                    p2 = sba("p2", [P, 2 * KW], BF16)
                    ptb2 = sba("ptb2", [P, 2 * KW], BF16)
                    o_fm = sba("o_fm", [P, INNER], BF16)
                    otm_sb = sba("otm_sb", [P, INNER], BF16)
                    rn_s = sba("rn_s", [P, 2 * HEADS], F32)
                    lsum = sba("lsum", [P, HEADS], F32)
                    lrec = sba("lrec", [P, HEADS], F32)
                    nm2 = sba("nm2", [P, 2], F32)

                    psQ = psa("psQ", [P, 2 * 512])
                    psPV = psa("psPV", [P, 512])
                    psB = psa("psB", [P, 4 * 512])
                    psC = psa("psC", [P, 512])

                    g1cv = g1c[:].rearrange("p (s ch n) -> p (s ch) n",
                                            s=2, ch=DC, n=129)
                    u1v_ = u1[:].rearrange("p (ch n) -> p ch n", n=CW1)
                    wqv = wqkvs[:].rearrange("p (ec n) -> p ec n",
                                             n=3 * INNER)
                    wov = wos[:].rearrange("p (kc n) -> p kc n", n=D)
                    krv = k_ring[:].rearrange("p (hc n) -> p hc n",
                                              n=8 * P)

                    sm.inc(GG.dma_start(
                        out=wqkvs[:].rearrange("p (ec n) -> p ec n",
                                               n=3 * INNER),
                        in_=wqkv_d[l].rearrange("(ec p) n -> p ec n", p=P)),
                        "dW", 16)
                    sm.inc(GG.dma_start(
                        out=wos[:].rearrange("p (kc n) -> p kc n", n=D),
                        in_=wo_d[l].rearrange("(kc p) n -> p kc n", p=P)),
                        "dW", 16)
                    w_rdy2 = C["dW"]
                    sm.inc(SY.dma_start(out=qsr[:], in_=qs_d[l]), "dM", 16)
                    sm.inc(SY.dma_start(out=ksr[:], in_=ks_d[l]), "dM", 16)
                    qs_ld = C["dM"]
                    TT.wait_ge(H["dM"], qs_ld)
                    TT.matmul(psC[:], ones_r[:], qsr[:],
                              start=True, stop=True)
                    _mm = TT.matmul(psB[:, 0:512], ones_r[:], ksr[:],
                                    start=True, stop=True)
                    sm.inc(_mm, "T")
                    qsb_mm = C["T"]
                    VV.wait_ge(H["T"], qsb_mm)
                    VV.tensor_copy(QSb[:], psC[:])
                    sm.inc(VV.tensor_copy(KSb[:], psB[:, 0:512]), "V")
                    qsb_rdy = C["V"]
                    GG.memset(k_ring[:], 0.0)
                    sm.inc(GG.memset(v_ring[:], 0.0), "G")
                    ring_z = C["G"]
                    for eng in (TT, VV, SS):
                        eng.wait_ge(H["G"], ring_z)

                    x1_ld = {}
                    g_rdy = {}
                    cast_v = {}
                    qkv_mm_t = {}
                    qkv_ev = {}
                    rot_rdy = {}
                    qkfm = {}
                    att_done = {}
                    pv_done = {}
                    psA_vfree = {}
                    u1_init_sd = {}
                    recip_v = {}
                    msk_ld = {}
                    rot_ld = {}
                    psC_free = 0
                    s2_rd = {}
                    s2mm_t = {}
                    tPt2 = {}
                    tPV2 = {}
                    ptb_ev = {}
                    exp_dn = {}
                    g_tr_t = {}
                    u1tr_t = {}

                    def x1src(mg):
                        if mg < HT_G:
                            return x1h_scr[mg * P:(mg + 1) * P, :]
                        return x1_scr[(mg - HT_G) * P:
                                      (mg - HT_G + 1) * P, :]

                    def load_rot(mg):
                        SY.wait_ge(H["V"], rot_rdy.get(mg - 2, 0))
                        r0 = ROT0[l](mg)
                        sm.inc(SY.dma_start(
                            out=rtab[:, (mg % 2) * 256:(mg % 2 + 1) * 256],
                            in_=rot_d[r0:r0 + P, :]), "dM", 16)
                        rot_ld[mg] = C["dM"]

                    def load_msk(mg):
                        SY.wait_ge(H["V"], att_done.get(mg - 2, 0))
                        sm.inc(SY.dma_start(
                            out=mskS[:, (mg % 2) * KW:
                                     (mg % 2 + 1) * KW],
                            in_=MSKD[l][(mg - att0) * P:
                                        (mg - att0 + 1) * P, :]),
                            "dM", 16)
                        msk_ld[mg] = C["dM"]

                    MSKD = [msk1_d, msk2_d]

                    if l == 0:
                        for i in range(min(2, ntl)):
                            sm.inc(SY.dma_start(
                                out=x1stg[:, (i % 2) * D:(i % 2 + 1) * D],
                                in_=x1src(i)), "dM", 16)
                            x1_ld[i] = C["dM"]
                    for i in range(QKV0[l], min(QKV0[l] + 2, ntl)):
                        load_rot(i)

                    def cast_u1c(mg):
                        lt = u1of(mg)
                        if mg >= 2:
                            VV.wait_ge(H["T"], qkv_mm_t.get(mg - 2, 0))
                        src = u1v_[:, :, lt:lt + 129]
                        _in = VV.tensor_copy(
                            g1cv[:, (mg % 2) * DC:(mg % 2) * DC + DC, :],
                            src)
                        sm.inc(_in, "V")
                        cast_v[mg] = C["V"]

                    if l == 1:
                        cast_u1c(0)

                    def emit_G(m):
                        nonlocal psC_free
                        if m < QKV0[l]:
                            att_done[m] = att_done.get(m - 1, 0)
                            pv_done[m] = pv_done.get(m - 1, 0)
                            psA_vfree[m] = psA_vfree.get(m - 1, 0)
                            return
                        if m < att0:
                            att_done[m] = att_done.get(m - 1, 0)
                            pv_done[m] = qkv_mm_t[m]
                            psA_vfree[m] = psA_vfree.get(m - 1, 0)
                            return
                        lt = u1of(m)
                        qfb = (m % 2) * 4 * P
                        slots = [(m - 4 + i) % 8 for i in range(NW)]
                        runs = []
                        rs = 0
                        for i in range(1, NW + 1):
                            if i == NW or slots[i] != slots[i - 1] + 1:
                                runs.append((rs, i))
                                rs = i
                        runs2 = []
                        for a, b in runs:
                            while a < b:
                                nb = min(b, ((a // 4) + 1) * 4)
                                runs2.append((a, nb))
                                a = nb

                        def e_S2(h):
                            sl2 = h % 2
                            hb = sl2 * 64
                            hc = h // 2
                            if h == 0:
                                TT.wait_ge(H["S"], qkfm[(m, 1)])
                                TT.wait_ge(H["V"],
                                           att_done.get(m - 1, 0))
                            if (m, h - 2) in ptb_ev:
                                TT.wait_ge(H["S"], ptb_ev[(m, h - 2)])
                            elif (m - 1, h + 6) in ptb_ev:
                                TT.wait_ge(H["S"],
                                           ptb_ev[(m - 1, h + 6)])
                            for a, b in runs2:
                                _mm = TT.matmul(
                                    psB[:, sl2 * 1024 + a * P:
                                        sl2 * 1024 + b * P],
                                    q_fm[hb:hb + 64,
                                         qfb + hc * P:qfb + (hc + 1) * P],
                                    krv[hb:hb + 64, hc,
                                        slots[a] * P:
                                        slots[b - 1] * P + P],
                                    start=(a % 4 == 0),
                                    stop=(b % 4 == 0 or b == NW))
                            sm.inc(_mm, "T")
                            s2mm_t[(m, h)] = C["T"]

                        def e_mask(h):
                            sl2 = h % 2
                            VV.wait_ge(H["T"], s2mm_t[(m, h)])
                            if h == 0:
                                VV.wait_ge(H["dM"], msk_ld[m])
                            if (m, h - 2) in exp_dn:
                                VV.wait_ge(H["S"], exp_dn[(m, h - 2)])
                            elif (m - 1, h + 6) in exp_dn:
                                VV.wait_ge(H["S"],
                                           exp_dn[(m - 1, h + 6)])
                            VV.tensor_tensor(
                                scr[:, sl2 * KW:(sl2 + 1) * KW],
                                psB[:, sl2 * 1024:sl2 * 1024 + KW],
                                mskS[:, (m % 2) * KW:
                                     (m % 2 + 1) * KW], ALU.add)
                            _in = VV.reduce_max(
                                nm2[:, sl2:sl2 + 1],
                                scr[:, sl2 * KW:(sl2 + 1) * KW],
                                axis=mybir.AxisListType.X, negate=True)
                            sm.inc(_in, "V")
                            s2_rd[(m, h)] = C["V"]

                        def e_exp(h):
                            sl2 = h % 2
                            SS.wait_ge(H["V"], s2_rd[(m, h)])
                            if h == 0:
                                SS.wait_ge(H["V"],
                                           recip_v.get(m - 1, 0))
                            if (m, h - 2) in tPt2:
                                SS.wait_ge(H["T"], tPt2[(m, h - 2)])
                            elif (m - 1, h + 6) in tPt2:
                                SS.wait_ge(H["T"],
                                           tPt2[(m - 1, h + 6)])
                            _in = SS.activation(
                                p2[:, sl2 * KW:(sl2 + 1) * KW],
                                scr[:, sl2 * KW:(sl2 + 1) * KW],
                                AF.Exp, bias=nm2[:, sl2:sl2 + 1],
                                scale=f32(1.0),
                                accum_out=lsum[:, h:h + 1])
                            sm.inc(_in, "S")
                            exp_dn[(m, h)] = C["S"]

                        def e_Pt(h):
                            sl2 = h % 2
                            TT.wait_ge(H["S"], exp_dn[(m, h)])
                            for c in range(NW):
                                _mm = TT.matmul(
                                    psB[:, sl2 * 1024 + c * P:
                                        sl2 * 1024 + (c + 1) * P],
                                    p2[:, sl2 * KW + c * P:
                                        sl2 * KW + (c + 1) * P],
                                    identb[:],
                                    start=(c == 0 or c == 4),
                                    stop=(c == 3 or c == 4))
                            sm.inc(_mm, "T")
                            tPt2[(m, h)] = C["T"]

                        def e_ptcopy(h):
                            sl2 = h % 2
                            SS.wait_ge(H["T"], tPt2[(m, h)])
                            if (m, h - 2) in tPV2:
                                SS.wait_ge(H["T"], tPV2[(m, h - 2)])
                            elif (m - 1, h + 6) in tPV2:
                                SS.wait_ge(H["T"],
                                           tPV2[(m - 1, h + 6)])
                            _in = SS.copy(
                                ptb2[:, sl2 * KW:(sl2 + 1) * KW],
                                psB[:, sl2 * 1024:sl2 * 1024 + KW])
                            sm.inc(_in, "S")
                            ptb_ev[(m, h)] = C["S"]

                        def e_PV(h):
                            sl2 = h % 2
                            TT.wait_ge(H["S"], ptb_ev[(m, h)])
                            if h == 0:
                                TT.wait_ge(H["V"],
                                           psA_vfree.get(m - 1, 0))
                            for i in range(NW):
                                stv = slots[i]
                                _mm = TT.matmul(
                                    psPV[:, h * DH:(h + 1) * DH],
                                    ptb2[:, sl2 * KW + i * P:
                                         sl2 * KW + (i + 1) * P],
                                    v_ring[:, stv * INNER + h * DH:
                                           stv * INNER +
                                           (h + 1) * DH],
                                    start=(h == 0 and i == 0),
                                    stop=(h == HEADS - 1 and
                                          i == NW - 1))
                            sm.inc(_mm, "T")
                            tPV2[(m, h)] = C["T"]

                        e_S2(0)
                        e_mask(0)
                        e_exp(0)
                        for h in range(1, HEADS):
                            e_S2(h)
                            e_mask(h)
                            e_Pt(h - 1)
                            e_ptcopy(h - 1)
                            e_PV(h - 1)
                            e_exp(h)
                        e_Pt(HEADS - 1)
                        e_ptcopy(HEADS - 1)
                        e_PV(HEADS - 1)
                        pv_mm = C["T"]

                        VV.wait_ge(H["S"], exp_dn[(m, HEADS - 1)])
                        sm.inc(VV.reciprocal(lrec[:], lsum[:]), "V")
                        recip_v[m] = C["V"]
                        VV.wait_ge(H["V"], recip_v[m])
                        VV.wait_ge(H["T"], pv_mm)
                        _in = VV.tensor_tensor(
                            otm_sb[:].rearrange("p (h c) -> p h c",
                                                c=DH),
                            psPV[:].rearrange("p (h c) -> p h c", c=DH),
                            lrec[:].unsqueeze(2)
                            .broadcast_to([P, HEADS, DH]), ALU.mult)
                        sm.inc(_in, "V")
                        psA_vfree[m] = C["V"]
                        otm_ev = C["V"]
                        TT.wait_ge(H["V"], otm_ev)
                        TT.wait_ge(H["S"], psC_free)
                        for hc in range(4):
                            _mm = TT.matmul(
                                psC[:, hc * P:(hc + 1) * P],
                                otm_sb[:, hc * P:(hc + 1) * P],
                                identb[:], start=(hc == 0),
                                stop=(hc == 3))
                        sm.inc(_mm, "T")
                        otr = C["T"]
                        SS.wait_ge(H["T"], otr)
                        _in = SS.copy(o_fm[:], psC[:])
                        sm.inc(_in, "S")
                        psC_free = C["S"]
                        o_ev = C["S"]
                        TT.wait_ge(H["S"], o_ev)
                        TT.wait_ge(H["S"], ptb_ev[(m, 6)])
                        for kc in range(4):
                            for mc in range(8):
                                _mm = TT.matmul(
                                    psB[:, mc * P:(mc + 1) * P],
                                    wov[:, kc, mc * P:(mc + 1) * P],
                                    o_fm[:, kc * P:(kc + 1) * P],
                                    start=(kc == 0 and mc % 4 == 0),
                                    stop=(kc == 3 and mc % 4 == 3))
                        sm.inc(_mm, "T")
                        wo_mm = C["T"]
                        pv_done[m] = wo_mm
                        VV.wait_ge(H["T"], wo_mm)
                        if l == 0:
                            VV.wait_ge(H["S"], u1_init_sd.get(m, 0))
                        u1t = u1v_[:, :, 1 + lt:1 + lt + P]
                        _in = VV.tensor_tensor(
                            u1t, u1t,
                            psB[:, 0:1024].rearrange(
                                "p (ch n) -> p ch n", n=P), ALU.add)
                        sm.inc(_in, "V")
                        att_done[m] = C["V"]

                    def emit_E(m):
                        if m < QKV0[l]:
                            return
                        qb2 = (m % 2) * 1536
                        SS.wait_ge(H["V"], rot_rdy.get(m - 1, 0))
                        _in = SS.activation(rsq[:],
                                            qkv_s[:, qb2:qb2 + 1024],
                                            AF.Square)
                        sm.inc(_in, "S")
                        sqd = C["S"]
                        VV.wait_ge(H["S"], sqd)
                        sm.inc(VV.reduce_sum(
                            rn_s[:].unsqueeze(2),
                            rsq[:].rearrange("p (h c) -> p h c", c=DH),
                            axis=mybir.AxisListType.X), "V")
                        VV.wait_ge(H["V"], C["V"])
                        ssqd = C["V"]
                        SS.wait_ge(H["V"], ssqd)
                        SS.activation(rn_s[:], rn_s[:], AF.Ln,
                                      bias=f32(1e-38), scale=f32(1.0))
                        sm.inc(SS.activation(rn_s[:], rn_s[:], AF.Exp,
                                             scale=f32(-0.5)), "S")
                        rnd = C["S"]
                        VV.wait_ge(H["S"], rnd)
                        VV.tensor_scalar_min(rn_s[:], rn_s[:], 1.0e12)
                        VV.wait_ge(H["dM"], rot_ld[m])
                        VV.wait_ge(H["V"], qsb_rdy)
                        VV.wait_ge(H["T"], pv_done.get(m - 2, 0))
                        VV.wait_ge(H["S"], qkfm.get((m - 1, 1), 0))
                        rt = rtab[:, (m % 2) * 256:(m % 2 + 1) * 256]
                        for side in range(2):
                            base = qb2 + side * 512
                            part = qkv_s[:, base:base + 512]
                            part3 = part.rearrange("p (h c) -> p h c",
                                                   c=DH)
                            VV.tensor_tensor(
                                part3, part3,
                                rn_s[:, side * 8:side * 8 + 8]
                                .unsqueeze(2).broadcast_to([P, 8, DH]),
                                ALU.mult)
                            VV.tensor_tensor(
                                part, part,
                                (QSb if side == 0 else KSb)[:], ALU.mult)
                            sh4 = qsh[:].rearrange("p (h s c) -> p h s c",
                                                   s=2, c=32)
                            pt4 = part.rearrange("p (h s c) -> p h s c",
                                                 s=2, c=32)
                            VV.tensor_copy(sh4[:, :, 0:1, :],
                                           pt4[:, :, 1:2, :])
                            VV.tensor_copy(sh4[:, :, 1:2, :],
                                           pt4[:, :, 0:1, :])
                            cq = rt[:, side * 128:side * 128 + DH]
                            bq = rt[:, side * 128 + DH:
                                    side * 128 + 2 * DH]
                            VV.tensor_tensor(
                                part3, part3,
                                cq.unsqueeze(1).broadcast_to([P, 8, DH]),
                                ALU.mult)
                            VV.tensor_tensor(
                                qsh[:].rearrange("p (h c) -> p h c",
                                                 c=DH),
                                qsh[:].rearrange("p (h c) -> p h c",
                                                 c=DH),
                                bq.unsqueeze(1).broadcast_to([P, 8, DH]),
                                ALU.mult)
                            _in = VV.tensor_tensor(
                                (qbf if side == 0 else kbf2)[:],
                                part, qsh[:], ALU.add)
                        st = m % 8
                        _in = VV.tensor_copy(
                            v_ring[:, st * INNER:(st + 1) * INNER],
                            qkv_s[:, qb2 + 1024:qb2 + 1536])
                        sm.inc(_in, "V")
                        rot_rdy[m] = C["V"]

                    def emit_F(m):
                        nonlocal psC_free
                        if m < QKV0[l]:
                            return
                        st = m % 8
                        TT.wait_ge(H["V"], rot_rdy[m])
                        for which in range(2):
                            srcb = qbf if which == 0 else kbf2
                            TT.wait_ge(H["S"], psC_free)
                            for hc in range(4):
                                _mm = TT.matmul(
                                    psC[:, hc * P:(hc + 1) * P],
                                    srcb[:, hc * P:(hc + 1) * P],
                                    identb[:], start=(hc == 0),
                                    stop=(hc == 3))
                            sm.inc(_mm, "T")
                            trq = C["T"]
                            SS.wait_ge(H["T"], trq)
                            if which == 0:
                                dst = q_fm[:, (m % 2) * 4 * P:
                                           (m % 2 + 1) * 4 * P].rearrange(
                                    "p (hc n) -> p hc n", n=P)
                            else:
                                dst = krv[:, :, st * P:(st + 1) * P]
                            sm.inc(SS.copy(dst, psC[:].rearrange(
                                "p (hc n) -> p hc n", n=P)), "S")
                            psC_free = C["S"]
                            qkfm[(m, which)] = C["S"]

                    for mg in range(ntl):
                        lt = u1of(mg)
                        if l == 0:
                            VV.wait_ge(H["dM"], x1_ld[mg])
                            xs = x1stg[:, (mg % 2) * D:(mg % 2 + 1) * D]
                            VV.bn_stats(stt2[:, 0:6], xs[:, 0:512])
                            sm.inc(VV.bn_stats(stt2[:, 6:12],
                                               xs[:, 512:1024]), "V")
                            VV.wait_ge(H["V"], C["V"])
                            sm.inc(VV.bn_aggr(agg2[:], stt2[:]), "V")
                            bnd = C["V"]
                            SS.wait_ge(H["V"], bnd)
                            SS.activation(rstd2[:], agg2[:, 1:2], AF.Ln,
                                          bias=f32(1e-5), scale=f32(1.0))
                            sm.inc(SS.activation(rstd2[:], rstd2[:],
                                                 AF.Exp, scale=f32(-0.5)),
                                   "S")
                            rsd = C["S"]
                            VV.wait_ge(H["S"], rsd)
                            VV.wait_ge(H["T"], g_tr_t.get(mg - 1, 0))
                            VV.wait_ge(H["T"], u1tr_t.get(mg - 1, 0))
                            VV.tensor_scalar(
                                gt[:], xs,
                                agg2[:, 0:1], rstd2[:],
                                ALU.subtract, ALU.mult)
                            sm.inc(VV.tensor_copy(
                                gbf[:], gt[:]), "V")
                            g_rdy[mg] = C["V"]
                            TT.wait_ge(H["V"], g_rdy[mg])
                            if mg >= 2:
                                TT.wait_ge(H["T"], qkv_mm_t.get(mg - 2, 0))
                            for c2 in range(2):
                                TT.wait_ge(H["S"], psC_free)
                                for c in range(4):
                                    ch = c2 * 4 + c
                                    _mm = TT.matmul(
                                        psC[:, c * P:(c + 1) * P],
                                        gbf[:, ch * P:(ch + 1) * P],
                                        identb[:], start=(c == 0),
                                        stop=(c == 3))
                                sm.inc(_mm, "T")
                                trc = C["T"]
                                SS.wait_ge(H["T"], trc)
                                dst = g1cv[:, (mg % 2) * DC + c2 * 4:
                                           (mg % 2) * DC + (c2 + 1) * 4,
                                           1:129]
                                sm.inc(SS.copy(dst, psC[:].rearrange(
                                    "p (c q) -> p c q", c=4)), "S")
                                psC_free = C["S"]
                            g_tr_t[mg] = C["T"]
                            g1c_s = psC_free
                            if mg > 0:
                                VV.wait_ge(H["S"], g1c_s)
                                _in = VV.tensor_copy(
                                    g1cv[:, (mg % 2) * DC:
                                         (mg % 2) * DC + DC, 0:1],
                                    g1cv[:, ((mg - 1) % 2) * DC:
                                         ((mg - 1) % 2) * DC + DC,
                                         128:129])
                                if mg == HT_G:
                                    _in = VV.tensor_tensor(
                                        g1cv[:, (mg % 2) * DC:
                                             (mg % 2) * DC + DC, 0:1],
                                        g1cv[:, (mg % 2) * DC:
                                             (mg % 2) * DC + DC, 0:1],
                                        hz_s[:].unsqueeze(2), ALU.mult)
                            else:
                                _in = VV.memset(
                                    g1cv[:, (mg % 2) * DC:
                                         (mg % 2) * DC + DC, 0:1], 0.0)
                            sm.inc(_in, "V")
                            g_full_v = C["V"]
                            g_full_s = g1c_s
                        else:
                            if mg + 1 < ntl:
                                cast_u1c(mg + 1)
                            g_full_v = cast_v[mg]
                            g_full_s = 0

                        if l == 0 and mg >= att0:
                            TT.wait_ge(H["V"], g_rdy[mg])
                            for c2 in range(2):
                                TT.wait_ge(H["S"], psC_free)
                                for c in range(4):
                                    ch = c2 * 4 + c
                                    _mm = TT.matmul(
                                        psC[:, c * P:(c + 1) * P],
                                        gt[:, ch * P:(ch + 1) * P],
                                        identf[:], start=(c == 0),
                                        stop=(c == 3))
                                sm.inc(_mm, "T")
                                trc = C["T"]
                                SS.wait_ge(H["T"], trc)
                                dst = u1v_[:, c2 * 4:(c2 + 1) * 4,
                                           1 + lt:1 + lt + P]
                                sm.inc(SS.copy(dst, psC[:].rearrange(
                                    "p (c q) -> p c q", c=4)), "S")
                                psC_free = C["S"]
                            u1tr_t[mg] = C["T"]
                            u1_init_sd[mg] = psC_free
                        if l == 0 and mg + 2 < ntl:
                            SY.wait_ge(H["V"], g_rdy[mg])
                            SY.wait_ge(H["T"], u1tr_t.get(mg, 0))
                            sm.inc(SY.dma_start(
                                out=x1stg[:, (mg % 2) * D:
                                          (mg % 2 + 1) * D],
                                in_=x1src(mg + 2)), "dM", 16)
                            x1_ld[mg + 2] = C["dM"]

                        if mg < QKV0[l]:
                            emit_G(mg - 2)
                            continue

                        emit_E(mg - 1)

                        qb = (mg % 2) * 1536
                        TT.wait_ge(H["V"], g_full_v)
                        if l == 0:
                            TT.wait_ge(H["S"], g_full_s)
                        TT.wait_ge(H["dW"], w_rdy2)
                        TT.wait_ge(H["S"], qkv_ev.get(mg - 1, 0))
                        for ec in range(DC):
                            off = 1 if ec < 4 else 0
                            lhs = g1cv[:, (mg % 2) * DC + ec, off:off + P]
                            for np_ in range(2):
                                _mm = TT.matmul(
                                    psQ[:, np_ * 512:(np_ + 1) * 512],
                                    lhs,
                                    wqv[:, ec, np_ * 512:(np_ + 1) * 512],
                                    start=(ec == 0), stop=(ec == DC - 1))
                        sm.inc(_mm, "T")
                        qkmm = C["T"]
                        SS.wait_ge(H["T"], qkmm)
                        SS.wait_ge(H["V"], rot_rdy.get(mg - 2, 0))
                        sm.inc(SS.copy(qkv_s[:, qb:qb + 1024],
                                       psQ[:, 0:1024]), "S")
                        qk_ev1 = C["S"]
                        TT.wait_ge(H["S"], qk_ev1)
                        for ec in range(DC):
                            off = 1 if ec < 4 else 0
                            lhs = g1cv[:, (mg % 2) * DC + ec, off:off + P]
                            _mm = TT.matmul(
                                psQ[:, 0:512], lhs,
                                wqv[:, ec, 1024:1536],
                                start=(ec == 0), stop=(ec == DC - 1))
                        sm.inc(_mm, "T")
                        qkv_mm_t[mg] = C["T"]
                        SS.wait_ge(H["T"], qkv_mm_t[mg])
                        sm.inc(SS.copy(qkv_s[:, qb + 1024:qb + 1536],
                                       psQ[:, 0:512]), "S")
                        qkv_ev[mg] = C["S"]

                        emit_F(mg - 1)
                        emit_G(mg - 2)
                        if mg + 1 < ntl and mg + 1 >= QKV0[l] + 2:
                            load_rot(mg + 1)
                        if mg >= att0:
                            load_msk(mg)

                    emit_E(ntl - 1)
                    emit_F(ntl - 1)
                    emit_G(ntl - 2)
                    emit_G(ntl - 1)
                    join()
                    zero_row_m1()
                if s2stop <= 1 + 2 * l:
                    return

                with contextlib.ExitStack() as sb_:
                    def sbb(name, shape, dt):
                        return sb_.enter_context(
                            nc.sbuf_tensor(f"{name}b_l{l}", shape, dt))

                    def psb_(name, shape, dt=F32):
                        return sb_.enter_context(
                            nc.psum_tensor(f"{name}b_l{l}", shape, dt))

                    nft = (HT_A1 + NT) if l == 0 else NT
                    ft0 = ATT0[0] if l == 0 else HT_K2
                    h_ln = sbb("h_ln", [P, DC * nft * P], BF16)
                    sqf = sbb("sqf", [P, D], F32)
                    srow = sbb("srow", [1, 2 * P], F32)
                    bcs = sbb("bcs", [P, 2 * P], F32)
                    lngb = sbb("lngb", [P, DC], F32)
                    lnbb = sbb("lnbb", [P, DC], F32)
                    b2row = sbb("b2row", [1, D], F32)
                    w1a_p = sbb("w1a_p", [P, DC * 512], BF16)
                    w1g_p = sbb("w1g_p", [P, DC * 512], BF16)
                    w2_p = sbb("w2_p", [P, 4 * D], BF16)
                    b1a_p = sbb("b1a_p", [1, 512], BF16)
                    b1g_p = sbb("b1g_p", [1, 512], BF16)
                    gel = sbb("gel", [P, 512], BF16)
                    pm_sb = sbb("pm_sb", [P, 512], BF16)
                    pmT4 = sbb("pmT4", [P, 2 * 4 * 4 * P], BF16)
                    ones5 = sbb("ones5", [1, 512], F32)

                    prep_ps = contextlib.ExitStack()
                    hs_ps = prep_ps.enter_context(
                        nc.psum_tensor(f"hs_psb_l{l}", [1, D], F32))
                    bc_ps = prep_ps.enter_context(
                        nc.psum_tensor(f"bc_psb_l{l}", [P, 2 * P], F32))

                    hv_ = h_ln[:].rearrange("p (ch n) -> p ch n",
                                            n=nft * P)
                    w1av = w1a_p[:].rearrange("p (ec n) -> p ec n", n=512)
                    w1gv = w1g_p[:].rearrange("p (ec n) -> p ec n", n=512)

                    sm.inc(SY.dma_start(out=lngb[:], in_=lng_d[l]),
                           "dM", 16)
                    sm.inc(SY.dma_start(out=lnbb[:], in_=lnb_d[l]),
                           "dM", 16)
                    sm.inc(SY.dma_start(out=b2row[:], in_=b2r_d[l]),
                           "dM", 16)
                    lnp_ld = C["dM"]

                    h_app = {}
                    bc_ev = {}
                    for ti in range(nft):
                        mgu = ft0 + ti
                        lt = u1of(mgu)
                        uv = u1[:].rearrange("p (ch n) -> p ch n", n=CW1)
                        vA = uv[:, 0:4, 1 + lt:1 + lt + P]
                        vB = uv[:, 4:8, lt:lt + P]
                        TT.wait_ge(H["V"], h_app.get(ti - 1, 0))
                        for gi, vv_ in ((0, vA), (1, vB)):
                            _mm = TT.matmul(
                                hs_ps[:, gi * 512:(gi + 1) * 512]
                                .rearrange("o (c n) -> o c n", n=P),
                                ones_c[:], vv_, start=True, stop=True)
                        sm.inc(_mm, "T")
                        smm = C["T"]
                        VV.wait_ge(H["T"], smm)
                        sm.inc(VV.reduce_sum(
                            srow[0:1, 0:P].unsqueeze(2),
                            hs_ps[0:1, 0:D].rearrange(
                                "o (g c n) -> o n (g c)", g=2, n=P),
                            axis=mybir.AxisListType.X), "V")
                        ssum = C["V"]
                        SS.wait_ge(H["V"], h_app.get(ti - 1, 0))
                        SS.activation(
                            sqf[:, 0:512].rearrange(
                                "p (c n) -> p c n", n=P), vA, AF.Square)
                        _in = SS.activation(
                            sqf[:, 512:1024].rearrange(
                                "p (c n) -> p c n", n=P), vB, AF.Square)
                        sm.inc(_in, "S")
                        sqd2 = C["S"]
                        TT.wait_ge(H["S"], sqd2)
                        TT.wait_ge(H["V"], ssum)
                        for gi in range(2):
                            _mm = TT.matmul(
                                hs_ps[:, gi * 512:(gi + 1) * 512],
                                ones_c[:],
                                sqf[:, gi * 512:(gi + 1) * 512],
                                start=True, stop=True)
                        sm.inc(_mm, "T")
                        qmm = C["T"]
                        VV.wait_ge(H["T"], qmm)
                        sm.inc(VV.reduce_sum(
                            srow[0:1, P:2 * P].unsqueeze(2),
                            hs_ps[0:1, 0:D].rearrange(
                                "o (g c n) -> o n (g c)", g=2, n=P),
                            axis=mybir.AxisListType.X), "V")
                        VV.wait_ge(H["V"], C["V"])
                        VV.tensor_scalar_mul(srow[0:1, 0:P],
                                             srow[0:1, 0:P], 1.0 / D)
                        sm.inc(VV.tensor_scalar_mul(
                            srow[0:1, P:2 * P], srow[0:1, P:2 * P],
                            1.0 / D), "V")
                        mud = C["V"]
                        SS.wait_ge(H["V"], mud)
                        _in = SS.activation(sqf[0:1, 0:P],
                                            srow[0:1, 0:P], AF.Square)
                        sm.inc(_in, "S")
                        mu2 = C["S"]
                        VV.wait_ge(H["S"], mu2)
                        sm.inc(VV.tensor_tensor(
                            srow[0:1, P:2 * P], srow[0:1, P:2 * P],
                            sqf[0:1, 0:P], ALU.subtract), "V")
                        vard = C["V"]
                        SS.wait_ge(H["V"], vard)
                        SS.activation(srow[0:1, P:2 * P],
                                      srow[0:1, P:2 * P], AF.Ln,
                                      bias=f32(1e-5), scale=f32(1.0))
                        _in = SS.activation(srow[0:1, P:2 * P],
                                            srow[0:1, P:2 * P], AF.Exp,
                                            scale=f32(-0.5))
                        sm.inc(_in, "S")
                        rrd = C["S"]
                        TT.wait_ge(H["S"], rrd)
                        TT.wait_ge(H["S"], bc_ev.get(ti - 1, 0))
                        for gi in range(2):
                            _mm = TT.matmul(
                                bc_ps[:, gi * P:(gi + 1) * P], ones_r[:],
                                srow[0:1, gi * P:(gi + 1) * P],
                                start=(gi == 0), stop=(gi == 1))
                        sm.inc(_mm, "T")
                        bcm = C["T"]
                        SS.wait_ge(H["T"], bcm)
                        sm.inc(SS.copy(bcs[:], bc_ps[:]), "S")
                        bc_ev[ti] = C["S"]
                        VV.wait_ge(H["S"], bc_ev[ti])
                        VV.wait_ge(H["dM"], lnp_ld)
                        sq3 = sqf[:].rearrange("p (c n) -> p c n", n=P)[
                            :, 0:4, :]
                        for gi, vv_ in ((0, vA), (1, vB)):
                            dst = hv_[:, gi * 4:(gi + 1) * 4,
                                      ti * P:(ti + 1) * P]
                            mub = bcs[:, 0:P].unsqueeze(1) \
                                .broadcast_to([P, 4, P])
                            rb = bcs[:, P:2 * P].unsqueeze(1) \
                                .broadcast_to([P, 4, P])
                            VV.tensor_tensor(sq3, vv_, mub, ALU.subtract)
                            VV.tensor_tensor(sq3, sq3, rb, ALU.mult)
                            VV.tensor_tensor(
                                sq3, sq3,
                                lngb[:, gi * 4:(gi + 1) * 4].unsqueeze(2)
                                .broadcast_to([P, 4, P]), ALU.mult)
                            _in = VV.tensor_tensor(
                                dst, sq3,
                                lnbb[:, gi * 4:(gi + 1) * 4].unsqueeze(2)
                                .broadcast_to([P, 4, P]), ALU.add)
                        sm.inc(_in, "V")
                        h_app[ti] = C["V"]
                    join()

                    prep_ps.close()
                    ab_ps = psb_("ab_ps", [P, 2 * 512])
                    pt_ps2 = psb_("pt_ps2", [P, 512])
                    y_ps4 = psb_("y_ps4", [P, 4 * 512])
                    sm.inc(GG.memset(ones5[:], 1.0), "G")
                    ones5_rdy = C["G"]

                    w_ld = {}
                    gate_dn = {}
                    geld = {}
                    ptv = {}
                    last_w2m = 0
                    prev_gate = 0
                    prev_geld = 0
                    prev_ptev = 0
                    prev_y4 = 0
                    slot_w2m = {0: 0, 1: 0}
                    groups = [list(range(g, min(g + 4, nft)))
                              for g in range(0, nft, 4)]
                    pm4s = [pmT4[:, 0:2048].rearrange(
                                "p (kc n) -> p kc n", n=4 * P),
                            pmT4[:, 2048:4096].rearrange(
                                "p (kc n) -> p kc n", n=4 * P)]
                    y4v = y_ps4[:].rearrange("p (c n) -> p c n", n=512)
                    for pc in range(6):
                        w = GW[pc]
                        g0 = GOF[pc]
                        nkc = (w + P - 1) // P
                        GG.wait_ge(H["T"], last_w2m)
                        sm.inc(GG.dma_start(
                            out=w1av[:, :, 0:w],
                            in_=w1a_d[l].rearrange(
                                "(ec p) n -> p ec n", p=P)[
                                :, :, g0:g0 + w]), "dW", 16)
                        sm.inc(GG.dma_start(
                            out=w1gv[:, :, 0:w],
                            in_=w1g_d[l].rearrange(
                                "(ec p) n -> p ec n", p=P)[
                                :, :, g0:g0 + w]), "dW", 16)
                        for kc in range(nkc):
                            r0 = g0 + kc * P
                            r1 = min(g0 + w, r0 + P)
                            sm.inc(GG.dma_start(
                                out=w2_p[0:r1 - r0, kc * D:(kc + 1) * D],
                                in_=w2_d[l][r0:r1, :]), "dW", 16)
                        sm.inc(GG.dma_start(out=b1a_p[0:1, 0:w],
                                            in_=b1a_d[l][0:1, g0:g0 + w]),
                               "dW", 16)
                        sm.inc(GG.dma_start(out=b1g_p[0:1, 0:w],
                                            in_=b1g_d[l][0:1, g0:g0 + w]),
                               "dW", 16)
                        w_ld[pc] = C["dW"]
                        last_is_b2 = (pc == 5)

                        def w2_round(pd, mh):
                            nonlocal last_w2m, prev_y4
                            TT.wait_ge(H["S"], pd["ptev"])
                            TT.wait_ge(H["V"], prev_y4)
                            if last_is_b2 and mh == 0:
                                TT.wait_ge(H["G"], ones5_rdy)
                            ntok_p = pd["ntok"]
                            for kc in range(nkc):
                                r0 = g0 + kc * P
                                rows = min(g0 + w, r0 + P) - r0
                                for mc4 in range(4):
                                    mc = mh * 4 + mc4
                                    _mm = TT.matmul(
                                        y_ps4[:, mc4 * 512:
                                              mc4 * 512 + ntok_p],
                                        w2_p[0:rows,
                                             kc * D + mc * P:
                                             kc * D + (mc + 1) * P],
                                        pm4s[pd["gslot"]][0:rows, kc,
                                                          0:ntok_p],
                                        start=(kc == 0),
                                        stop=(kc == nkc - 1 and
                                              not last_is_b2))
                            if last_is_b2:
                                for mc4 in range(4):
                                    mc = mh * 4 + mc4
                                    _mm = TT.matmul(
                                        y_ps4[:, mc4 * 512:
                                              mc4 * 512 + ntok_p],
                                        b2row[0:1, mc * P:(mc + 1) * P],
                                        ones5[0:1, 0:ntok_p],
                                        start=False, stop=True)
                            sm.inc(_mm, "T")
                            last_w2m = C["T"]
                            slot_w2m[pd["gslot"]] = C["T"]
                            VV.wait_ge(H["T"], last_w2m)
                            u1t = u1[:].rearrange(
                                "p (ch n) -> p ch n", n=CW1)[
                                :, mh * 4:(mh + 1) * 4,
                                1 + pd["lt0"]:1 + pd["lt0"] + ntok_p]
                            _in = VV.tensor_tensor(
                                u1t, u1t, y4v[:, :, 0:ntok_p], ALU.add)
                            sm.inc(_in, "V")
                            prev_y4 = C["V"]

                        pend = None
                        for gi, grp in enumerate(groups):
                            gslot = gi % 2
                            ntok = len(grp) * P
                            lt0 = u1of(ft0 + grp[0])
                            for si, ti in enumerate(grp):
                                TT.wait_ge(H["dW"], w_ld[pc])
                                TT.wait_ge(H["V"], prev_gate)
                                TT.wait_ge(H["S"], prev_geld)
                                for ec in range(DC):
                                    lhs = hv_[:, ec, ti * P:(ti + 1) * P]
                                    TT.matmul(ab_ps[:, 0:w], lhs,
                                              w1av[:, ec, 0:w],
                                              start=(ec == 0), stop=False)
                                    TT.matmul(ab_ps[:, 512:512 + w], lhs,
                                              w1gv[:, ec, 0:w],
                                              start=(ec == 0), stop=False)
                                TT.matmul(ab_ps[:, 0:w], onesb_r[:],
                                          b1a_p[0:1, 0:w], start=False,
                                          stop=True)
                                _mm = TT.matmul(ab_ps[:, 512:512 + w],
                                                onesb_r[:], b1g_p[0:1, 0:w],
                                                start=False, stop=True)
                                sm.inc(_mm, "T")
                                agm = C["T"]
                                SS.wait_ge(H["T"], agm)
                                _in = SS.activation(gel[:, 0:w],
                                                    ab_ps[:, 512:512 + w],
                                                    AF.Gelu)
                                sm.inc(_in, "S")
                                geld[(pc, ti)] = C["S"]
                                prev_geld = C["S"]
                                VV.wait_ge(H["S"], geld[(pc, ti)])
                                _in = VV.tensor_tensor(pm_sb[:, 0:w],
                                                       ab_ps[:, 0:w],
                                                       gel[:, 0:w],
                                                       ALU.mult)
                                sm.inc(_in, "V")
                                gate_dn[(pc, ti)] = C["V"]
                                prev_gate = C["V"]
                                if pend is not None and si == 1:
                                    w2_round(pend, 0)
                                if pend is not None and si == 2:
                                    w2_round(pend, 1)
                                TT.wait_ge(H["V"], gate_dn[(pc, ti)])
                                TT.wait_ge(H["S"], prev_ptev)
                                for kc in range(nkc):
                                    c0 = kc * P
                                    c1 = min(w, c0 + P)
                                    _mm = TT.matmul(
                                        pt_ps2[0:c1 - c0,
                                               kc * P:kc * P + P],
                                        pm_sb[:, c0:c1], identb[:],
                                        start=True, stop=True)
                                sm.inc(_mm, "T")
                                ptv[(pc, ti)] = C["T"]
                                SS.wait_ge(H["T"], ptv[(pc, ti)])
                                SS.wait_ge(H["T"], slot_w2m[gslot])
                                _in = SS.copy(
                                    pm4s[gslot][:, :, si * P:(si + 1) * P],
                                    pt_ps2[:].rearrange(
                                        "p (kc n) -> p kc n", n=P))
                                sm.inc(_in, "S")
                                ptv[('ev', pc, ti)] = C["S"]
                                prev_ptev = C["S"]
                            if pend is not None and len(grp) < 3:
                                w2_round(pend, 1)
                            pend = dict(gslot=gslot, ntok=ntok, lt0=lt0,
                                        ptev=prev_ptev)
                        w2_round(pend, 0)
                        w2_round(pend, 1)
                        pend = None
                    join()
                    if l == 0:
                        zero_row_m1()
                if s2stop <= 2 + 2 * l:
                    return

            with contextlib.ExitStack() as sf:
                def sbf(name, shape, dt):
                    return sf.enter_context(nc.sbuf_tensor(name, shape, dt))

                x1rt = sbf("x1rt", [P, 2 * D], F32)
                x2st = sbf("x2st", [P, 2 * D], F32)
                zt = sbf("zt", [P, D], F32)
                xt_ps = sf.enter_context(nc.psum_tensor("xt_ps",
                                                        [P, 2 * D], F32))

                x1r_ld = {}
                for q in range(2):
                    sm.inc(SY.dma_start(out=x1rt[:, q * D:(q + 1) * D],
                                        in_=x1_scr[q * P:(q + 1) * P, :]),
                           "dM", 16)
                    x1r_ld[q] = C["dM"]
                tr_cons = {}
                x2_done = {}
                u1vf = u1[:].rearrange("p (ch n) -> p ch n", n=CW1)
                for q in range(NT):
                    lt = U1OF[1](HT_K2 + q)
                    slot = q % 2
                    TT.wait_ge(H["V"], tr_cons.get(q - 2, 0))
                    for c2 in range(2):
                        for c in range(4):
                            ch = c2 * 4 + c
                            _mm = TT.matmul(
                                xt_ps[:, slot * D + c2 * 512 + c * P:
                                      slot * D + c2 * 512 + (c + 1) * P],
                                u1vf[:, ch, 1 + lt:1 + lt + P],
                                identf[:], start=(c == 0),
                                stop=(c == 3))
                    sm.inc(_mm, "T")
                    trm = C["T"]
                    VV.wait_ge(H["T"], trm)
                    VV.bn_stats(stt2[:, 0:6],
                                xt_ps[:, slot * D:slot * D + 512])
                    sm.inc(VV.bn_stats(stt2[:, 6:12],
                                       xt_ps[:, slot * D + 512:
                                             slot * D + D]), "V")
                    VV.wait_ge(H["V"], C["V"])
                    sm.inc(VV.bn_aggr(agg2[:], stt2[:]), "V")
                    bnf = C["V"]
                    SS.wait_ge(H["V"], bnf)
                    SS.activation(rstd2[:], agg2[:, 1:2], AF.Ln,
                                  bias=f32(1e-5), scale=f32(1.0))
                    sm.inc(SS.activation(rstd2[:], rstd2[:], AF.Exp,
                                         scale=f32(-0.5)), "S")
                    rsf = C["S"]
                    VV.wait_ge(H["S"], rsf)
                    _in = VV.tensor_scalar(
                        zt[:], xt_ps[:, slot * D:slot * D + D],
                        agg2[:, 0:1], rstd2[:], ALU.subtract, ALU.mult)
                    sm.inc(_in, "V")
                    tr_cons[q] = C["V"]
                    VV.wait_ge(H["dM"], x1r_ld[q])
                    if q >= 2:
                        VV.wait_ge(H["dOU"], x2_done[q - 2])
                    _in = VV.tensor_tensor(
                        x2st[:, slot * D:(slot + 1) * D], zt[:],
                        x1rt[:, slot * D:(slot + 1) * D], ALU.add)
                    sm.inc(_in, "V")
                    x2v = C["V"]
                    if q + 2 < NT:
                        SY.wait_ge(H["V"], x2v)
                        sm.inc(SY.dma_start(
                            out=x1rt[:, slot * D:(slot + 1) * D],
                            in_=x1_scr[(q + 2) * P:(q + 3) * P, :]),
                            "dM", 16)
                        x1r_ld[q + 2] = C["dM"]
                    SY.wait_ge(H["V"], x2v)
                    sm.inc(SY.dma_start(
                        out=out_d[q * P:(q + 1) * P, :],
                        in_=x2st[:, slot * D:(slot + 1) * D]), "dOU", 16)
                    x2_done[q] = C["dOU"]

    return emit
